# version 09
# baseline (speedup 1.0000x reference)
import contextlib
import numpy as np
import ml_dtypes

import concourse.bass as bass
import concourse.mybir as mybir
from concourse.bass_utils import run_bass_kernel_spmd

F32 = mybir.dt.float32
BF16 = mybir.dt.bfloat16
AF = mybir.ActivationFunctionType
ALU = mybir.AluOpType
AX = mybir.AxisListType

D = 2048
DI = 4096
T = 2048
FF = 8192
NKT = 16
NDG = 32
NFT = 64
NS = 16
NH = 16
RSQ_HD = 0.08838834764831845

_DEBUG = False
_NCORES = 8
LAST_DEBUG = None


class Ctr:
    def __init__(self, handle):
        self.h = handle
        self.v = 0


def build_program(debug=False):
    nc = bass.Bass("TRN2", target_bir_lowering=False)

    xT = nc.dram_tensor("xT", [D, T], F32, kind="ExternalInput").ap()
    w_inT = nc.dram_tensor("w_inT", [D, 2 * DI], F32, kind="ExternalInput").ap()
    wxT = nc.dram_tensor("wxT", [DI, 33], F32, kind="ExternalInput").ap()
    outWT = nc.dram_tensor("outWT", [DI, D], F32, kind="ExternalInput").ap()
    wqT = nc.dram_tensor("wqT", [D, D], F32, kind="ExternalInput").ap()
    wkT = nc.dram_tensor("wkT", [D, D], F32, kind="ExternalInput").ap()
    wvT = nc.dram_tensor("wvT", [D, D], F32, kind="ExternalInput").ap()
    woT = nc.dram_tensor("woT", [D, D], F32, kind="ExternalInput").ap()
    a_ffn = nc.dram_tensor("a_ffn", [D, NS], F32, kind="ExternalInput").ap()
    b_ffn = nc.dram_tensor("b_ffn", [NS, FF], F32, kind="ExternalInput").ap()
    winbT = nc.dram_tensor("winbT", [D, FF], F32, kind="ExternalInput").ap()
    wupbT = nc.dram_tensor("wupbT", [D, FF], F32, kind="ExternalInput").ap()
    wdownT = nc.dram_tensor("wdownT", [FF, D], F32, kind="ExternalInput").ap()
    a_perm = nc.dram_tensor("a_perm", [128, 512], F32, kind="ExternalInput").ap()
    packv = nc.dram_tensor("packv", [256, 128], F32, kind="ExternalInput").ap()
    dtb2d = nc.dram_tensor("dtb2d", [32, 128], F32, kind="ExternalInput").ap()
    ident_in = nc.dram_tensor("ident_in", [128, 128], F32, kind="ExternalInput").ap()

    outT = nc.dram_tensor("outT", [D, T], F32, kind="ExternalOutput").ap()

    sc_xn = nc.dram_tensor("sc_xn", [D, T], BF16, kind="Internal").ap()
    sc_xc = nc.dram_tensor("sc_xc", [DI, T], BF16, kind="Internal").ap()
    sc_z = nc.dram_tensor("sc_z", [DI, T], BF16, kind="Internal").ap()
    sc_yg = nc.dram_tensor("sc_yg", [DI, T], BF16, kind="Internal").ap()
    sc_s = nc.dram_tensor("sc_s", [D, T], F32, kind="Internal").ap()
    sc_attn = nc.dram_tensor("sc_attn", [D, T], BF16, kind="Internal").ap()
    sc_gu = nc.dram_tensor("sc_gu", [FF, T], BF16, kind="Internal").ap()

    dbg = {}
    if debug:
        for nm, shp, dt_ in [("dbg_p", [33, T], F32), ("dbg_rows", [4, T], F32),
                             ("dbg_xc0", [128, T], BF16), ("dbg_z0", [128, T], BF16),
                             ("dbg_yg0", [128, T], BF16), ("dbg_s0", [128, T], F32),
                             ("dbg_xn0", [128, T], BF16), ("dbg_h10", [128, T], BF16),
                             ("dbg_attn0", [128, T], BF16), ("dbg_mod", [16, T], F32),
                             ("dbg_gu0", [128, T], BF16), ("dbg_hA", [16, T], F32),
                             ("dbg_aux", [4, T], F32)]:
            dbg[nm] = nc.dram_tensor(nm, shp, dt_, kind="ExternalOutput").ap()

    ctx = contextlib.ExitStack()
    with ctx:
        packT = ctx.enter_context(nc.sbuf_tensor("packT", [128, 256], F32))
        ident_f = ctx.enter_context(nc.sbuf_tensor("ident_f", [128, 128], F32))
        ident_b = ctx.enter_context(nc.sbuf_tensor("ident_b", [128, 128], BF16))
        ones_r_f = ctx.enter_context(nc.sbuf_tensor("ones_r_f", [65, 128], F32))
        ones_c_f = ctx.enter_context(nc.sbuf_tensor("ones_c_f", [128, 1], F32))
        ones_c_b = ctx.enter_context(nc.sbuf_tensor("ones_c_b", [128, 1], BF16))
        ones32 = ctx.enter_context(nc.sbuf_tensor("ones32", [32, 1], F32))
        mdtb = ctx.enter_context(nc.sbuf_tensor("mdtb", [1, 128], F32))
        rows65 = ctx.enter_context(nc.sbuf_tensor("rows65", [65, T], F32))
        dt_row = ctx.enter_context(nc.sbuf_tensor("dt_row", [1, T], F32))
        eps5 = ctx.enter_context(nc.sbuf_tensor("eps5", [1, 1], F32))
        eps6 = ctx.enter_context(nc.sbuf_tensor("eps6", [1, 1], F32))

        sems = {}
        for nm in ["ld", "ld2", "sp", "sp2", "mm", "mm2", "ev", "ev2",
                   "dv", "dv2", "po", "fin", "pb", "pc", "pd",
                   "lda", "ldb", "l2a", "l2b", "spa", "spb", "s2a", "s2b",
                   "xs0", "xs1", "xs2", "xs3", "fo0", "fo1"]:
            sems[nm] = Ctr(ctx.enter_context(nc.semaphore(nm)))
        ld, ld2, sp, sp2 = sems["ld"], sems["ld2"], sems["sp"], sems["sp2"]
        mm, mm2, ev, ev2 = sems["mm"], sems["mm2"], sems["ev"], sems["ev2"]
        dv, dv2, po, fin = sems["dv"], sems["dv2"], sems["po"], sems["fin"]
        pb, pc, pd = sems["pb"], sems["pc"], sems["pd"]
        lda, ldb = sems["lda"], sems["ldb"]
        l2a, l2b = sems["l2a"], sems["l2b"]
        spa, spb = sems["spa"], sems["spb"]
        s2a, s2b = sems["s2a"], sems["s2b"]
        xs4 = [sems["xs0"], sems["xs1"], sems["xs2"], sems["xs3"]]
        fo2 = [sems["fo0"], sems["fo1"]]
        def rst():
            pass

        mctx = contextlib.ExitStack()
        wx_b = mctx.enter_context(nc.sbuf_tensor("wx_b", [128, 33 * NDG], BF16))
        aneg = mctx.enter_context(nc.sbuf_tensor("aneg", [128, 512], F32))
        p_sb = mctx.enter_context(nc.sbuf_tensor("p_sb", [33, T], F32))

        with contextlib.ExitStack() as sc:
            stg = sc.enter_context(nc.sbuf_tensor("s_stg", [128, 1056], F32))
            stg2 = sc.enter_context(nc.sbuf_tensor("s_stg2", [128, 256], F32))
            dtb32 = sc.enter_context(nc.sbuf_tensor("s_dtb32", [32, 128], F32))
            s_row = sc.enter_context(nc.sbuf_tensor("s_row", [1, 256], F32))
            expa = sc.enter_context(nc.sbuf_tensor("s_expa", [128, 512], F32))
            ps_a = sc.enter_context(nc.psum_tensor("s_psa", [128, 512], F32))
            ps_b = sc.enter_context(nc.psum_tensor("s_psb", [128, 512], F32))
            blk = sc.enter_context(nc.Block())
            rst()
            L0, M0, E0, P0, V0 = ld.v, mm.v, ev.v, po.v, dv.v

            @blk.sync
            def _(e):
                e.dma_start(ident_f[:], ident_in).then_inc(ld.h, 16)
                e.dma_start(stg2[:, 0:128], packv[0:128, :]).then_inc(ld.h, 16)
                e.dma_start(stg2[:, 128:256], packv[128:256, :]).then_inc(ld.h, 16)
                e.dma_start(expa[:], a_perm).then_inc(ld.h, 16)
                e.dma_start(dtb32[:], dtb2d).then_inc(ld.h, 16)
                for dg in range(NDG):
                    e.dma_start(stg[:, dg * 33:(dg + 1) * 33],
                                wxT[dg * 128:(dg + 1) * 128, :]).then_inc(ld.h, 16)

            ld.v = L0 + 16 * (5 + NDG)

            @blk.gpsimd
            def _(e):
                e.memset(ones_r_f[:], 1.0)
                e.memset(ones_c_f[:], 1.0)
                e.memset(ones_c_b[:], 1.0)
                e.memset(ones32[:], 1.0)
                e.memset(eps5[:], 1e-5)
                e.memset(eps6[:], 1e-6)
                e.memset(p_sb[:], 0.0).then_inc(po.h, 1)
                e.wait_ge(ld.h, L0 + 16 * (5 + NDG))
                e.tensor_copy(ident_b[:], ident_f[:])
                e.tensor_copy(wx_b[:], stg[:, 0:1056]).then_inc(po.h, 1)

            po.v = P0 + 2

            @blk.tensor
            def _(e):
                e.wait_ge(ld.h, L0 + 16 * (5 + NDG))
                e.transpose(ps_a[:, 0:128], stg2[:, 0:128], ident_f[:])
                e.transpose(ps_b[:, 0:128], stg2[:, 128:256], ident_f[:]) \
                    .then_inc(mm.h, 1)
                e.wait_ge(ev.h, E0 + 1)
                e.wait_ge(po.h, P0 + 1)
                e.wait_ge(ld.h, L0 + 16 * (5 + NDG))
                e.matmul(ps_a[0:1, 128:256], ones32[:], dtb32[:],
                         start=True, stop=True).then_inc(mm.h, 1)

            mm.v = M0 + 2

            @blk.scalar
            def _(e):
                e.wait_ge(mm.h, M0 + 1)
                e.copy(packT[:, 0:128], ps_a[:, 0:128])
                e.copy(packT[:, 128:256], ps_b[:, 0:128]).then_inc(ev.h, 1)
                e.wait_ge(ld.h, L0 + 16 * (5 + NDG))
                e.activation(aneg[:], expa[:], AF.Exp).then_inc(ev.h, 1)

            ev.v = E0 + 2

            @blk.vector
            def _(e):
                e.wait_ge(mm.h, M0 + 2)
                e.tensor_copy(s_row[0:1, 0:128], ps_a[0:1, 128:256])
                e.wait_ge(ev.h, E0 + 2)
                e.tensor_scalar(aneg[:], aneg[:], -1.0, None, ALU.mult)
                e.scalar_tensor_tensor(s_row[0:1, 128:256], s_row[0:1, 0:128],
                                       1.0 / DI, s_row[0:1, 0:128],
                                       ALU.mult, ALU.bypass,
                                       accum_out=mdtb[0:1, 0:1]) \
                    .then_inc(dv.h, 1)

            dv.v = V0 + 1

        with contextlib.ExitStack() as xscope:
            xn_sb = xscope.enter_context(
                nc.sbuf_tensor("xn_sb", [128, NKT * T], BF16))

            with contextlib.ExitStack() as sc:
                x_t = [sc.enter_context(nc.sbuf_tensor(f"p1x{i}", [128, T], F32))
                       for i in range(2)]
                xsq2 = [sc.enter_context(nc.sbuf_tensor(f"p1sq{i}", [128, T], F32))
                        for i in range(2)]
                mean_b = sc.enter_context(nc.sbuf_tensor("p1mb", [128, T], F32))
                rstd_b = sc.enter_context(nc.sbuf_tensor("p1rb", [128, T], F32))
                ps_s = [sc.enter_context(nc.psum_tensor(f"p1s{i}", [128, 512], F32))
                        for i in range(4)]
                ps_q = [sc.enter_context(nc.psum_tensor(f"p1q{i}", [128, 512], F32))
                        for i in range(4)]
                row_a = sc.enter_context(nc.sbuf_tensor("p1ra", [1, T], F32))
                row_b = sc.enter_context(nc.sbuf_tensor("p1rb_", [1, T], F32))
                row_c = sc.enter_context(nc.sbuf_tensor("p1rc", [1, T], F32))
                blk = sc.enter_context(nc.Block())
                rst()
                L0, L2, E0, M0, V0, S0 = ld.v, ld2.v, ev.v, mm.v, dv.v, sp.v
                LP = [lda.v, ldb.v]
                xmark = {}
                for kt in range(NKT):
                    LP[kt % 2] += 16
                    xmark[kt] = LP[kt % 2]
                L2P = [l2a.v, l2b.v]
                xmark2 = {}
                for kt in range(NKT):
                    L2P[kt % 2] += 16
                    xmark2[kt] = L2P[kt % 2]

                @blk.sync
                def _(e):
                    for kt in range(NKT):
                        if kt >= 2:
                            e.wait_ge(ev.h, E0 + kt - 1)
                            e.wait_ge(mm.h, M0 + 2 * (kt - 1))
                        e.dma_start(x_t[kt % 2][:],
                                    xT[kt * 128:(kt + 1) * 128, :]) \
                            .then_inc([lda, ldb][kt % 2].h, 16)

                @blk.scalar
                def _(e):
                    for kt in range(NKT):
                        e.wait_ge([lda, ldb][kt % 2].h, xmark[kt])
                        if kt >= 2:
                            e.wait_ge(mm.h, M0 + 2 * (kt - 1))
                        e.activation(xsq2[kt % 2][:], x_t[kt % 2][:], AF.Square) \
                            .then_inc(ev.h, 1)

                ev.v = E0 + NKT

                @blk.tensor
                def _(e):
                    for kt in range(NKT):
                        e.wait_ge([lda, ldb][kt % 2].h, xmark[kt])
                        for n in range(4):
                            e.matmul(ps_s[n][0:1, :], ones_c_f[:],
                                     x_t[kt % 2][:, n * 512:(n + 1) * 512],
                                     start=(kt == 0), stop=(kt == NKT - 1))
                        e.wait_ge(ev.h, E0 + kt + 1)
                        last = None
                        for n in range(4):
                            last = e.matmul(ps_q[n][0:1, :], ones_c_f[:],
                                            xsq2[kt % 2][:, n * 512:(n + 1) * 512],
                                            start=(kt == 0), stop=(kt == NKT - 1))
                        last.then_inc(mm.h, 2)
                    e.wait_ge(dv.h, V0 + 2)
                    for n in range(4):
                        e.matmul(ps_s[n][:], ones_r_f[0:1, :],
                                 rows65[0:1, n * 512:(n + 1) * 512],
                                 start=True, stop=True)
                    last = None
                    for n in range(4):
                        last = e.matmul(ps_q[n][:], ones_r_f[32:33, :],
                                        rows65[32:33, n * 512:(n + 1) * 512],
                                        start=True, stop=True)
                    last.then_inc(mm.h, 1)

                mm.v = M0 + 2 * NKT + 1

                @blk.vector
                def _(e):
                    e.wait_ge(mm.h, M0 + 2 * NKT)
                    for n in range(4):
                        e.tensor_scalar(rows65[0:1, n * 512:(n + 1) * 512],
                                        ps_s[n][0:1, :], 1.0 / D, None, ALU.mult)
                        e.tensor_scalar(row_a[0:1, n * 512:(n + 1) * 512],
                                        ps_q[n][0:1, :], 1.0 / D, None, ALU.mult)
                    e.tensor_tensor(row_b[:], rows65[0:1, :], rows65[0:1, :], ALU.mult)
                    e.tensor_tensor(row_a[:], row_a[:], row_b[:], ALU.subtract) \
                        .then_inc(dv.h, 1)
                    e.wait_ge(ev.h, E0 + NKT + 1)
                    e.reciprocal(rows65[32:33, :], row_c[:]).then_inc(dv.h, 1)

                dv.v = V0 + 2

                @blk.scalar
                def _(e):
                    e.wait_ge(dv.h, V0 + 1)
                    e.activation(row_c[:], row_a[:], AF.Sqrt, bias=eps5[0:1, 0:1]) \
                        .then_inc(ev.h, 1)
                    e.wait_ge(mm.h, M0 + 2 * NKT + 1)
                    for n in range(4):
                        e.copy(mean_b[:, n * 512:(n + 1) * 512], ps_s[n][:])
                    last = None
                    for n in range(4):
                        last = e.copy(rstd_b[:, n * 512:(n + 1) * 512], ps_q[n][:])
                    last.then_inc(ev.h, 1)

                ev.v = E0 + NKT + 2

                @blk.sync
                def _(e):
                    e.wait_ge(ev.h, E0 + NKT + 2)
                    for kt in range(NKT):
                        if kt >= 2:
                            e.wait_ge(dv.h, V0 + 2 + kt - 1)
                        e.dma_start(x_t[kt % 2][:],
                                    xT[kt * 128:(kt + 1) * 128, :]) \
                            .then_inc([l2a, l2b][kt % 2].h, 16)

                @blk.vector
                def _(e):
                    for kt in range(NKT):
                        e.wait_ge([l2a, l2b][kt % 2].h, xmark2[kt])
                        e.tensor_tensor(xsq2[kt % 2][:], x_t[kt % 2][:], mean_b[:],
                                        ALU.subtract)
                        e.tensor_tensor(xn_sb[:, kt * T:(kt + 1) * T],
                                        xsq2[kt % 2][:],
                                        rstd_b[:], ALU.mult).then_inc(dv.h, 1)

                dv.v = V0 + 2 + NKT

                @blk.gpsimd
                def _(e):
                    for kt in range(NKT):
                        e.wait_ge(dv.h, V0 + 2 + kt + 1)
                        e.dma_start(sc_xn[kt * 128:(kt + 1) * 128, :],
                                    xn_sb[:, kt * T:(kt + 1) * T]).then_inc(sp.h, 16)
                    e.wait_ge(sp.h, S0 + 16 * NKT)

                sp.v = S0 + 16 * NKT
                lda.v, ldb.v = LP[0], LP[1]
                l2a.v, l2b.v = L2P[0], L2P[1]

                if debug:
                    @blk.sync
                    def _(e):
                        e.wait_ge(dv.h, V0 + 2 + NKT)
                        e.dma_start(dbg["dbg_rows"][0:1, :], rows65[0:1, :]) \
                            .then_inc(ld2.h, 16)
                        e.dma_start(dbg["dbg_rows"][1:2, :], rows65[32:33, :]) \
                            .then_inc(ld2.h, 16)
                        e.dma_start(dbg["dbg_xn0"], sc_xn[0:128, :]) \
                            .then_inc(ld2.h, 16)
                        e.wait_ge(ld2.h, ld2.v + 48)
                    ld2.v = ld2.v + 48

            with contextlib.ExitStack() as sc:
                wstg = [sc.enter_context(nc.sbuf_tensor(f"m1wstg{i}", [128, 2048], F32))
                        for i in range(2)]
                wqb = [sc.enter_context(nc.sbuf_tensor(f"m1wq{i}", [128, 8192], BF16))
                       for i in range(2)]
                xs_pad = [sc.enter_context(nc.sbuf_tensor(f"m1xp{i}", [128, 2052], BF16))
                          for i in range(2)]
                acc_a = sc.enter_context(nc.sbuf_tensor("m1aa", [128, T], F32))
                acc_b = sc.enter_context(nc.sbuf_tensor("m1ab", [128, T], F32))
                xc_t = [sc.enter_context(nc.sbuf_tensor(f"m1xc{i}", [128, T], BF16))
                        for i in range(2)]
                z_t = [sc.enter_context(nc.sbuf_tensor(f"m1z{i}", [128, T], BF16))
                       for i in range(2)]
                ps_xz = [sc.enter_context(nc.psum_tensor(f"m1p{i}", [128, 512], F32))
                         for i in range(4)]
                ps_p = [sc.enter_context(nc.psum_tensor(f"m1pp{i}", [128, 512], F32))
                        for i in range(4)]
                blk = sc.enter_context(nc.Block())
                rst()
                L0, P0, M0, M20, E0, E20, V0, V20, S0, S20 = \
                    ld.v, po.v, mm.v, mm2.v, ev.v, ev2.v, dv.v, dv2.v, sp.v, sp2.v
                SPP = [spa.v, spb.v]
                xcmark = {}
                for m in range(32):
                    SPP[m % 2] += 16
                    xcmark[m] = SPP[m % 2]
                SZP = [s2a.v, s2b.v]
                zmark = {}
                for m in range(32, 64):
                    SZP[m % 2] += 16
                    zmark[m] = SZP[m % 2]

                @blk.gpsimd
                def _(e):
                    e.memset(xs_pad[0][:, 0:3], 0.0)
                    e.memset(xs_pad[1][:, 0:3], 0.0).then_inc(po.h, 1)

                P1 = P0 + 1
                po.v = P1

                PB1 = pb.v
                pb.v = PB1 + 64

                @blk.sync
                def _(e):
                    for q in range(16):
                        for quarter in range(4):
                            hs = 4 * q + quarter
                            if hs >= 2:
                                e.wait_ge(pb.h, PB1 + hs - 1)
                            for kt in range(4):
                                ktg = quarter * 4 + kt
                                e.dma_start(
                                    wstg[hs % 2][:, kt * 512:(kt + 1) * 512],
                                    w_inT[ktg * 128:(ktg + 1) * 128,
                                          q * 512:(q + 1) * 512]).then_inc(ld.h, 16)

                ld.v = L0 + 16 * 256

                @blk.gpsimd
                def _(e):
                    def spills(qq):
                        for ms in range(4):
                            m = qq * 4 + ms
                            if m < 32:
                                e.wait_ge(ev2.h, E20 + m + 1)
                                e.dma_start(sc_xc[m * 128:(m + 1) * 128, :],
                                            xc_t[m % 2][:]) \
                                    .then_inc([spa, spb][m % 2].h, 16)
                            else:
                                e.wait_ge(ev.h, E0 + 4 * (m + 1))
                                e.dma_start(sc_z[(m - 32) * 128:(m - 31) * 128, :],
                                            z_t[m % 2][:]) \
                                    .then_inc([s2a, s2b][m % 2].h, 16)
                    for q in range(16):
                        if q >= 2:
                            spills(q - 2)
                    spills(14)
                    spills(15)
                    e.wait_ge(spa.h, SPP[0])
                    e.wait_ge(spb.h, SPP[1])
                    e.wait_ge(s2a.h, SZP[0])
                    e.wait_ge(s2b.h, SZP[1])

                spa.v, spb.v = SPP[0], SPP[1]
                s2a.v, s2b.v = SZP[0], SZP[1]

                @blk.tensor
                def _(e):
                    for m in range(64):
                        q, msub = m // 4, m % 4
                        pm = m - 2
                        if 0 <= pm < 32:
                            e.wait_ge(ev2.h, E20 + pm + 1)
                            for n in range(4):
                                if pm >= 1:
                                    e.wait_ge(dv2.h, V20 + (pm - 1) * 4 + n + 1)
                                e.matmul(ps_p[n][0:33, :],
                                         wx_b[:, pm * 33:(pm + 1) * 33],
                                         xc_t[pm % 2][:, n * 512:(n + 1) * 512],
                                         start=True, stop=True).then_inc(mm2.h, 1)
                        if msub == 0:
                            e.wait_ge(pb.h, PB1 + 4 * (q + 1))
                        for n in range(4):
                            g = 4 * m + n
                            if g >= 4:
                                e.wait_ge(ev.h, E0 + g - 3)
                            last = None
                            for kt in range(NKT):
                                last = e.matmul(
                                    ps_xz[g % 4][:],
                                    wqb[q % 2][:, kt * 512 + msub * 128:
                                               kt * 512 + msub * 128 + 128],
                                    xn_sb[:, kt * T + n * 512:kt * T + n * 512 + 512],
                                    start=(kt == 0), stop=(kt == NKT - 1))
                            last.then_inc(mm.h, 1)

                mm.v = M0 + 256
                mm2.v = M20 + 128

                @blk.scalar
                def _(e):
                    def wcast(hs):
                        q = hs // 4
                        quarter = hs % 4
                        e.wait_ge(ld.h, L0 + 16 * 4 * (hs + 1))
                        if q >= 2 and quarter == 0:
                            e.wait_ge(mm.h, M0 + 16 * (q - 1))
                        e.copy(wqb[q % 2][:, quarter * 2048:
                                          (quarter + 1) * 2048],
                               wstg[hs % 2][:]).then_inc(pb.h, 1)
                    for hs in range(8):
                        wcast(hs)
                    for m in range(64):
                        if m % 4 == 0 and m >= 4:
                            q = m // 4 + 1
                            if q <= 15:
                                for quarter in range(4):
                                    wcast(4 * q + quarter)
                        for n in range(4):
                            g = 4 * m + n
                            e.wait_ge(mm.h, M0 + g + 1)
                            if m < 32:
                                if n == 0 and m >= 2:
                                    e.wait_ge(dv.h, V0 + m - 1)
                                e.copy(xs_pad[m % 2][:, 3 + n * 512:3 + n * 512 + 512],
                                       ps_xz[g % 4][:]).then_inc(ev.h, 1)
                            else:
                                if n == 0 and m >= 34:
                                    e.wait_ge([s2a, s2b][m % 2].h, zmark[m - 2])
                                e.copy(z_t[m % 2][:, n * 512:(n + 1) * 512],
                                       ps_xz[g % 4][:]).then_inc(ev.h, 1)
                        if m < 32:
                            e.wait_ge(dv.h, V0 + m + 1)
                            if m >= 2:
                                e.wait_ge([spa, spb][m % 2].h, xcmark[m - 2])
                                e.wait_ge(mm2.h, M20 + 4 * (m - 1))
                            e.activation(xc_t[m % 2][:], acc_b[:], AF.Silu,
                                         bias=packT[:, 128 + m:129 + m]) \
                                .then_inc(ev2.h, 1)

                ev.v = E0 + 256
                ev2.v = E20 + 32

                @blk.vector
                def _(e):
                    for m in range(32):
                        xs = xs_pad[m % 2]
                        e.wait_ge(ev.h, E0 + 4 * (m + 1))
                        e.tensor_scalar(acc_a[:], xs[:, 0:2048],
                                        packT[:, m:m + 1], None, ALU.mult)
                        if m >= 1:
                            e.wait_ge(ev2.h, E20 + m)
                        e.scalar_tensor_tensor(acc_b[:], xs[:, 1:2049],
                                               packT[:, 32 + m:33 + m], acc_a[:],
                                               ALU.mult, ALU.add)
                        e.scalar_tensor_tensor(acc_a[:], xs[:, 2:2050],
                                               packT[:, 64 + m:65 + m], acc_b[:],
                                               ALU.mult, ALU.add)
                        e.scalar_tensor_tensor(acc_b[:], xs[:, 3:2051],
                                               packT[:, 96 + m:97 + m], acc_a[:],
                                               ALU.mult, ALU.add).then_inc(dv.h, 1)
                        pm = m - 2
                        if pm >= 0:
                            for n in range(4):
                                e.wait_ge(mm2.h, M20 + pm * 4 + n + 1)
                                e.tensor_tensor(p_sb[0:33, n * 512:(n + 1) * 512],
                                                p_sb[0:33, n * 512:(n + 1) * 512],
                                                ps_p[n][0:33, :], ALU.add) \
                                    .then_inc(dv2.h, 1)
                    for pm in [30, 31]:
                        for n in range(4):
                            e.wait_ge(mm2.h, M20 + pm * 4 + n + 1)
                            e.tensor_tensor(p_sb[0:33, n * 512:(n + 1) * 512],
                                            p_sb[0:33, n * 512:(n + 1) * 512],
                                            ps_p[n][0:33, :], ALU.add) \
                                .then_inc(dv2.h, 1)

                dv.v = V0 + 32
                dv2.v = V20 + 128

        m2ctx = contextlib.ExitStack()
        dtb_sb = m2ctx.enter_context(nc.sbuf_tensor("dtb_sb", [128, T], F32))
        carry = m2ctx.enter_context(nc.sbuf_tensor("carry", [128, 512], F32))
        with contextlib.ExitStack() as sc:
            ps_bc = [sc.enter_context(nc.psum_tensor(f"m2b{i}", [128, 512], F32))
                     for i in range(4)]
            row_a = sc.enter_context(nc.sbuf_tensor("m2ra", [1, T], F32))
            row_b = sc.enter_context(nc.sbuf_tensor("m2rb", [1, T], F32))
            row_c = sc.enter_context(nc.sbuf_tensor("m2rc", [1, T], F32))
            blk = sc.enter_context(nc.Block())
            rst()
            L0, E0, V0, M0 = ld.v, ev.v, dv.v, mm.v

            @blk.sync
            def _(e):
                e.dma_start(row_a[:], p_sb[32:33, :]).then_inc(ld.h, 16)
                if debug:
                    e.wait_ge(ev.h, E0 + 2)
                    e.dma_start(dbg["dbg_p"], p_sb[:]).then_inc(ld.h, 16)
                    e.dma_start(dbg["dbg_rows"][2:3, :], dt_row[0:1, :]).then_inc(ld.h, 16)
                    e.dma_start(dbg["dbg_aux"][0:1, :], row_a[:]).then_inc(ld.h, 16)
                    e.dma_start(dbg["dbg_aux"][1:2, :], row_b[:]).then_inc(ld.h, 16)
                    e.dma_start(dbg["dbg_aux"][2:3, :], row_c[:]).then_inc(ld.h, 16)
                    e.dma_start(dbg["dbg_aux"][3:4, 0:1], mdtb[0:1, 0:1]).then_inc(ld.h, 16)
                    e.wait_ge(ld.h, L0 + 16 * 7)

            ld.v = L0 + (16 * 7 if debug else 16)

            @blk.scalar
            def _(e):
                e.wait_ge(ld.h, L0 + 16)
                e.activation(row_b[:], row_a[:], AF.Abs, bias=mdtb[0:1, 0:1])
                e.activation(row_c[:], row_b[:], AF.Exp, scale=-1.0)
                e.activation(row_b[:], row_c[:], AF.Ln, bias=1.0)
                e.activation(row_c[:], row_a[:], AF.Relu, bias=mdtb[0:1, 0:1]) \
                    .then_inc(ev.h, 1)
                e.wait_ge(mm.h, M0 + 1)
                last = None
                for n in range(4):
                    last = e.copy(dtb_sb[:, n * 512:(n + 1) * 512], ps_bc[n][:])
                last.then_inc(ev.h, 1)

            ev.v = E0 + 2

            @blk.vector
            def _(e):
                e.wait_ge(ev.h, E0 + 1)
                e.tensor_tensor(dt_row[0:1, :], row_b[:], row_c[:], ALU.add) \
                    .then_inc(dv.h, 1)

            dv.v = V0 + 1

            @blk.tensor
            def _(e):
                e.wait_ge(dv.h, V0 + 1)
                last = None
                for n in range(4):
                    last = e.matmul(ps_bc[n][:], ones_r_f[0:1, :],
                                    dt_row[0:1, n * 512:(n + 1) * 512],
                                    start=True, stop=True)
                last.then_inc(mm.h, 1)

            mm.v = M0 + 1

        with contextlib.ExitStack() as sc:
            wsb_all = sc.enter_context(nc.sbuf_tensor("m3w", [128, 16 * 1024], BF16))
            cb_all = sc.enter_context(nc.sbuf_tensor("m3c", [128, 16 * 1024], BF16))
            xc_l = [sc.enter_context(nc.sbuf_tensor(f"m3xc{i}", [128, 1024], BF16))
                    for i in range(2)]
            z_l = [sc.enter_context(nc.sbuf_tensor(f"m3z{i}", [128, 1024], BF16))
                   for i in range(2)]
            a_t = [sc.enter_context(nc.sbuf_tensor(f"m3a{i}", [128, 1024], F32))
                   for i in range(2)]
            b_t = [sc.enter_context(nc.sbuf_tensor(f"m3b{i}", [128, 1024], BF16))
                   for i in range(2)]
            h_t = [sc.enter_context(nc.sbuf_tensor(f"m3h{i}", [128, 1024], BF16))
                   for i in range(2)]
            hc_t = [sc.enter_context(nc.sbuf_tensor(f"m3hc{i}", [128, 1024], BF16))
                    for i in range(2)]
            zs_t = [sc.enter_context(nc.sbuf_tensor(f"m3zs{i}", [128, 1024], F32))
                    for i in range(2)]
            y1_t = sc.enter_context(nc.sbuf_tensor("m3y1", [128, 1024], F32))
            yg_t = [sc.enter_context(nc.sbuf_tensor(f"m3yg{i}", [128, 1024], BF16))
                    for i in range(2)]
            ps_y = [sc.enter_context(nc.psum_tensor(f"m3py{i}", [128, 1024], F32))
                    for i in range(2)]
            ps_w = sc.enter_context(nc.psum_tensor("m3pw", [128, 1024], F32))
            ps_c = sc.enter_context(nc.psum_tensor("m3pc", [128, 1024], F32))
            row_a = sc.enter_context(nc.sbuf_tensor("m3ra", [1, T], F32))
            row_b = sc.enter_context(nc.sbuf_tensor("m3rb", [1, T], F32))
            row_c = sc.enter_context(nc.sbuf_tensor("m3rc", [1, T], F32))
            blk = sc.enter_context(nc.Block())
            rst()
            L0, L2, E0, E20, V0, V20, M0, S0 = \
                ld.v, ld2.v, ev.v, ev2.v, dv.v, dv2.v, mm.v, sp.v
            PB, PC, PD = pb.v, pc.v, pd.v
            P0 = po.v
            RP = [lda.v, ldb.v]
            rmark = {}
            for ps_ in range(32):
                RP[ps_ % 2] += 32
                rmark[ps_] = RP[ps_ % 2]
            LDP = [s2a.v, s2b.v]
            lmark = {}
            for g3 in range(64):
                LDP[g3 % 2] += 32
                lmark[g3] = LDP[g3 % 2]
            YP = [spa.v, spb.v]
            ymark = {}
            for g3 in range(64):
                YP[g3 % 2] += 16
                ymark[g3] = YP[g3 % 2]

            @blk.sync
            def _(e):
                for th in range(2):
                    for s in range(16):
                        ps_ = th * 16 + s
                        if ps_ >= 1:
                            e.wait_ge(pc.h, PC + ps_)
                        e.dma_start(row_b[:, 0:1024],
                                    p_sb[s:s + 1, th * 1024:(th + 1) * 1024]) \
                            .then_inc([lda, ldb][ps_ % 2].h, 16)
                        e.dma_start(row_c[:, 0:1024],
                                    p_sb[16 + s:17 + s, th * 1024:(th + 1) * 1024]) \
                            .then_inc([lda, ldb][ps_ % 2].h, 16)

            @blk.vector
            def _(e):
                it = 0
                for th in range(2):
                    for s in range(16):
                        ps_ = th * 16 + s
                        e.wait_ge([lda, ldb][ps_ % 2].h, rmark[ps_])
                        if ps_ >= 1:
                            e.wait_ge(pc.h, PC + ps_)
                        e.tensor_tensor(row_a[0:1, 0:1024], row_b[0:1, 0:1024],
                                        dt_row[0:1, th * 1024:(th + 1) * 1024],
                                        ALU.mult).then_inc(pb.h, 1)
                    for dg in range(NDG):
                        g3 = th * 32 + dg
                        for s in range(16):
                            p = it % 2
                            if s == 0:
                                e.wait_ge([s2a, s2b][g3 % 2].h, lmark[g3])
                                e.wait_ge(pd.h, PD + th * 16 + 16)
                            e.wait_ge(ev.h, E0 + it + 1)
                            e.tensor_tensor(b_t[p][:], xc_l[dg % 2][:],
                                            wsb_all[:, s * 1024:(s + 1) * 1024],
                                            ALU.mult)
                            if it >= 2:
                                e.wait_ge(po.h, P0 + it - 1)
                            if th == 0:
                                e.tensor_tensor_scan(h_t[p][:], a_t[p][:], b_t[p][:],
                                                     0.0, ALU.mult, ALU.add) \
                                    .then_inc(dv.h, 1)
                            else:
                                e.tensor_tensor_scan(
                                    h_t[p][:], a_t[p][:], b_t[p][:],
                                    carry[:, dg * 16 + s:dg * 16 + s + 1],
                                    ALU.mult, ALU.add).then_inc(dv.h, 1)
                            if th == 0:
                                e.tensor_copy(carry[:, dg * 16 + s:dg * 16 + s + 1],
                                              h_t[p][:, 1023:1024])
                            it += 1
                        e.wait_ge(mm.h, M0 + 16 * (g3 + 1))
                        e.wait_ge(ev2.h, E20 + g3 + 1)
                        if g3 >= 2:
                            e.wait_ge([spa, spb][g3 % 2].h, ymark[g3 - 2])
                        e.scalar_tensor_tensor(y1_t[:], xc_l[dg % 2][:],
                                               packT[:, 160 + dg:161 + dg],
                                               ps_y[dg % 2][:], ALU.mult, ALU.add)
                        e.tensor_tensor(yg_t[dg % 2][:], y1_t[:], zs_t[dg % 2][:],
                                        ALU.mult).then_inc(dv2.h, 1)

            dv.v = V0 + 1024
            dv2.v = V20 + 64
            pb.v = PB + 32

            @blk.tensor
            def _(e):
                it = 0
                for th in range(2):
                    for s in range(16):
                        ps_ = th * 16 + s
                        e.wait_ge(pb.h, PB + ps_ + 1)
                        e.wait_ge([lda, ldb][ps_ % 2].h, rmark[ps_])
                        if ps_ >= 1:
                            e.wait_ge(pd.h, PD + ps_)
                        for nn in range(2):
                            e.matmul(ps_w[:, nn * 512:(nn + 1) * 512],
                                     ones_r_f[0:1, :],
                                     row_a[0:1, nn * 512:(nn + 1) * 512],
                                     start=True, stop=True)
                        last = None
                        for nn in range(2):
                            last = e.matmul(ps_c[:, nn * 512:(nn + 1) * 512],
                                            ones_r_f[0:1, :],
                                            row_c[0:1, nn * 512:(nn + 1) * 512],
                                            start=True, stop=True)
                        last.then_inc(pc.h, 1)
                    for dg in range(NDG):
                        g3 = th * 32 + dg
                        for s in range(16):
                            p = it % 2
                            e.wait_ge(po.h, P0 + it + 1)
                            if s == 0 and g3 >= 2:
                                e.wait_ge(dv2.h, V20 + g3 - 1)
                            last = None
                            for nn in range(2):
                                last = e.matmul(
                                    ps_y[dg % 2][:, nn * 512:(nn + 1) * 512],
                                    ident_b[:],
                                    hc_t[p][:, nn * 512:(nn + 1) * 512],
                                    start=(s == 0), stop=(s == 15))
                            last.then_inc(mm.h, 1)
                            it += 1

            mm.v = M0 + 1024
            pc.v = PC + 32

            @blk.scalar
            def _(e):
                it = 0
                for th in range(2):
                    for s in range(16):
                        ps_ = th * 16 + s
                        e.wait_ge(pc.h, PC + ps_ + 1)
                        if th == 1:
                            e.wait_ge(po.h, P0 + 512)
                        e.copy(wsb_all[:, s * 1024:(s + 1) * 1024], ps_w[:])
                        e.copy(cb_all[:, s * 1024:(s + 1) * 1024], ps_c[:]) \
                            .then_inc(pd.h, 1)
                    for dg in range(NDG):
                        g3 = th * 32 + dg
                        for s in range(16):
                            p = it % 2
                            if it >= 2:
                                e.wait_ge(dv.h, V0 + it - 1)
                            e.activation(a_t[p][:],
                                         dtb_sb[:, th * 1024:(th + 1) * 1024],
                                         AF.Exp,
                                         scale=aneg[:, dg * 16 + s:dg * 16 + s + 1]) \
                                .then_inc(ev.h, 1)
                            it += 1
                        e.wait_ge([s2a, s2b][g3 % 2].h, lmark[g3])
                        if g3 >= 2:
                            e.wait_ge(dv2.h, V20 + g3 - 1)
                        e.activation(zs_t[dg % 2][:], z_l[dg % 2][:], AF.Silu) \
                            .then_inc(ev2.h, 1)

            ev.v = E0 + 1024
            ev2.v = E20 + 64
            pd.v = PD + 32

            @blk.gpsimd
            def _(e):
                it = 0
                for th in range(2):
                    for dg in range(NDG):
                        g3 = th * 32 + dg
                        if dg == 0:
                            if g3 >= 2:
                                e.wait_ge(dv2.h, V20 + g3 - 1)
                            e.dma_start(xc_l[0][:],
                                        sc_xc[dg * 128:(dg + 1) * 128,
                                              th * 1024:(th + 1) * 1024]) \
                                .then_inc([s2a, s2b][g3 % 2].h, 16)
                            e.dma_start(z_l[0][:],
                                        sc_z[dg * 128:(dg + 1) * 128,
                                             th * 1024:(th + 1) * 1024]) \
                                .then_inc([s2a, s2b][g3 % 2].h, 16)
                        if dg < NDG - 1:
                            ng = g3 + 1
                            if ng >= 2:
                                e.wait_ge(dv2.h, V20 + ng - 1)
                            e.dma_start(xc_l[(dg + 1) % 2][:],
                                        sc_xc[(dg + 1) * 128:(dg + 2) * 128,
                                              th * 1024:(th + 1) * 1024]) \
                                .then_inc([s2a, s2b][ng % 2].h, 16)
                            e.dma_start(z_l[(dg + 1) % 2][:],
                                        sc_z[(dg + 1) * 128:(dg + 2) * 128,
                                             th * 1024:(th + 1) * 1024]) \
                                .then_inc([s2a, s2b][ng % 2].h, 16)
                        for s in range(16):
                            p = it % 2
                            if s == 0:
                                e.wait_ge(pd.h, PD + th * 16 + 16)
                            e.wait_ge(dv.h, V0 + it + 1)
                            if it >= 2:
                                e.wait_ge(mm.h, M0 + it - 1)
                            e.tensor_tensor(hc_t[p][:], h_t[p][:],
                                            cb_all[:, s * 1024:(s + 1) * 1024],
                                            ALU.mult).then_inc(po.h, 1)
                            it += 1
                        e.wait_ge(dv2.h, V20 + g3 + 1)
                        e.dma_start(sc_yg[dg * 128:(dg + 1) * 128,
                                          th * 1024:(th + 1) * 1024],
                                    yg_t[dg % 2][:]) \
                            .then_inc([spa, spb][g3 % 2].h, 16)
                e.wait_ge(spa.h, YP[0])
                e.wait_ge(spb.h, YP[1])

            po.v = P0 + 1024

            lda.v, ldb.v = RP[0], RP[1]
            spa.v, spb.v = YP[0], YP[1]
            s2a.v, s2b.v = LDP[0], LDP[1]

            if debug:
                @blk.sync
                def _(e):
                    e.wait_ge(spa.h, YP[0])
                    e.wait_ge(spb.h, YP[1])
                    e.dma_start(dbg["dbg_xc0"], sc_xc[0:128, :]).then_inc(ld.h, 16)
                    e.dma_start(dbg["dbg_z0"], sc_z[0:128, :]).then_inc(ld.h, 16)
                    e.dma_start(dbg["dbg_yg0"], sc_yg[0:128, :]).then_inc(ld.h, 16)
                    e.wait_ge(ld.h, ld.v + 48)
                ld.v += 48

        m2ctx.close()
        mctx.close()

        with contextlib.ExitStack() as sc:
            yg_h = sc.enter_context(nc.sbuf_tensor("m4yg", [128, 32 * 1024], BF16))
            wstg4 = [sc.enter_context(nc.sbuf_tensor(f"m4ws{i}", [128, 512], F32))
                     for i in range(2)]
            wb4 = [sc.enter_context(nc.sbuf_tensor(f"m4wb{i}", [128, 32 * 512], BF16))
                   for i in range(2)]
            s_t = [sc.enter_context(nc.sbuf_tensor(f"m4st{i}", [128, 512], F32))
                   for i in range(2)]
            ps4 = [sc.enter_context(nc.psum_tensor(f"m4p{i}", [128, 512], F32))
                   for i in range(4)]
            blk = sc.enter_context(nc.Block())
            rst()
            L0, L2, E0, M0, P0, S20 = ld.v, ld2.v, ev.v, mm.v, po.v, sp2.v
            WP = [l2a.v, l2b.v]
            wmark = {}
            for jk in range(256):
                WP[jk % 2] += 16
                wmark[jk] = WP[jk % 2]
            STP = [spa.v, spb.v]
            smark = {}
            for gg in range(64):
                STP[gg % 2] += 16
                smark[gg] = STP[gg % 2]
            S2A0 = s2a.v

            PB4 = pb.v
            pb.v = PB4 + 256

            @blk.sync
            def _(e):
                for nh in range(2):
                    for mg in range(4):
                        j = nh * 4 + mg
                        for k in range(32):
                            jk = j * 32 + k
                            if jk >= 2:
                                e.wait_ge(pb.h, PB4 + jk - 1)
                            e.dma_start(wstg4[k % 2][:],
                                        outWT[k * 128:(k + 1) * 128,
                                              mg * 512:(mg + 1) * 512]) \
                                .then_inc([l2a, l2b][jk % 2].h, 16)

            @blk.gpsimd
            def _(e):
                for k in range(32):
                    e.dma_start(yg_h[:, k * 1024:(k + 1) * 1024],
                                sc_yg[k * 128:(k + 1) * 128, 0:1024]) \
                        .then_inc(s2a.h, 16)
                for j in range(8):
                    for g in range(8):
                        ms, nsub = g // 2, g % 2
                        gg = j * 8 + g
                        nh, mg = j // 4, j % 4
                        m2 = mg * 4 + ms
                        e.wait_ge(ev.h, E0 + gg + 1)
                        e.dma_start(sc_s[m2 * 128:(m2 + 1) * 128,
                                         nh * 1024 + nsub * 512:
                                         nh * 1024 + nsub * 512 + 512],
                                    s_t[gg % 2][:]) \
                            .then_inc([spa, spb][gg % 2].h, 16)
                    if j == 3:
                        e.wait_ge(mm.h, M0 + 32)
                        for k in range(32):
                            e.dma_start(yg_h[:, k * 1024:(k + 1) * 1024],
                                        sc_yg[k * 128:(k + 1) * 128, 1024:2048]) \
                                .then_inc(s2a.h, 16)
                e.wait_ge(spa.h, STP[0])
                e.wait_ge(spb.h, STP[1])

            spa.v, spb.v = STP[0], STP[1]
            s2a.v = S2A0 + 16 * 64



            @blk.tensor
            def _(e):
                for nh in range(2):
                    for mg in range(4):
                        j = nh * 4 + mg
                        e.wait_ge(pb.h, PB4 + 32 * (j + 1))
                        e.wait_ge(s2a.h, S2A0 + 16 * 32 * (nh + 1))
                        for g in range(8):
                            ms, nsub = g // 2, g % 2
                            gg = j * 8 + g
                            if gg >= 4:
                                e.wait_ge(ev.h, E0 + gg - 3)
                            last = None
                            for k in range(32):
                                last = e.matmul(
                                    ps4[g % 4][:],
                                    wb4[j % 2][:, k * 512 + ms * 128:
                                               k * 512 + ms * 128 + 128],
                                    yg_h[:, k * 1024 + nsub * 512:
                                         k * 1024 + nsub * 512 + 512],
                                    start=(k == 0), stop=(k == 31))
                            last.then_inc(mm.h, 1)

            mm.v = M0 + 64

            @blk.scalar
            def _(e):
                def cast4(j, g):
                    for k in range(4 * g, 4 * g + 4):
                        jk = j * 32 + k
                        e.wait_ge([l2a, l2b][jk % 2].h, wmark[jk])
                        e.copy(wb4[j % 2][:, k * 512:(k + 1) * 512],
                               wstg4[k % 2][:]).then_inc(pb.h, 1)
                for j in range(2):
                    for g in range(8):
                        cast4(j, g)
                for j in range(8):
                    for g in range(8):
                        gg = j * 8 + g
                        e.wait_ge(mm.h, M0 + gg + 1)
                        if gg >= 2:
                            e.wait_ge([spa, spb][gg % 2].h, smark[gg - 2])
                        e.copy(s_t[gg % 2][:], ps4[gg % 4][:]).then_inc(ev.h, 1)
                        if 1 <= j < 7:
                            cast4(j + 1, g)

            ev.v = E0 + 64

            l2a.v, l2b.v = WP[0], WP[1]

            if debug:
                @blk.sync
                def _(e):
                    e.wait_ge(spa.h, STP[0])
                    e.wait_ge(spb.h, STP[1])
                    e.dma_start(dbg["dbg_s0"], sc_s[0:128, :]).then_inc(ld.h, 16)
                    e.wait_ge(ld.h, ld.v + 16)
                ld.v += 16


        rmsctx = contextlib.ExitStack()
        rms_b = rmsctx.enter_context(nc.sbuf_tensor("rms_b", [128, T], F32))

        h1ctx = contextlib.ExitStack()
        h1_sb = h1ctx.enter_context(nc.sbuf_tensor("h1_sb", [128, NKT * T], BF16))

        with contextlib.ExitStack() as sc:
            xn_l = [sc.enter_context(nc.sbuf_tensor(f"f0x{i}", [128, T], BF16))
                    for i in range(2)]
            sq2 = [sc.enter_context(nc.sbuf_tensor(f"f0q{i}", [128, T], F32))
                   for i in range(2)]
            ps_r = [sc.enter_context(nc.psum_tensor(f"f0p{i}", [128, 512], F32))
                    for i in range(4)]
            row_a = sc.enter_context(nc.sbuf_tensor("f0ra", [1, T], F32))
            row_c = sc.enter_context(nc.sbuf_tensor("f0rc", [1, T], F32))
            blk = sc.enter_context(nc.Block())
            rst()
            L0, L2, E0, M0, V0 = ld.v, ld2.v, ev.v, mm.v, dv.v
            LP = [lda.v, ldb.v]
            xmark = {}
            for kt in range(NKT):
                LP[kt % 2] += 16
                xmark[kt] = LP[kt % 2]
            L2P = [l2a.v, l2b.v]
            xmark2 = {}
            for kt in range(NKT):
                L2P[kt % 2] += 16
                xmark2[kt] = L2P[kt % 2]

            @blk.sync
            def _(e):
                for kt in range(NKT):
                    if kt >= 2:
                        e.wait_ge(ev.h, E0 + kt - 1)
                    e.dma_start(xn_l[kt % 2][:],
                                sc_xn[kt * 128:(kt + 1) * 128, :]) \
                        .then_inc([lda, ldb][kt % 2].h, 16)

            lda.v, ldb.v = LP[0], LP[1]

            @blk.scalar
            def _(e):
                for kt in range(NKT):
                    e.wait_ge([lda, ldb][kt % 2].h, xmark[kt])
                    if kt >= 2:
                        e.wait_ge(mm.h, M0 + kt - 1)
                    e.activation(sq2[kt % 2][:], xn_l[kt % 2][:], AF.Square) \
                        .then_inc(ev.h, 1)
                e.wait_ge(dv.h, V0 + 1)
                e.activation(row_c[:], row_a[:], AF.Sqrt, bias=eps6[0:1, 0:1]) \
                    .then_inc(ev.h, 1)
                e.wait_ge(mm.h, M0 + NKT + 1)
                last = None
                for n in range(4):
                    last = e.copy(rms_b[:, n * 512:(n + 1) * 512], ps_r[n][:])
                last.then_inc(ev.h, 1)

            ev.v = E0 + NKT + 2

            @blk.tensor
            def _(e):
                for kt in range(NKT):
                    e.wait_ge(ev.h, E0 + kt + 1)
                    last = None
                    for n in range(4):
                        last = e.matmul(ps_r[n][0:1, :], ones_c_f[:],
                                        sq2[kt % 2][:, n * 512:(n + 1) * 512],
                                        start=(kt == 0), stop=(kt == NKT - 1))
                    last.then_inc(mm.h, 1)
                e.wait_ge(dv.h, V0 + 2)
                last = None
                for n in range(4):
                    last = e.matmul(ps_r[n][:], ones_r_f[64:65, :],
                                    rows65[64:65, n * 512:(n + 1) * 512],
                                    start=True, stop=True)
                last.then_inc(mm.h, 1)

            mm.v = M0 + NKT + 1

            @blk.vector
            def _(e):
                e.wait_ge(mm.h, M0 + NKT)
                last = None
                for n in range(4):
                    last = e.tensor_scalar(row_a[0:1, n * 512:(n + 1) * 512],
                                           ps_r[n][0:1, :], 1.0 / D, None, ALU.mult)
                last.then_inc(dv.h, 1)
                e.wait_ge(ev.h, E0 + NKT + 1)
                e.reciprocal(rows65[64:65, :], row_c[:]).then_inc(dv.h, 1)

            dv.v = V0 + 2

            @blk.sync
            def _(e):
                e.wait_ge(ev.h, E0 + NKT + 2)
                for kt in range(NKT):
                    if kt >= 2:
                        e.wait_ge(dv.h, V0 + 2 + kt - 1)
                    e.dma_start(xn_l[kt % 2][:],
                                sc_xn[kt * 128:(kt + 1) * 128, :]) \
                        .then_inc([l2a, l2b][kt % 2].h, 16)

            l2a.v, l2b.v = L2P[0], L2P[1]

            @blk.vector
            def _(e):
                for kt in range(NKT):
                    e.wait_ge([l2a, l2b][kt % 2].h, xmark2[kt])
                    e.scalar_tensor_tensor(h1_sb[:, kt * T:(kt + 1) * T],
                                           xn_l[kt % 2][:],
                                           packT[:, 192 + kt:193 + kt],
                                           rms_b[:], ALU.mult, ALU.mult) \
                        .then_inc(dv.h, 1)

            dv.v = V0 + 2 + NKT

            if debug:
                @blk.sync
                def _(e):
                    e.wait_ge(dv.h, V0 + 2 + NKT)
                    e.dma_start(dbg["dbg_h10"], h1_sb[:, 0:T]).then_inc(ld.h, 16)
                    e.dma_start(dbg["dbg_rows"][3:4, :], rows65[64:65, :]).then_inc(ld.h, 16)
                    e.wait_ge(ld.h, ld.v + 32)
                ld.v += 32

        with contextlib.ExitStack() as sc:
            wsl = [sc.enter_context(nc.sbuf_tensor(f"f1sl{i}", [128, 8192], BF16))
                   for i in range(3)]
            wstg1 = [sc.enter_context(nc.sbuf_tensor(f"f1stg{i}", [128, 1024], F32))
                     for i in range(2)]
            q_sb = [sc.enter_context(nc.sbuf_tensor(f"f1q{i}", [128, T], BF16))
                    for i in range(2)]
            k_sb = [sc.enter_context(nc.sbuf_tensor(f"f1k{i}", [128, T], BF16))
                    for i in range(2)]
            v_sb = sc.enter_context(nc.sbuf_tensor("f1v", [128, T], BF16))
            vT_sb = [sc.enter_context(nc.sbuf_tensor(f"f1vt{i}", [128, T], BF16))
                     for i in range(2)]
            sqn = sc.enter_context(nc.sbuf_tensor("f1sq", [128, T], BF16))
            sqnk = sc.enter_context(nc.sbuf_tensor("f1sqk", [128, T], BF16))
            nb_col = sc.enter_context(nc.sbuf_tensor("f1nb", [128, 8], F32))
            nrx = sc.enter_context(nc.sbuf_tensor("f1nx", [128, 64], F32))
            nrxk = sc.enter_context(nc.sbuf_tensor("f1nxk", [128, 64], F32))
            nrow2 = sc.enter_context(nc.sbuf_tensor("f1nr2", [33, T], F32))
            nrow1 = sc.enter_context(nc.sbuf_tensor("f1nr1", [33, 512], F32))
            pt_t = [sc.enter_context(nc.sbuf_tensor(f"f1pt{i}", [128, 512], BF16))
                    for i in range(4)]
            ptm_t = [sc.enter_context(nc.sbuf_tensor(f"f1pm{i}", [128, 512], BF16))
                     for i in range(2)]
            rb_sb = sc.enter_context(nc.sbuf_tensor("f1rb", [128, 512], F32))
            at_t = sc.enter_context(nc.sbuf_tensor("f1at", [128, 512], BF16))
            psA = sc.enter_context(nc.psum_tensor("f1pA", [128, 512], F32))
            psB = sc.enter_context(nc.psum_tensor("f1pB", [128, 512], F32))
            psC = sc.enter_context(nc.psum_tensor("f1pC", [128, 128], BF16))
            psD = sc.enter_context(nc.psum_tensor("f1pD", [128, 128], BF16))
            psE = sc.enter_context(nc.psum_tensor("f1pE", [128, 512], F32))
            psF = sc.enter_context(nc.psum_tensor("f1pF", [128, 512], F32))
            psG = sc.enter_context(nc.psum_tensor("f1pG", [128, 512], F32))
            psH = sc.enter_context(nc.psum_tensor("f1pH", [128, 512], F32))
            PS4 = [psE, psF, psG, psH]
            blk = sc.enter_context(nc.Block())
            rst()

            mk = {}
            c = dict(mm=mm.v, mm2=mm2.v, ev=ev.v, ev2=ev2.v, dv=dv.v,
                     dv2=dv2.v, po=po.v, sp=sp.v, pb=pb.v, ld=ld.v,
                     lda=lda.v, ldb=ldb.v)

            def bump(sem, key, amt=1):
                c[sem] += amt
                mk[key] = c[sem]

            for s in range(24):
                bump('ld', ('ldW', 0, s), 32)
                bump('pb', ('cast', 0, s))
            for h in range(17):
                g = h - 1
                par = ['lda', 'ldb'][h % 2]
                if h < 16:
                    for j in range(12):
                        bump('mm', ('qkv', h, j))
                    for kt in range(16):
                        bump('mm2', ('tr', h, kt))
                    bump('mm', ('nq', h))
                    bump('mm', ('nk', h))
                    for j in range(4):
                        bump('ev', ('evac', h, j))
                    bump('ev', ('sqq', h))
                    for j in range(4, 8):
                        bump('ev', ('evac', h, j))
                    bump('ev', ('sqk', h))
                    for j in range(8, 12):
                        bump('ev', ('evac', h, j))
                    for kt in range(16):
                        bump('dv', ('vt', h, kt))
                    bump('dv', ('nqc', h))
                    bump('dv', ('nkc', h))
                    for r in range(1, 6):
                        bump(par, ('pyqd', h, r), 16)
                        bump(par, ('pykd', h, r), 16)
                    if h % 4 == 3 and h // 4 + 1 < 4:
                        gr = h // 4 + 1
                        for s in range(24):
                            bump('ld', ('ldW', gr, s), 32)
                            bump('pb', ('cast', gr, s))
                for r in range(1, 6):
                    if h < 16:
                        bump('dv', ('pyq', h, r))
                        bump('dv', ('pyk', h, r))
                    if h >= 1 and r >= 2:
                        qc = r - 2
                        bump('dv', ('rc', g, qc))
                        bump('dv2', ('at', g, qc))
                if h >= 1:
                    jj = 0
                    for qc in range(4):
                        nkt_q = 4 * qc + 4
                        for kt in range(nkt_q):
                            bump('mm', ('sc', g, jj))
                            if kt == 2 and qc >= 1:
                                bump('mm', ('rb', g, qc - 1))
                            bump('mm2', ('pv', g, jj))
                            bump('ev2', ('exp', g, jj))
                            if kt >= 4 * qc:
                                bump('po', ('aff', g, jj))
                            jj += 1
                        bump('ev', ('rbe', g, qc))
                        bump('sp', ('sp', g, qc), 16)
                    bump('mm', ('rb', g, 3))
                if h < 16:
                    bump('mm', ('negB', h))
                    bump('ev', ('sn', h))
                    bump('ev', ('nbe', h))

            WMATS = [wqT, wkT, wvT]

            def slab_stage(e, gr, s):
                mat, c8 = s // 8, s % 8
                if s >= 2:
                    e.wait_ge(pb.h, mk[('cast', gr, s - 2)])
                elif gr >= 1:
                    e.wait_ge(pb.h, mk[('cast', gr - 1, 22 + s)])
                for kq in range(2):
                    ktg = c8 * 2 + kq
                    e.dma_start(wstg1[s % 2][:, kq * 512:(kq + 1) * 512],
                                WMATS[mat][ktg * 128:(ktg + 1) * 128,
                                           gr * 512:(gr + 1) * 512]) \
                        .then_inc(ld.h, 16)

            @blk.sync
            def _(e):
                for s in range(24):
                    slab_stage(e, 0, s)
                for h in range(16):
                    par = [lda, ldb][h % 2]
                    bnd = (h % 4 == 3 and h // 4 + 1 < 4)
                    gr = h // 4 + 1
                    e.wait_ge(dv.h, mk[('nqc', h)])
                    e.dma_start(nrx[:, 0:16], nrow2[0:1, 0:2048]).then_inc(par.h, 16)
                    e.wait_ge(dv.h, mk[('nkc', h)])
                    e.dma_start(nrxk[:, 0:16], nrow2[32:33, 0:2048]) \
                        .then_inc(par.h, 16)
                    if bnd:
                        for s_ in range(8):
                            slab_stage(e, gr, s_)
                    e.wait_ge(dv.h, mk[('pyq', h, 1)])
                    e.dma_start(nrx[0:16, 24:32], nrx[:, 16:17]).then_inc(par.h, 16)
                    e.wait_ge(dv.h, mk[('pyk', h, 1)])
                    e.dma_start(nrxk[0:16, 24:32], nrxk[:, 16:17]).then_inc(par.h, 16)
                    e.wait_ge(dv.h, mk[('pyq', h, 2)])
                    e.dma_start(nrx[0:4, 40:44], nrx[0:16, 32:33]).then_inc(par.h, 16)
                    e.wait_ge(dv.h, mk[('pyk', h, 2)])
                    e.dma_start(nrxk[0:4, 40:44], nrxk[0:16, 32:33]).then_inc(par.h, 16)
                    e.wait_ge(dv.h, mk[('pyq', h, 3)])
                    e.dma_start(nrx[0:2, 48:50], nrx[0:4, 44:45]).then_inc(par.h, 16)
                    e.wait_ge(dv.h, mk[('pyk', h, 3)])
                    e.dma_start(nrxk[0:2, 48:50], nrxk[0:4, 44:45]).then_inc(par.h, 16)
                    e.wait_ge(dv.h, mk[('pyq', h, 4)])
                    e.dma_start(nrx[0:1, 56:58], nrx[0:2, 50:51]).then_inc(par.h, 16)
                    e.wait_ge(dv.h, mk[('pyk', h, 4)])
                    e.dma_start(nrxk[0:1, 56:58], nrxk[0:2, 50:51]).then_inc(par.h, 16)
                    if bnd:
                        for s_ in range(8, 24):
                            slab_stage(e, gr, s_)

            @blk.tensor
            def _(e):
                for h in range(17):
                    g = h - 1
                    hh = h % 4
                    if h < 16:
                        for j in range(12):
                            mat, n = j // 4, j % 4
                            if j < 2 and h >= 1:
                                e.wait_ge(ev.h, mk[('nbe', h - 1)])
                            if j >= 2:
                                e.wait_ge(ev.h, mk[('evac', h, j - 2)])
                            if n == 0:
                                e.wait_ge(pb.h, mk[('cast', h // 4, 8 * mat + 7)])
                            last = None
                            for kt in range(NKT):
                                last = e.matmul(
                                    [psA, psB][j % 2][:],
                                    wsl[mat][:, kt * 512 + hh * 128:
                                             kt * 512 + hh * 128 + 128],
                                    h1_sb[:, kt * T + n * 512:kt * T + n * 512 + 512],
                                    start=(kt == 0), stop=(kt == NKT - 1))
                            last.then_inc(mm.h, 1)
                        for kt in range(16):
                            if kt == 0:
                                e.wait_ge(ev.h, mk[('evac', h, 11)])
                            if kt >= 2:
                                e.wait_ge(dv.h, mk[('vt', h, kt - 2)])
                            elif h >= 1:
                                e.wait_ge(dv.h, mk[('vt', h - 1, 15)])
                            e.transpose([psC, psD][kt % 2][:],
                                        v_sb[:, kt * 128:(kt + 1) * 128],
                                        ident_b[:]).then_inc(mm2.h, 1)
                        for half in range(2):
                            if half == 0:
                                e.wait_ge(ev.h, mk[('sqq', h)])
                                if h >= 2:
                                    e.wait_ge(ev2.h, mk[('exp', h - 2, 39)])
                                    e.wait_ge(dv.h, mk[('rc', h - 2, 3)])
                                    e.wait_ge(dv2.h, mk[('at', h - 2, 3)])
                                src2 = sqn
                            else:
                                e.wait_ge(ev.h, mk[('sqk', h)])
                                e.wait_ge(dv.h, mk[('nqc', h)])
                                src2 = sqnk
                            last = None
                            for n in range(4):
                                last = e.matmul(PS4[n][0:1, :], ones_c_b[:],
                                                src2[:, n * 512:(n + 1) * 512],
                                                start=True, stop=True)
                            last.then_inc(mm.h, 1)
                    if h >= 1:
                        SCB = [psE, psF, psA, psB]
                        jjb = 0
                        for qc in range(4):
                            nkt_q = 4 * qc + 4
                            for step in range(nkt_q + 4):
                                if step < nkt_q:
                                    jj = jjb + step
                                    kt = step
                                    if jj >= 4:
                                        e.wait_ge(ev2.h, mk[('exp', g, jj - 4)])
                                    elif jj >= 2:
                                        if h < 16:
                                            e.wait_ge(ev.h, mk[('evac', h, 11)])
                                        else:
                                            e.wait_ge(ev.h, mk[('nbe', 15)])
                                    else:
                                        if h < 16:
                                            e.wait_ge(dv.h, mk[('nkc', h)])
                                        else:
                                            e.wait_ge(ev2.h, mk[('exp', g - 1, 39)])
                                            e.wait_ge(ev.h, mk[('rbe', g - 1, 3)])
                                            e.wait_ge(dv.h, mk[('rc', g - 1, 3)])
                                            e.wait_ge(dv2.h, mk[('at', g - 1, 3)])
                                    e.matmul(SCB[jj % 4][:],
                                             k_sb[g % 2][:, kt * 128:(kt + 1) * 128],
                                             q_sb[g % 2][:, qc * 512:(qc + 1) * 512],
                                             start=True, stop=True).then_inc(mm.h, 1)
                                    if step == 2 and qc >= 1:
                                        e.wait_ge(dv.h, mk[('rc', g, qc - 1)])
                                        e.matmul(psG[:], ones_r_f[0:1, :],
                                                 nrow1[0:1, 0:512],
                                                 start=True, stop=True) \
                                            .then_inc(mm.h, 1)
                                if step >= 4:
                                    jj = jjb + step - 4
                                    kt = step - 4
                                    e.wait_ge(ev2.h, mk[('exp', g, jj)])
                                    if kt >= 4 * qc:
                                        e.wait_ge(po.h, mk[('aff', g, jj)])
                                        src = ptm_t[jj % 2]
                                    else:
                                        src = pt_t[jj % 4]
                                    if kt == 0 and qc >= 1:
                                        e.wait_ge(ev.h, mk[('rbe', g, qc - 1)])
                                        e.wait_ge(dv2.h, mk[('at', g, qc - 1)])
                                    e.matmul(psG[0:1, :], ones_c_b[:], src[:],
                                             start=(kt == 0), stop=(kt == nkt_q - 1))
                                    e.matmul(psH[:],
                                             vT_sb[g % 2][:, kt * 128:(kt + 1) * 128],
                                             src[:],
                                             start=(kt == 0), stop=(kt == nkt_q - 1)) \
                                        .then_inc(mm2.h, 1)
                            jjb += nkt_q
                        e.wait_ge(dv.h, mk[('rc', g, 3)])
                        e.matmul(psG[:], ones_r_f[0:1, :], nrow1[0:1, 0:512],
                                 start=True, stop=True).then_inc(mm.h, 1)
                    if h < 16:
                        if h >= 1:
                            e.wait_ge(ev2.h, mk[('exp', h - 1, 38)])
                        e.wait_ge(ev.h, mk[('sn', h)])
                        e.matmul(psA[:, 0:1], ones_r_f[0:1, :], nrx[0:1, 63:64],
                                 start=True, stop=True).then_inc(mm.h, 1)

            @blk.scalar
            def _(e):
                for s in range(24):
                    e.wait_ge(ld.h, mk[('ldW', 0, s)])
                    mat, c8 = s // 8, s % 8
                    e.copy(wsl[mat][:, c8 * 1024:(c8 + 1) * 1024],
                           wstg1[s % 2][:]).then_inc(pb.h, 1)
                for h in range(17):
                    g = h - 1
                    bnd = (h < 16 and h % 4 == 3 and h // 4 + 1 < 4)
                    gr = h // 4 + 1

                    def cast4(s0):
                        for s in range(s0, s0 + 4):
                            mat, c8 = s // 8, s % 8
                            e.wait_ge(ld.h, mk[('ldW', gr, s)])
                            e.wait_ge(mm.h, mk[('qkv', h, 4 * mat + 3)])
                            e.copy(wsl[mat][:, c8 * 1024:(c8 + 1) * 1024],
                                   wstg1[s % 2][:]).then_inc(pb.h, 1)
                    if h < 16:
                        for j in range(12):
                            mat, n = j // 4, j % 4
                            e.wait_ge(mm.h, mk[('qkv', h, j)])
                            dst = [q_sb[h % 2], k_sb[h % 2], v_sb][mat]
                            if mat == 0:
                                e.activation(dst[:, n * 512:(n + 1) * 512],
                                             [psA, psB][j % 2][:],
                                             AF.Identity, scale=RSQ_HD) \
                                    .then_inc(ev.h, 1)
                            else:
                                e.copy(dst[:, n * 512:(n + 1) * 512],
                                       [psA, psB][j % 2][:]).then_inc(ev.h, 1)
                            if j == 3:
                                e.activation(sqn[:], q_sb[h % 2][:], AF.Square) \
                                    .then_inc(ev.h, 1)
                            if j == 7:
                                e.activation(sqnk[:], k_sb[h % 2][:], AF.Square) \
                                    .then_inc(ev.h, 1)
                        if bnd:
                            cast4(0)
                            cast4(4)
                    if h >= 1:
                        jj = 0
                        for qc in range(4):
                            nkt_q = 4 * qc + 4
                            for kt in range(nkt_q):
                                e.wait_ge(mm.h, mk[('sc', g, jj)])
                                if jj >= 4:
                                    e.wait_ge(mm2.h, mk[('pv', g, jj - 4)])
                                e.activation(pt_t[jj % 4][:],
                                             [psE, psF, psA, psB][jj % 4][:],
                                             AF.Exp, bias=nb_col[:, 0:1]) \
                                    .then_inc(ev2.h, 1)
                                jj += 1
                            e.wait_ge(mm.h, mk[('rb', g, qc)])
                            e.copy(rb_sb[:], psG[:]).then_inc(ev.h, 1)
                            if bnd and qc >= 2:
                                cast4(8 * qc - 8)
                                cast4(8 * qc - 4)
                    if h < 16:
                        e.wait_ge(dv.h, mk[('pyk', h, 5)])
                        e.activation(nrx[0:1, 62:63], nrx[0:1, 60:61],
                                     AF.Sqrt, scale=nrx[0:1, 61:62])
                        e.activation(nrx[0:1, 63:64], nrx[0:1, 62:63],
                                     AF.Identity, scale=-1.0).then_inc(ev.h, 1)
                        e.wait_ge(mm.h, mk[('negB', h)])
                        e.copy(nb_col[:, 0:1], psA[:, 0:1]).then_inc(ev.h, 1)

            @blk.vector
            def _(e):
                for h in range(17):
                    g = h - 1
                    par = [lda, ldb][h % 2]
                    if h < 16:
                        for kt in range(16):
                            e.wait_ge(mm2.h, mk[('tr', h, kt)])
                            e.tensor_copy(vT_sb[h % 2][:, kt * 128:(kt + 1) * 128],
                                          [psC, psD][kt % 2][:]).then_inc(dv.h, 1)
                        e.wait_ge(mm.h, mk[('nq', h)])
                        last = None
                        for n in range(4):
                            last = e.tensor_copy(nrow2[0:1, n * 512:(n + 1) * 512],
                                                 PS4[n][0:1, :])
                        last.then_inc(dv.h, 1)
                        e.wait_ge(mm.h, mk[('nk', h)])
                        last = None
                        for n in range(4):
                            last = e.tensor_copy(nrow2[32:33, n * 512:(n + 1) * 512],
                                                 PS4[n][0:1, :])
                        last.then_inc(dv.h, 1)
                    for r in range(1, 6):
                        if h < 16:
                            e.wait_ge(par.h, mk[('pyqd', h, r)])
                            if r == 1:
                                e.reduce_max(nrx[:, 16:17], nrx[:, 0:16], AX.X) \
                                    .then_inc(dv.h, 1)
                            elif r == 2:
                                e.reduce_max(nrx[0:16, 32:33], nrx[0:16, 24:32],
                                             AX.X).then_inc(dv.h, 1)
                            elif r == 3:
                                e.reduce_max(nrx[0:4, 44:45], nrx[0:4, 40:44],
                                             AX.X).then_inc(dv.h, 1)
                            elif r == 4:
                                e.reduce_max(nrx[0:2, 50:51], nrx[0:2, 48:50],
                                             AX.X).then_inc(dv.h, 1)
                            else:
                                e.tensor_tensor(nrx[0:1, 60:61], nrx[0:1, 56:57],
                                                nrx[0:1, 57:58], ALU.max) \
                                    .then_inc(dv.h, 1)
                            e.wait_ge(par.h, mk[('pykd', h, r)])
                            if r == 1:
                                e.reduce_max(nrxk[:, 16:17], nrxk[:, 0:16], AX.X) \
                                    .then_inc(dv.h, 1)
                            elif r == 2:
                                e.reduce_max(nrxk[0:16, 32:33], nrxk[0:16, 24:32],
                                             AX.X).then_inc(dv.h, 1)
                            elif r == 3:
                                e.reduce_max(nrxk[0:4, 44:45], nrxk[0:4, 40:44],
                                             AX.X).then_inc(dv.h, 1)
                            elif r == 4:
                                e.reduce_max(nrxk[0:2, 50:51], nrxk[0:2, 48:50],
                                             AX.X).then_inc(dv.h, 1)
                            else:
                                e.tensor_tensor(nrx[0:1, 61:62], nrxk[0:1, 56:57],
                                                nrxk[0:1, 57:58], ALU.max) \
                                    .then_inc(dv.h, 1)
                        if h >= 1 and r >= 2:
                            qc = r - 2
                            e.wait_ge(mm2.h, mk[('pv', g, sum(4 * q_ + 4 for q_ in range(qc + 1)) - 1)])
                            e.tensor_copy(nrow1[32:33, 0:512], psG[0:1, :])
                            e.reciprocal(nrow1[0:1, 0:512], nrow1[32:33, 0:512]) \
                                .then_inc(dv.h, 1)
                            e.wait_ge(ev.h, mk[('rbe', g, qc)])
                            if (g, qc) != (0, 0):
                                pk = ('sp', g, qc - 1) if qc >= 1 else ('sp', g - 1, 3)
                                e.wait_ge(sp.h, mk[pk])
                            e.tensor_tensor(at_t[:], psH[:], rb_sb[:], ALU.mult) \
                                .then_inc(dv2.h, 1)

            @blk.gpsimd
            def _(e):
                for h in range(1, 17):
                    g = h - 1
                    jj = 0
                    for qc in range(4):
                        nkt_q = 4 * qc + 4
                        for kt in range(nkt_q):
                            if kt >= 4 * qc:
                                e.wait_ge(ev2.h, mk[('exp', g, jj)])
                                if jj >= 2:
                                    e.wait_ge(mm2.h, mk[('pv', g, jj - 2)])
                                e.affine_select(ptm_t[jj % 2][:],
                                                pt_t[jj % 4][:],
                                                [[1, 512]], ALU.is_ge, 0.0,
                                                base=qc * 512 - kt * 128,
                                                channel_multiplier=-1) \
                                    .then_inc(po.h, 1)
                            jj += 1
                        e.wait_ge(dv2.h, mk[('at', g, qc)])
                        e.dma_start(sc_attn[g * 128:(g + 1) * 128,
                                            qc * 512:(qc + 1) * 512],
                                    at_t[:]).then_inc(sp.h, 16)
                e.wait_ge(sp.h, mk[('sp', 15, 3)])

            mm.v, mm2.v, ev.v, ev2.v = c['mm'], c['mm2'], c['ev'], c['ev2']
            dv.v, dv2.v, po.v, sp.v = c['dv'], c['dv2'], c['po'], c['sp']
            pb.v, ld.v = c['pb'], c['ld']
            lda.v, ldb.v = c['lda'], c['ldb']

        h1ctx.close()

        fctx = contextlib.ExitStack()
        af_b = fctx.enter_context(nc.sbuf_tensor("af_b", [128, 16 * NKT], BF16))
        mod_row = fctx.enter_context(nc.sbuf_tensor("mod_row", [16, T], F32))
        hA_row = fctx.enter_context(nc.sbuf_tensor("hA_row", [16, T], F32))
        modhA_b = fctx.enter_context(nc.sbuf_tensor("modhA_b", [16, T], BF16))

        with contextlib.ExitStack() as sc:
            attn_all = sc.enter_context(nc.sbuf_tensor("f2at", [128, NKT * T], BF16))
            wstg2 = sc.enter_context(nc.sbuf_tensor("f2stg", [128, 4096], F32))
            wo_sl = sc.enter_context(nc.sbuf_tensor("f2wo", [128, 8192], BF16))
            afstg = sc.enter_context(nc.sbuf_tensor("f2af", [128, 256], F32))
            ao_sb = [sc.enter_context(nc.sbuf_tensor(f"f2ao{i}", [128, 512], BF16))
                     for i in range(2)]
            ps_ao = [sc.enter_context(nc.psum_tensor(f"f2pa{i}", [128, 512], F32))
                     for i in range(2)]
            ps_mod = [sc.enter_context(nc.psum_tensor(f"f2pm{i}", [128, 512], F32))
                      for i in range(4)]
            blk = sc.enter_context(nc.Block())
            rst()
            L0, L2, E0, E20, M0, M20, V0, P0 = \
                ld.v, ld2.v, ev.v, ev2.v, mm.v, mm2.v, dv.v, po.v
            PBF2 = pb.v
            pb.v = PBF2 + 9

            @blk.sync
            def _(e):
                for kt in range(NKT):
                    e.dma_start(afstg[:, kt * 16:(kt + 1) * 16],
                                a_ffn[kt * 128:(kt + 1) * 128, :]).then_inc(ld.h, 16)
                for kt in range(NKT):
                    e.dma_start(attn_all[:, kt * T:(kt + 1) * T],
                                sc_attn[kt * 128:(kt + 1) * 128, :]).then_inc(lda.h, 16)
                for mg in range(4):
                    for c in range(2):
                        mc = mg * 2 + c
                        if mc >= 1:
                            e.wait_ge(pb.h, PBF2 + 1 + mc)
                        for kt in range(8):
                            ktg = c * 8 + kt
                            e.dma_start(wstg2[:, kt * 512:(kt + 1) * 512],
                                        woT[ktg * 128:(ktg + 1) * 128,
                                            mg * 512:(mg + 1) * 512]) \
                                .then_inc(ld2.h, 16)

            ld.v = L0 + 16 * 16
            ld2.v = L2 + 16 * 64
            lda.v = lda.v + 16 * 16
            SPF2 = lda.v

            @blk.tensor
            def _(e):
                e.wait_ge(lda.h, SPF2)
                e.wait_ge(pb.h, PBF2 + 1)
                for j in range(64):
                    m, n = j // 4, j % 4
                    mg = m // 4
                    if j >= 1:
                        pj = j - 1
                        pm_, pn_ = pj // 4, pj % 4
                        e.wait_ge(ev.h, E0 + pj + 1)
                        e.matmul(ps_mod[pn_][0:16, :],
                                 af_b[:, pm_ * 16:(pm_ + 1) * 16],
                                 ao_sb[pj % 2][:],
                                 start=(pm_ == 0), stop=(pm_ == 15)) \
                            .then_inc(mm2.h, 1)
                    if n == 0 and m % 4 == 0:
                        e.wait_ge(pb.h, PBF2 + 1 + 2 * (mg + 1))
                    if j >= 2:
                        e.wait_ge(ev.h, E0 + j - 1)
                    last = None
                    for kt in range(NKT):
                        last = e.matmul(ps_ao[j % 2][:],
                                        wo_sl[:, kt * 512 + (m % 4) * 128:
                                              kt * 512 + (m % 4) * 128 + 128],
                                        attn_all[:, kt * T + n * 512:kt * T + n * 512 + 512],
                                        start=(kt == 0), stop=(kt == NKT - 1))
                    last.then_inc(mm.h, 1)
                e.wait_ge(ev.h, E0 + 64)
                e.matmul(ps_mod[3][0:16, :], af_b[:, 15 * 16:16 * 16],
                         ao_sb[63 % 2][:], start=False, stop=True).then_inc(mm2.h, 1)

            mm.v = M0 + 64
            mm2.v = M20 + 64

            @blk.scalar
            def _(e):
                e.wait_ge(ld.h, L0 + 16 * 16)
                e.copy(af_b[:], afstg[:]).then_inc(pb.h, 1)
                for c in range(2):
                    e.wait_ge(ld2.h, L2 + 16 * 8 * (c + 1))
                    e.copy(wo_sl[:, c * 4096:(c + 1) * 4096],
                           wstg2[:]).then_inc(pb.h, 1)
                for j in range(64):
                    if j % 16 == 0 and j > 0:
                        mg = j // 16
                        for c in range(2):
                            mc = mg * 2 + c
                            e.wait_ge(ld2.h, L2 + 16 * 8 * (mc + 1))
                            e.wait_ge(mm.h, M0 + 16 * mg)
                            e.copy(wo_sl[:, c * 4096:(c + 1) * 4096],
                                   wstg2[:]).then_inc(pb.h, 1)
                    e.wait_ge(mm.h, M0 + j + 1)
                    if j >= 2:
                        e.wait_ge(mm2.h, M20 + j - 1)
                    e.copy(ao_sb[j % 2][:], ps_ao[j % 2][:]).then_inc(ev.h, 1)
                e.wait_ge(mm2.h, M20 + 64)
                last = None
                for n in range(4):
                    last = e.activation(mod_row[0:16, n * 512:(n + 1) * 512],
                                        ps_mod[n][0:16, :], AF.Tanh)
                last.then_inc(ev.h, 1)

            ev.v = E0 + 65

            @blk.vector
            def _(e):
                e.wait_ge(ev.h, E0 + 65)
                e.tensor_scalar(mod_row[0:16, :], mod_row[0:16, :], 0.1, None,
                                ALU.mult).then_inc(dv.h, 1)

            dv.v = V0 + 1

            if debug:
                @blk.sync
                def _(e):
                    e.wait_ge(dv.h, V0 + 1)
                    e.dma_start(dbg["dbg_attn0"], sc_attn[0:128, :]).then_inc(ld.h, 16)
                    e.dma_start(dbg["dbg_mod"], mod_row[0:16, :]).then_inc(ld.h, 16)
                    e.wait_ge(ld.h, ld.v + 32)
                ld.v += 32

        h2ctx = contextlib.ExitStack()
        h2_sb = h2ctx.enter_context(nc.sbuf_tensor("h2_sb", [128, NKT * T], BF16))
        with contextlib.ExitStack() as sc:
            xn_l = [sc.enter_context(nc.sbuf_tensor(f"f3x{i}", [128, T], BF16))
                    for i in range(2)]
            ps_h = [sc.enter_context(nc.psum_tensor(f"f3p{i}", [128, 512], F32))
                    for i in range(4)]
            blk = sc.enter_context(nc.Block())
            rst()
            L0, E0, M0, V0 = ld.v, ev.v, mm.v, dv.v
            LP = [lda.v, ldb.v]
            xmark = {}
            for kt in range(NKT):
                LP[kt % 2] += 16
                xmark[kt] = LP[kt % 2]

            @blk.sync
            def _(e):
                for kt in range(NKT):
                    if kt >= 2:
                        e.wait_ge(dv.h, V0 + kt - 1)
                    e.dma_start(xn_l[kt % 2][:],
                                sc_xn[kt * 128:(kt + 1) * 128, :]) \
                        .then_inc([lda, ldb][kt % 2].h, 16)

            lda.v, ldb.v = LP[0], LP[1]

            @blk.vector
            def _(e):
                for kt in range(NKT):
                    e.wait_ge([lda, ldb][kt % 2].h, xmark[kt])
                    e.scalar_tensor_tensor(h2_sb[:, kt * T:(kt + 1) * T],
                                           xn_l[kt % 2][:],
                                           packT[:, 208 + kt:209 + kt],
                                           rms_b[:], ALU.mult, ALU.mult) \
                        .then_inc(dv.h, 1)
                e.wait_ge(ev.h, E0 + 1)
                e.tensor_tensor(modhA_b[0:16, :], mod_row[0:16, :], hA_row[0:16, :],
                                ALU.mult).then_inc(dv.h, 1)

            dv.v = V0 + NKT + 1

            @blk.tensor
            def _(e):
                for kt in range(NKT):
                    e.wait_ge(dv.h, V0 + kt + 1)
                    last = None
                    for n in range(4):
                        last = e.matmul(ps_h[n][0:16, :],
                                        af_b[:, kt * 16:(kt + 1) * 16],
                                        h2_sb[:, kt * T + n * 512:kt * T + n * 512 + 512],
                                        start=(kt == 0), stop=(kt == NKT - 1))
                    last.then_inc(mm.h, 1)

            mm.v = M0 + NKT

            @blk.scalar
            def _(e):
                e.wait_ge(mm.h, M0 + NKT)
                last = None
                for n in range(4):
                    last = e.copy(hA_row[0:16, n * 512:(n + 1) * 512],
                                  ps_h[n][0:16, :])
                last.then_inc(ev.h, 1)

            ev.v = E0 + 1

            if debug:
                @blk.sync
                def _(e):
                    e.wait_ge(dv.h, V0 + NKT + 1)
                    e.dma_start(dbg["dbg_hA"], hA_row[0:16, :]).then_inc(ld.h, 16)
                    e.wait_ge(ld.h, ld.v + 16)
                ld.v += 16

        with contextlib.ExitStack() as sc:
            wstg4f = sc.enter_context(nc.sbuf_tensor("f4stg", [128, 2048], F32))
            wib = [sc.enter_context(nc.sbuf_tensor(f"f4wi{i}", [128, 8192], BF16))
                   for i in range(2)]
            wub = [sc.enter_context(nc.sbuf_tensor(f"f4wu{i}", [128, 8192], BF16))
                   for i in range(2)]
            bstg = sc.enter_context(nc.sbuf_tensor("f4bs", [16, 1024], F32))
            bf_b = sc.enter_context(nc.sbuf_tensor("bf_b", [16, FF], BF16))
            gate_t = [sc.enter_context(nc.sbuf_tensor(f"f4g{i}", [128, 512], BF16))
                      for i in range(2)]
            gu_t = [sc.enter_context(nc.sbuf_tensor(f"f4u{i}", [128, 512], BF16))
                    for i in range(2)]
            ps_g = [sc.enter_context(nc.psum_tensor(f"f4pg{i}", [128, 512], F32))
                    for i in range(2)]
            ps_u = [sc.enter_context(nc.psum_tensor(f"f4pu{i}", [128, 512], F32))
                    for i in range(2)]
            blk = sc.enter_context(nc.Block())
            rst()
            L0, L2, E0, M0, M20, V0, P0, S0 = \
                ld.v, ld2.v, ev.v, mm.v, mm2.v, dv.v, po.v, sp.v
            GSP = [spa.v, spb.v]
            gmark = {}
            for g in range(256):
                GSP[g % 2] += 16
                gmark[g] = GSP[g % 2]

            PBF4 = pb.v
            pb.v = PBF4 + 136

            @blk.sync
            def _(e):
                for bc in range(8):
                    if bc >= 1:
                        e.wait_ge(pb.h, PBF4 + bc)
                    e.dma_start(bstg[:], b_ffn[:, bc * 1024:(bc + 1) * 1024]) \
                        .then_inc(ld.h, 16)
                for fq in range(16):
                    for mat, wsrc in enumerate([winbT, wupbT]):
                        for q4 in range(4):
                            mc = fq * 8 + mat * 4 + q4
                            if mc >= 1:
                                e.wait_ge(pb.h, PBF4 + 8 + mc)
                            for kq in range(4):
                                ktg = q4 * 4 + kq
                                e.dma_start(wstg4f[:, kq * 512:(kq + 1) * 512],
                                            wsrc[ktg * 128:(ktg + 1) * 128,
                                                 fq * 512:(fq + 1) * 512]) \
                                    .then_inc(ld2.h, 16)

            ld.v = L0 + 128
            ld2.v = L2 + 16 * 512

            @blk.gpsimd
            def _(e):
                for g in range(256):
                    f, n = g // 4, g % 4
                    e.wait_ge(dv.h, V0 + g + 1)
                    e.dma_start(sc_gu[f * 128:(f + 1) * 128,
                                      n * 512:(n + 1) * 512],
                                gu_t[g % 2][:]) \
                        .then_inc([spa, spb][g % 2].h, 16)
                e.wait_ge(spa.h, GSP[0])
                e.wait_ge(spb.h, GSP[1])

            spa.v, spb.v = GSP[0], GSP[1]

            @blk.tensor
            def _(e):
                for fq in range(16):
                    e.wait_ge(pb.h, PBF4 + 8 + 8 * (fq + 1))
                    for ms in range(4):
                        f = fq * 4 + ms
                        for n in range(4):
                            g = f * 4 + n
                            if g >= 2:
                                e.wait_ge(ev.h, E0 + g - 1)
                            last = None
                            for kt in range(NKT):
                                last = e.matmul(
                                    ps_g[g % 2][:],
                                    wib[fq % 2][:, kt * 512 + ms * 128:
                                               kt * 512 + ms * 128 + 128],
                                    h2_sb[:, kt * T + n * 512:kt * T + n * 512 + 512],
                                    start=(kt == 0), stop=False)
                            e.matmul(ps_g[g % 2][:], bf_b[:, f * 128:(f + 1) * 128],
                                     modhA_b[0:16, n * 512:(n + 1) * 512],
                                     start=False, stop=True).then_inc(mm.h, 1)
                            if g >= 2:
                                e.wait_ge(dv.h, V0 + g - 1)
                            last = None
                            for kt in range(NKT):
                                last = e.matmul(
                                    ps_u[g % 2][:],
                                    wub[fq % 2][:, kt * 512 + ms * 128:
                                               kt * 512 + ms * 128 + 128],
                                    h2_sb[:, kt * T + n * 512:kt * T + n * 512 + 512],
                                    start=(kt == 0), stop=(kt == NKT - 1))
                            last.then_inc(mm2.h, 1)

            mm.v = M0 + 256
            mm2.v = M20 + 256

            @blk.scalar
            def _(e):
                for g in range(256):
                    e.wait_ge(mm.h, M0 + g + 1)
                    if g >= 2:
                        e.wait_ge(dv.h, V0 + g - 1)
                    e.activation(gate_t[g % 2][:], ps_g[g % 2][:], AF.Silu) \
                        .then_inc(ev.h, 1)

            ev.v = E0 + 256

            @blk.vector
            def _(e):
                def wcast1(fq, i):
                    mat, q4 = i // 4, i % 4
                    mc = fq * 8 + mat * 4 + q4
                    e.wait_ge(ld2.h, L2 + 16 * 4 * (mc + 1))
                    dst = [wib, wub][mat][fq % 2]
                    e.tensor_copy(dst[:, q4 * 2048:(q4 + 1) * 2048],
                                  wstg4f[:]).then_inc(pb.h, 1)
                for bc in range(8):
                    e.wait_ge(ld.h, L0 + 16 * (bc + 1))
                    e.tensor_copy(bf_b[:, bc * 1024:(bc + 1) * 1024],
                                  bstg[:]).then_inc(pb.h, 1)
                for i in range(8):
                    wcast1(0, i)
                for i in range(8):
                    wcast1(1, i)
                for g in range(256):
                    if g % 2 == 0 and g >= 16 and g // 16 + 1 <= 15:
                        wcast1(g // 16 + 1, (g % 16) // 2)
                    e.wait_ge(ev.h, E0 + g + 1)
                    e.wait_ge(mm2.h, M20 + g + 1)
                    if g >= 2:
                        e.wait_ge([spa, spb][g % 2].h, gmark[g - 2])
                    e.tensor_tensor(gu_t[g % 2][:], gate_t[g % 2][:],
                                    ps_u[g % 2][:], ALU.mult).then_inc(dv.h, 1)

            dv.v = V0 + 256


            if debug:
                @blk.sync
                def _(e):
                    e.wait_ge(spa.h, GSP[0])
                    e.wait_ge(spb.h, GSP[1])
                    e.dma_start(dbg["dbg_gu0"], sc_gu[0:128, :]).then_inc(ld.h, 16)
                    e.wait_ge(ld.h, ld.v + 16)
                ld.v += 16

        h2ctx.close()
        fctx.close()
        rmsctx.close()

        with contextlib.ExitStack() as sc:
            gu_sb = sc.enter_context(nc.sbuf_tensor("f5gu", [128, 64 * 1024], BF16))
            wstg5 = [sc.enter_context(nc.sbuf_tensor(f"f5ws{i}", [128, 512], F32))
                     for i in range(4)]
            wd_b = [sc.enter_context(nc.sbuf_tensor(f"f5wd{i}", [128, 512], BF16))
                    for i in range(8)]
            mean_b5 = sc.enter_context(nc.sbuf_tensor("f5mb", [128, 1024], F32))
            rstd_b5 = sc.enter_context(nc.sbuf_tensor("f5rb", [128, 1024], F32))
            x_l = [sc.enter_context(nc.sbuf_tensor(f"f5x{i}", [128, 512], F32))
                   for i in range(4)]
            s_l = [sc.enter_context(nc.sbuf_tensor(f"f5s{i}", [128, 512], F32))
                   for i in range(4)]
            g_t = sc.enter_context(nc.sbuf_tensor("f5g", [128, 512], F32))
            t1 = sc.enter_context(nc.sbuf_tensor("f5t1", [128, 512], F32))
            t2 = sc.enter_context(nc.sbuf_tensor("f5t2", [128, 512], F32))
            o_t = [sc.enter_context(nc.sbuf_tensor(f"f5o{i}", [128, 512], F32))
                   for i in range(2)]
            ps5 = [sc.enter_context(nc.psum_tensor(f"f5p{i}", [128, 512], F32))
                   for i in range(8)]
            blk = sc.enter_context(nc.Block())
            rst()
            L0, L2, E0, E20, M0, M20, V0, V20, P0, S0, FI = \
                ld.v, ld2.v, ev.v, ev2.v, mm.v, mm2.v, dv.v, dv2.v, po.v, sp.v, fin.v
            WP5 = [l2a.v, l2b.v, lda.v, ldb.v]
            wmark5 = {}
            for kk in range(512):
                WP5[kk % 4] += 16
                wmark5[kk] = WP5[kk % 4]
            XSP = [x.v for x in xs4]
            xsmark = {}
            for gg in range(64):
                XSP[gg % 4] += 32
                xsmark[gg] = XSP[gg % 4]
            FOP = [fo2[0].v, fo2[1].v]
            fomark = {}
            for gg in range(64):
                FOP[gg % 2] += 16
                fomark[gg] = FOP[gg % 2]

            @blk.tensor
            def _(e):
                for nh in range(2):
                    if nh == 1:
                        e.wait_ge(ev2.h, E20 + 1)
                        e.wait_ge(dv2.h, V20 + 32)
                    for nn in range(2):
                        e.matmul(ps5[nn][:], ones_r_f[0:1, :],
                                 rows65[0:1, nh * 1024 + nn * 512:
                                          nh * 1024 + nn * 512 + 512],
                                 start=True, stop=True)
                    last = None
                    for nn in range(2):
                        last = e.matmul(ps5[2 + nn][:], ones_r_f[32:33, :],
                                        rows65[32:33, nh * 1024 + nn * 512:
                                                 nh * 1024 + nn * 512 + 512],
                                        start=True, stop=True)
                    last.then_inc(mm2.h, 1)
                    e.wait_ge(ld.h, L0 + 16 * 64 * (nh + 1))
                    e.wait_ge(ev2.h, E20 + nh + 1)
                    for mq in range(4):
                        hj = nh * 4 + mq
                        if hj >= 1:
                            e.wait_ge(dv2.h, V20 + 8 * hj)
                        for k in range(64):
                            kk = hj * 64 + k
                            e.wait_ge(po.h, P0 + kk + 1)
                            last = None
                            for g in range(8):
                                ms, nsub = g // 2, g % 2
                                last = e.matmul(ps5[g][:],
                                                wd_b[k % 8][:, ms * 128:(ms + 1) * 128],
                                                gu_sb[:, k * 1024 + nsub * 512:
                                                      k * 1024 + nsub * 512 + 512],
                                                start=(k == 0), stop=(k == 63))
                            last.then_inc(mm.h, 1)

            mm.v = M0 + 512
            mm2.v = M20 + 2

            @blk.scalar
            def _(e):
                def wcast(kk):
                    e.wait_ge([l2a, l2b, lda, ldb][kk % 4].h, wmark5[kk])
                    if kk >= 8:
                        e.wait_ge(mm.h, M0 + kk - 7)
                    e.copy(wd_b[kk % 8][:], wstg5[kk % 4][:]) \
                        .then_inc(po.h, 1)

                for hj in range(9):
                    nh, mq = hj // 4, hj % 4
                    if hj >= 1:
                        for g in range(8):
                            gg = (hj - 1) * 8 + g
                            e.wait_ge(dv.h, V0 + 3 * gg + 3)
                            e.activation(g_t[:], t2[:], AF.Silu).then_inc(ev.h, 1)
                    if hj < 8 and mq == 0:
                        e.wait_ge(mm2.h, M20 + nh + 1)
                        for nn in range(2):
                            e.copy(mean_b5[:, nn * 512:(nn + 1) * 512], ps5[nn][:])
                        last = None
                        for nn in range(2):
                            last = e.copy(rstd_b5[:, nn * 512:(nn + 1) * 512],
                                          ps5[2 + nn][:])
                        last.then_inc(ev2.h, 1)
                    if hj < 8:
                        for k in range(64):
                            wcast(hj * 64 + k)

            ev.v = E0 + 64
            ev2.v = E20 + 2
            po.v = P0 + 512

            @blk.vector
            def _(e):
                for nh in range(2):
                    for mq in range(4):
                        hj = nh * 4 + mq
                        e.wait_ge(mm.h, M0 + 64 * (hj + 1))
                        for g in range(8):
                            ms, nsub = g // 2, g % 2
                            gg = hj * 8 + g
                            e.wait_ge(xs4[gg % 4].h, xsmark[gg])
                            e.tensor_tensor(t1[:], x_l[g % 4][:],
                                            mean_b5[:, nsub * 512:(nsub + 1) * 512],
                                            ALU.subtract)
                            e.tensor_tensor(t2[:], t1[:],
                                            rstd_b5[:, nsub * 512:(nsub + 1) * 512],
                                            ALU.mult)
                            e.tensor_tensor(t2[:], t2[:], ps5[g][:], ALU.add) \
                                .then_inc(dv.h, 3)
                            e.wait_ge(ev.h, E0 + gg + 1)
                            e.tensor_tensor(t1[:], g_t[:], s_l[g % 4][:], ALU.mult)
                            if gg >= 2:
                                e.wait_ge(fo2[gg % 2].h, fomark[gg - 2])
                            e.tensor_tensor(o_t[gg % 2][:], t1[:], x_l[g % 4][:],
                                            ALU.add).then_inc(dv2.h, 1)

            dv.v = V0 + 3 * 64
            dv2.v = V20 + 64

            @blk.gpsimd
            def _(e):
                def coords(gg):
                    hj, g = gg // 8, gg % 8
                    nh, mq = hj // 4, hj % 4
                    ms, nsub = g // 2, g % 2
                    m2 = mq * 4 + ms
                    col = nh * 1024 + nsub * 512
                    return g, m2, col

                def load(gg):
                    g, m2, col = coords(gg)
                    if gg >= 4:
                        e.wait_ge(dv2.h, V20 + gg - 3)
                    e.dma_start(x_l[g % 4][:],
                                xT[m2 * 128:(m2 + 1) * 128, col:col + 512]) \
                        .then_inc(xs4[gg % 4].h, 16)
                    e.dma_start(s_l[g % 4][:],
                                sc_s[m2 * 128:(m2 + 1) * 128, col:col + 512]) \
                        .then_inc(xs4[gg % 4].h, 16)

                for gg in range(4):
                    load(gg)
                for hj in range(8):
                    for g in range(8):
                        gg = hj * 8 + g
                        _, m2, col = coords(gg)
                        e.wait_ge(dv2.h, V20 + gg + 1)
                        e.dma_start(outT[m2 * 128:(m2 + 1) * 128, col:col + 512],
                                    o_t[gg % 2][:]).then_inc(fo2[gg % 2].h, 16)
                        if gg + 4 < 64:
                            load(gg + 4)

            for i in range(4):
                xs4[i].v = XSP[i]
            fo2[0].v, fo2[1].v = FOP[0], FOP[1]

            @blk.sync
            def _(e):
                for nh in range(2):
                    if nh == 1:
                        e.wait_ge(mm.h, M0 + 256)
                    for k in range(64):
                        e.dma_start(gu_sb[:, k * 1024:(k + 1) * 1024],
                                    sc_gu[k * 128:(k + 1) * 128,
                                          nh * 1024:(nh + 1) * 1024]) \
                            .then_inc(ld.h, 16)
                    for mq in range(4):
                        hj = nh * 4 + mq
                        for k in range(64):
                            kk = hj * 64 + k
                            if kk >= 4:
                                e.wait_ge(po.h, P0 + kk - 3)
                            e.dma_start(wstg5[kk % 4][:],
                                        wdownT[k * 128:(k + 1) * 128,
                                               mq * 512:(mq + 1) * 512]) \
                                .then_inc([l2a, l2b, lda, ldb][kk % 4].h, 16)
                e.wait_ge(fo2[0].h, FOP[0])
                e.wait_ge(fo2[1].h, FOP[1])

            ld.v = L0 + 16 * 128
            l2a.v, l2b.v = WP5[0], WP5[1]
            lda.v, ldb.v = WP5[2], WP5[3]


    return nc



def _f32c(a):
    return np.ascontiguousarray(np.asarray(a), dtype=np.float32)


def kernel(**inputs):
    global LAST_DEBUG
    I = inputs
    x = _f32c(I["L0_attn_x"])
    W_in = _f32c(I["L0_attn_W_in"])
    conv_w = _f32c(I["L0_attn_conv_w"])
    conv_b = _f32c(I["L0_attn_conv_b"])
    W_x = _f32c(I["L0_attn_W_x"])
    A_log = _f32c(I["L0_attn_A_log"])
    dt_bias = _f32c(I["L0_attn_dt_bias"])
    out_W = _f32c(I["L0_attn_out_W"])
    Dvec = _f32c(I["L0_attn_D"])
    ln1 = _f32c(I["L0_ffn_ln1_w"])
    ln2 = _f32c(I["L0_ffn_ln2_w"])
    Wq = _f32c(I["L0_ffn_Wq"])
    Wk = _f32c(I["L0_ffn_Wk"])
    Wv = _f32c(I["L0_ffn_Wv"])
    Wo = _f32c(I["L0_ffn_Wo"])
    A = _f32c(I["L0_ffn_A"])
    B = _f32c(I["L0_ffn_B"])
    W_in_base = _f32c(I["L0_ffn_W_in_base"])
    W_up_base = _f32c(I["L0_ffn_W_up_base"])
    W_down = _f32c(I["L0_ffn_W_down"])

    packv = np.zeros((256, 128), np.float32)
    packv[0:128] = conv_w[:, 0, :].reshape(32, 128, 4).transpose(2, 0, 1) \
        .reshape(128, 128)
    packv[128:160] = conv_b.reshape(32, 128)
    packv[160:192] = Dvec.reshape(32, 128)
    packv[192:208] = ln1.reshape(16, 128)
    packv[208:224] = ln2.reshape(16, 128)

    shared = dict(
        w_inT=_f32c(W_in.T),
        wxT=_f32c(W_x.T),
        outWT=_f32c(out_W.T),
        wqT=_f32c(Wq.T), wkT=_f32c(Wk.T), wvT=_f32c(Wv.T), woT=_f32c(Wo.T),
        a_ffn=A, b_ffn=B,
        winbT=_f32c(W_in_base.T), wupbT=_f32c(W_up_base.T),
        wdownT=_f32c(W_down.T),
        a_perm=_f32c(A_log.reshape(32, 128, 16).transpose(1, 0, 2)
                     .reshape(128, 512)),
        packv=packv,
        dtb2d=_f32c(dt_bias.reshape(32, 128)),
        ident_in=np.eye(128, dtype=np.float32),
    )

    ncores = _NCORES
    nc = build_program(debug=_DEBUG)
    in_maps = []
    for c in range(ncores):
        m = dict(shared)
        m["xT"] = _f32c(x[c].T)
        in_maps.append(m)
    res = run_bass_kernel_spmd(nc, in_maps, core_ids=list(range(ncores)))
    LAST_DEBUG = res.results
    out = np.empty((8, T, D), np.float32)
    for c in range(ncores):
        out[c] = np.asarray(res.results[c]["outT"]).T
    return out


# version 11
# speedup vs baseline: 1.0914x; 1.0914x over previous
import contextlib
import numpy as np
import ml_dtypes

import concourse.bass as bass
import concourse.mybir as mybir
from concourse.bass_utils import run_bass_kernel_spmd

F32 = mybir.dt.float32
BF16 = mybir.dt.bfloat16
AF = mybir.ActivationFunctionType
ALU = mybir.AluOpType
AX = mybir.AxisListType

D = 2048
DI = 4096
T = 2048
FF = 8192
NKT = 16
NDG = 32
NFT = 64
NS = 16
NH = 16
RSQ_HD = 0.08838834764831845

_DEBUG = False
_NCORES = 8
LAST_DEBUG = None


class Ctr:
    def __init__(self, handle):
        self.h = handle
        self.v = 0


def build_program(debug=False):
    nc = bass.Bass("TRN2", target_bir_lowering=False)

    xT = nc.dram_tensor("xT", [D, T], F32, kind="ExternalInput").ap()
    w_inT = nc.dram_tensor("w_inT", [D, 2 * DI], F32, kind="ExternalInput").ap()
    wxT = nc.dram_tensor("wxT", [DI, 33], F32, kind="ExternalInput").ap()
    outWT = nc.dram_tensor("outWT", [DI, D], F32, kind="ExternalInput").ap()
    wqT = nc.dram_tensor("wqT", [D, D], F32, kind="ExternalInput").ap()
    wkT = nc.dram_tensor("wkT", [D, D], F32, kind="ExternalInput").ap()
    wvT = nc.dram_tensor("wvT", [D, D], F32, kind="ExternalInput").ap()
    woT = nc.dram_tensor("woT", [D, D], F32, kind="ExternalInput").ap()
    a_ffn = nc.dram_tensor("a_ffn", [D, NS], F32, kind="ExternalInput").ap()
    b_ffn = nc.dram_tensor("b_ffn", [NS, FF], F32, kind="ExternalInput").ap()
    winbT = nc.dram_tensor("winbT", [D, FF], F32, kind="ExternalInput").ap()
    wupbT = nc.dram_tensor("wupbT", [D, FF], F32, kind="ExternalInput").ap()
    wdownT = nc.dram_tensor("wdownT", [FF, D], F32, kind="ExternalInput").ap()
    a_perm = nc.dram_tensor("a_perm", [128, 512], F32, kind="ExternalInput").ap()
    packv = nc.dram_tensor("packv", [256, 128], F32, kind="ExternalInput").ap()
    dtb2d = nc.dram_tensor("dtb2d", [32, 128], F32, kind="ExternalInput").ap()
    ident_in = nc.dram_tensor("ident_in", [128, 128], F32, kind="ExternalInput").ap()

    outT = nc.dram_tensor("outT", [D, T], F32, kind="ExternalOutput").ap()

    sc_xn = nc.dram_tensor("sc_xn", [D, T], BF16, kind="Internal").ap()
    sc_h1 = nc.dram_tensor("sc_h1", [D, T], BF16, kind="Internal").ap()
    sc_h2 = nc.dram_tensor("sc_h2", [D, T], BF16, kind="Internal").ap()
    sc_xc = nc.dram_tensor("sc_xc", [DI, T], BF16, kind="Internal").ap()
    sc_z = nc.dram_tensor("sc_z", [DI, T], BF16, kind="Internal").ap()
    sc_yg = nc.dram_tensor("sc_yg", [DI, T], BF16, kind="Internal").ap()
    sc_s = nc.dram_tensor("sc_s", [D, T], F32, kind="Internal").ap()
    sc_attn = nc.dram_tensor("sc_attn", [D, T], BF16, kind="Internal").ap()
    sc_gu = nc.dram_tensor("sc_gu", [FF, T], BF16, kind="Internal").ap()

    dbg = {}
    if debug:
        for nm, shp, dt_ in [("dbg_p", [33, T], F32), ("dbg_rows", [4, T], F32),
                             ("dbg_xc0", [128, T], BF16), ("dbg_z0", [128, T], BF16),
                             ("dbg_yg0", [128, T], BF16), ("dbg_s0", [128, T], F32),
                             ("dbg_xn0", [128, T], BF16), ("dbg_h10", [128, T], BF16),
                             ("dbg_attn0", [128, T], BF16), ("dbg_mod", [16, T], F32),
                             ("dbg_gu0", [128, T], BF16), ("dbg_hA", [16, T], F32),
                             ("dbg_aux", [4, T], F32)]:
            dbg[nm] = nc.dram_tensor(nm, shp, dt_, kind="ExternalOutput").ap()

    ctx = contextlib.ExitStack()
    with ctx:
        packT = ctx.enter_context(nc.sbuf_tensor("packT", [128, 256], F32))
        ident_f = ctx.enter_context(nc.sbuf_tensor("ident_f", [128, 128], F32))
        ident_b = ctx.enter_context(nc.sbuf_tensor("ident_b", [128, 128], BF16))
        ones_r_f = ctx.enter_context(nc.sbuf_tensor("ones_r_f", [65, 128], F32))
        ones_c_f = ctx.enter_context(nc.sbuf_tensor("ones_c_f", [128, 1], F32))
        ones_c_b = ctx.enter_context(nc.sbuf_tensor("ones_c_b", [128, 1], BF16))
        ones32 = ctx.enter_context(nc.sbuf_tensor("ones32", [32, 1], F32))
        mdtb = ctx.enter_context(nc.sbuf_tensor("mdtb", [1, 128], F32))
        rows65 = ctx.enter_context(nc.sbuf_tensor("rows65", [65, T], F32))
        dt_row = ctx.enter_context(nc.sbuf_tensor("dt_row", [1, T], F32))
        hA_g = ctx.enter_context(nc.sbuf_tensor("hA_g", [16, T], F32))
        eps5 = ctx.enter_context(nc.sbuf_tensor("eps5", [1, 1], F32))
        eps6 = ctx.enter_context(nc.sbuf_tensor("eps6", [1, 1], F32))

        sems = {}
        for nm in ["ld", "ld2", "sp", "sp2", "mm", "mm2", "ev", "ev2",
                   "dv", "dv2", "po", "fin", "pb", "pc", "pd",
                   "lda", "ldb", "l2a", "l2b", "spa", "spb", "s2a", "s2b",
                   "xs0", "xs1", "xs2", "xs3", "fo0", "fo1"]:
            sems[nm] = Ctr(ctx.enter_context(nc.semaphore(nm)))
        ld, ld2, sp, sp2 = sems["ld"], sems["ld2"], sems["sp"], sems["sp2"]
        mm, mm2, ev, ev2 = sems["mm"], sems["mm2"], sems["ev"], sems["ev2"]
        dv, dv2, po, fin = sems["dv"], sems["dv2"], sems["po"], sems["fin"]
        pb, pc, pd = sems["pb"], sems["pc"], sems["pd"]
        lda, ldb = sems["lda"], sems["ldb"]
        l2a, l2b = sems["l2a"], sems["l2b"]
        spa, spb = sems["spa"], sems["spb"]
        s2a, s2b = sems["s2a"], sems["s2b"]
        xs4 = [sems["xs0"], sems["xs1"], sems["xs2"], sems["xs3"]]
        fo2 = [sems["fo0"], sems["fo1"]]
        def rst():
            pass

        mctx = contextlib.ExitStack()
        wx_b = mctx.enter_context(nc.sbuf_tensor("wx_b", [128, 33 * NDG], BF16))
        aneg = mctx.enter_context(nc.sbuf_tensor("aneg", [128, 512], F32))
        p_sb = mctx.enter_context(nc.sbuf_tensor("p_sb", [33, T], F32))

        with contextlib.ExitStack() as sc:
            stg = sc.enter_context(nc.sbuf_tensor("s_stg", [128, 1056], F32))
            stg2 = sc.enter_context(nc.sbuf_tensor("s_stg2", [128, 256], F32))
            dtb32 = sc.enter_context(nc.sbuf_tensor("s_dtb32", [32, 128], F32))
            s_row = sc.enter_context(nc.sbuf_tensor("s_row", [1, 256], F32))
            expa = sc.enter_context(nc.sbuf_tensor("s_expa", [128, 512], F32))
            ps_a = sc.enter_context(nc.psum_tensor("s_psa", [128, 512], F32))
            ps_b = sc.enter_context(nc.psum_tensor("s_psb", [128, 512], F32))
            blk = sc.enter_context(nc.Block())
            rst()
            L0, M0, E0, P0, V0 = ld.v, mm.v, ev.v, po.v, dv.v

            @blk.sync
            def _(e):
                e.dma_start(ident_f[:], ident_in).then_inc(ld.h, 16)
                e.dma_start(stg2[:, 0:128], packv[0:128, :]).then_inc(ld.h, 16)
                e.dma_start(stg2[:, 128:256], packv[128:256, :]).then_inc(ld.h, 16)
                e.dma_start(expa[:], a_perm).then_inc(ld.h, 16)
                e.dma_start(dtb32[:], dtb2d).then_inc(ld.h, 16)
                for dg in range(NDG):
                    e.dma_start(stg[:, dg * 33:(dg + 1) * 33],
                                wxT[dg * 128:(dg + 1) * 128, :]).then_inc(ld.h, 16)

            ld.v = L0 + 16 * (5 + NDG)

            @blk.gpsimd
            def _(e):
                e.memset(ones_r_f[:], 1.0)
                e.memset(ones_c_f[:], 1.0)
                e.memset(ones_c_b[:], 1.0)
                e.memset(ones32[:], 1.0)
                e.memset(eps5[:], 1e-5)
                e.memset(eps6[:], 1e-6)
                e.memset(p_sb[:], 0.0).then_inc(po.h, 1)
                e.wait_ge(ld.h, L0 + 16 * (5 + NDG))
                e.tensor_copy(ident_b[:], ident_f[:])
                e.tensor_copy(wx_b[:], stg[:, 0:1056]).then_inc(po.h, 1)

            po.v = P0 + 2

            @blk.tensor
            def _(e):
                e.wait_ge(ld.h, L0 + 16 * (5 + NDG))
                e.transpose(ps_a[:, 0:128], stg2[:, 0:128], ident_f[:])
                e.transpose(ps_b[:, 0:128], stg2[:, 128:256], ident_f[:]) \
                    .then_inc(mm.h, 1)
                e.wait_ge(ev.h, E0 + 1)
                e.wait_ge(po.h, P0 + 1)
                e.wait_ge(ld.h, L0 + 16 * (5 + NDG))
                e.matmul(ps_a[0:1, 128:256], ones32[:], dtb32[:],
                         start=True, stop=True).then_inc(mm.h, 1)

            mm.v = M0 + 2

            @blk.scalar
            def _(e):
                e.wait_ge(mm.h, M0 + 1)
                e.copy(packT[:, 0:128], ps_a[:, 0:128])
                e.copy(packT[:, 128:256], ps_b[:, 0:128]).then_inc(ev.h, 1)
                e.wait_ge(ld.h, L0 + 16 * (5 + NDG))
                e.activation(aneg[:], expa[:], AF.Exp).then_inc(ev.h, 1)

            ev.v = E0 + 2

            @blk.vector
            def _(e):
                e.wait_ge(mm.h, M0 + 2)
                e.tensor_copy(s_row[0:1, 0:128], ps_a[0:1, 128:256])
                e.wait_ge(ev.h, E0 + 2)
                e.tensor_scalar(aneg[:], aneg[:], -1.0, None, ALU.mult)
                e.scalar_tensor_tensor(s_row[0:1, 128:256], s_row[0:1, 0:128],
                                       1.0 / DI, s_row[0:1, 0:128],
                                       ALU.mult, ALU.bypass,
                                       accum_out=mdtb[0:1, 0:1]) \
                    .then_inc(dv.h, 1)

            dv.v = V0 + 1

        with contextlib.ExitStack() as xscope:
            xn_sb = xscope.enter_context(
                nc.sbuf_tensor("xn_sb", [128, NKT * T], BF16))

            with contextlib.ExitStack() as sc:
                x_t = [sc.enter_context(nc.sbuf_tensor(f"p1x{i}", [128, T], F32))
                       for i in range(2)]
                xsq2 = [sc.enter_context(nc.sbuf_tensor(f"p1sq{i}", [128, T], F32))
                        for i in range(2)]
                mean_b = sc.enter_context(nc.sbuf_tensor("p1mb", [128, T], F32))
                rstd_b = sc.enter_context(nc.sbuf_tensor("p1rb", [128, T], F32))
                ps_s = [sc.enter_context(nc.psum_tensor(f"p1s{i}", [128, 512], F32))
                        for i in range(4)]
                ps_q = [sc.enter_context(nc.psum_tensor(f"p1q{i}", [128, 512], F32))
                        for i in range(4)]
                row_a = sc.enter_context(nc.sbuf_tensor("p1ra", [1, T], F32))
                row_b = sc.enter_context(nc.sbuf_tensor("p1rb_", [1, T], F32))
                row_c = sc.enter_context(nc.sbuf_tensor("p1rc", [1, T], F32))
                blk = sc.enter_context(nc.Block())
                rst()
                L0, L2, E0, M0, V0, S0 = ld.v, ld2.v, ev.v, mm.v, dv.v, sp.v
                LP = [lda.v, ldb.v]
                xmark = {}
                for kt in range(NKT):
                    LP[kt % 2] += 16
                    xmark[kt] = LP[kt % 2]
                L2P = [l2a.v, l2b.v]
                xmark2 = {}
                for kt in range(NKT):
                    L2P[kt % 2] += 16
                    xmark2[kt] = L2P[kt % 2]

                @blk.sync
                def _(e):
                    for kt in range(NKT):
                        if kt >= 2:
                            e.wait_ge(ev.h, E0 + kt - 1)
                            e.wait_ge(mm.h, M0 + 2 * (kt - 1))
                        e.dma_start(x_t[kt % 2][:],
                                    xT[kt * 128:(kt + 1) * 128, :]) \
                            .then_inc([lda, ldb][kt % 2].h, 16)

                @blk.scalar
                def _(e):
                    for kt in range(NKT):
                        e.wait_ge([lda, ldb][kt % 2].h, xmark[kt])
                        if kt >= 2:
                            e.wait_ge(mm.h, M0 + 2 * (kt - 1))
                        e.activation(xsq2[kt % 2][:], x_t[kt % 2][:], AF.Square) \
                            .then_inc(ev.h, 1)

                ev.v = E0 + NKT

                @blk.tensor
                def _(e):
                    for kt in range(NKT):
                        e.wait_ge([lda, ldb][kt % 2].h, xmark[kt])
                        for n in range(4):
                            e.matmul(ps_s[n][0:1, :], ones_c_f[:],
                                     x_t[kt % 2][:, n * 512:(n + 1) * 512],
                                     start=(kt == 0), stop=(kt == NKT - 1))
                        e.wait_ge(ev.h, E0 + kt + 1)
                        last = None
                        for n in range(4):
                            last = e.matmul(ps_q[n][0:1, :], ones_c_f[:],
                                            xsq2[kt % 2][:, n * 512:(n + 1) * 512],
                                            start=(kt == 0), stop=(kt == NKT - 1))
                        last.then_inc(mm.h, 2)
                    e.wait_ge(dv.h, V0 + 2)
                    for n in range(4):
                        e.matmul(ps_s[n][:], ones_r_f[0:1, :],
                                 rows65[0:1, n * 512:(n + 1) * 512],
                                 start=True, stop=True)
                    last = None
                    for n in range(4):
                        last = e.matmul(ps_q[n][:], ones_r_f[32:33, :],
                                        rows65[32:33, n * 512:(n + 1) * 512],
                                        start=True, stop=True)
                    last.then_inc(mm.h, 1)

                mm.v = M0 + 2 * NKT + 1

                @blk.vector
                def _(e):
                    e.wait_ge(mm.h, M0 + 2 * NKT)
                    for n in range(4):
                        e.tensor_scalar(rows65[0:1, n * 512:(n + 1) * 512],
                                        ps_s[n][0:1, :], 1.0 / D, None, ALU.mult)
                        e.tensor_scalar(row_a[0:1, n * 512:(n + 1) * 512],
                                        ps_q[n][0:1, :], 1.0 / D, None, ALU.mult)
                    e.tensor_tensor(row_b[:], rows65[0:1, :], rows65[0:1, :], ALU.mult)
                    e.tensor_tensor(row_a[:], row_a[:], row_b[:], ALU.subtract) \
                        .then_inc(dv.h, 1)
                    e.wait_ge(ev.h, E0 + NKT + 1)
                    e.reciprocal(rows65[32:33, :], row_c[:]).then_inc(dv.h, 1)

                dv.v = V0 + 2

                @blk.scalar
                def _(e):
                    e.wait_ge(dv.h, V0 + 1)
                    e.activation(row_c[:], row_a[:], AF.Sqrt, bias=eps5[0:1, 0:1]) \
                        .then_inc(ev.h, 1)
                    e.wait_ge(mm.h, M0 + 2 * NKT + 1)
                    for n in range(4):
                        e.copy(mean_b[:, n * 512:(n + 1) * 512], ps_s[n][:])
                    last = None
                    for n in range(4):
                        last = e.copy(rstd_b[:, n * 512:(n + 1) * 512], ps_q[n][:])
                    last.then_inc(ev.h, 1)

                ev.v = E0 + NKT + 2

                @blk.sync
                def _(e):
                    e.wait_ge(ev.h, E0 + NKT + 2)
                    for kt in range(NKT):
                        if kt >= 2:
                            e.wait_ge(dv.h, V0 + 2 + kt - 1)
                        e.dma_start(x_t[kt % 2][:],
                                    xT[kt * 128:(kt + 1) * 128, :]) \
                            .then_inc([l2a, l2b][kt % 2].h, 16)

                @blk.vector
                def _(e):
                    for kt in range(NKT):
                        e.wait_ge([l2a, l2b][kt % 2].h, xmark2[kt])
                        e.tensor_tensor(xsq2[kt % 2][:], x_t[kt % 2][:], mean_b[:],
                                        ALU.subtract)
                        e.tensor_tensor(xn_sb[:, kt * T:(kt + 1) * T],
                                        xsq2[kt % 2][:],
                                        rstd_b[:], ALU.mult).then_inc(dv.h, 1)

                dv.v = V0 + 2 + NKT

                @blk.gpsimd
                def _(e):
                    for kt in range(NKT):
                        e.wait_ge(dv.h, V0 + 2 + kt + 1)
                        e.dma_start(sc_xn[kt * 128:(kt + 1) * 128, :],
                                    xn_sb[:, kt * T:(kt + 1) * T]).then_inc(sp.h, 16)
                    e.wait_ge(sp.h, S0 + 16 * NKT)

                sp.v = S0 + 16 * NKT
                lda.v, ldb.v = LP[0], LP[1]
                l2a.v, l2b.v = L2P[0], L2P[1]

                if debug:
                    @blk.sync
                    def _(e):
                        e.wait_ge(dv.h, V0 + 2 + NKT)
                        e.dma_start(dbg["dbg_rows"][0:1, :], rows65[0:1, :]) \
                            .then_inc(ld2.h, 16)
                        e.dma_start(dbg["dbg_rows"][1:2, :], rows65[32:33, :]) \
                            .then_inc(ld2.h, 16)
                        e.dma_start(dbg["dbg_xn0"], sc_xn[0:128, :]) \
                            .then_inc(ld2.h, 16)
                        e.wait_ge(ld2.h, ld2.v + 48)
                    ld2.v = ld2.v + 48

            with contextlib.ExitStack() as sc:
                wstg = [sc.enter_context(nc.sbuf_tensor(f"m1wstg{i}", [128, 2048], F32))
                        for i in range(2)]
                wqb = [sc.enter_context(nc.sbuf_tensor(f"m1wq{i}", [128, 8192], BF16))
                       for i in range(2)]
                xs_pad = [sc.enter_context(nc.sbuf_tensor(f"m1xp{i}", [128, 2052], BF16))
                          for i in range(2)]
                acc_a = sc.enter_context(nc.sbuf_tensor("m1aa", [128, T], F32))
                acc_b = sc.enter_context(nc.sbuf_tensor("m1ab", [128, T], F32))
                xc_t = [sc.enter_context(nc.sbuf_tensor(f"m1xc{i}", [128, T], BF16))
                        for i in range(2)]
                z_t = [sc.enter_context(nc.sbuf_tensor(f"m1z{i}", [128, T], BF16))
                       for i in range(2)]
                ps_xz = [sc.enter_context(nc.psum_tensor(f"m1p{i}", [128, 512], F32))
                         for i in range(4)]
                ps_p = [sc.enter_context(nc.psum_tensor(f"m1pp{i}", [128, 512], F32))
                        for i in range(4)]
                blk = sc.enter_context(nc.Block())
                rst()
                L0, P0, M0, M20, E0, E20, V0, V20, S0, S20 = \
                    ld.v, po.v, mm.v, mm2.v, ev.v, ev2.v, dv.v, dv2.v, sp.v, sp2.v
                SPP = [spa.v, spb.v]
                xcmark = {}
                for m in range(32):
                    SPP[m % 2] += 16
                    xcmark[m] = SPP[m % 2]
                SZP = [s2a.v, s2b.v]
                zmark = {}
                for m in range(32, 64):
                    SZP[m % 2] += 16
                    zmark[m] = SZP[m % 2]

                @blk.gpsimd
                def _(e):
                    e.memset(xs_pad[0][:, 0:3], 0.0)
                    e.memset(xs_pad[1][:, 0:3], 0.0).then_inc(po.h, 1)

                P1 = P0 + 1
                po.v = P1

                PB1 = pb.v
                pb.v = PB1 + 64

                @blk.sync
                def _(e):
                    for q in range(16):
                        for quarter in range(4):
                            hs = 4 * q + quarter
                            if hs >= 2:
                                e.wait_ge(pb.h, PB1 + hs - 1)
                            for kt in range(4):
                                ktg = quarter * 4 + kt
                                e.dma_start(
                                    wstg[hs % 2][:, kt * 512:(kt + 1) * 512],
                                    w_inT[ktg * 128:(ktg + 1) * 128,
                                          q * 512:(q + 1) * 512]).then_inc(ld.h, 16)

                ld.v = L0 + 16 * 256

                @blk.gpsimd
                def _(e):
                    def spills(qq):
                        for ms in range(4):
                            m = qq * 4 + ms
                            if m < 32:
                                e.wait_ge(ev2.h, E20 + m + 1)
                                e.dma_start(sc_xc[m * 128:(m + 1) * 128, :],
                                            xc_t[m % 2][:]) \
                                    .then_inc([spa, spb][m % 2].h, 16)
                            else:
                                e.wait_ge(ev.h, E0 + 4 * (m + 1))
                                e.dma_start(sc_z[(m - 32) * 128:(m - 31) * 128, :],
                                            z_t[m % 2][:]) \
                                    .then_inc([s2a, s2b][m % 2].h, 16)
                    for q in range(16):
                        if q >= 2:
                            spills(q - 2)
                    spills(14)
                    spills(15)
                    e.wait_ge(spa.h, SPP[0])
                    e.wait_ge(spb.h, SPP[1])
                    e.wait_ge(s2a.h, SZP[0])
                    e.wait_ge(s2b.h, SZP[1])

                spa.v, spb.v = SPP[0], SPP[1]
                s2a.v, s2b.v = SZP[0], SZP[1]

                @blk.tensor
                def _(e):
                    for m in range(64):
                        q, msub = m // 4, m % 4
                        pm = m - 2
                        if 0 <= pm < 32:
                            e.wait_ge(ev2.h, E20 + pm + 1)
                            for n in range(4):
                                if pm >= 1:
                                    e.wait_ge(dv2.h, V20 + (pm - 1) * 4 + n + 1)
                                e.matmul(ps_p[n][0:33, :],
                                         wx_b[:, pm * 33:(pm + 1) * 33],
                                         xc_t[pm % 2][:, n * 512:(n + 1) * 512],
                                         start=True, stop=True).then_inc(mm2.h, 1)
                        if msub == 0:
                            e.wait_ge(pb.h, PB1 + 4 * (q + 1))
                        for n in range(4):
                            g = 4 * m + n
                            if g >= 4:
                                e.wait_ge(ev.h, E0 + g - 3)
                            last = None
                            for kt in range(NKT):
                                last = e.matmul(
                                    ps_xz[g % 4][:],
                                    wqb[q % 2][:, kt * 512 + msub * 128:
                                               kt * 512 + msub * 128 + 128],
                                    xn_sb[:, kt * T + n * 512:kt * T + n * 512 + 512],
                                    start=(kt == 0), stop=(kt == NKT - 1))
                            last.then_inc(mm.h, 1)

                mm.v = M0 + 256
                mm2.v = M20 + 128

                @blk.scalar
                def _(e):
                    def wcast(hs):
                        q = hs // 4
                        quarter = hs % 4
                        e.wait_ge(ld.h, L0 + 16 * 4 * (hs + 1))
                        if q >= 2 and quarter == 0:
                            e.wait_ge(mm.h, M0 + 16 * (q - 1))
                        e.copy(wqb[q % 2][:, quarter * 2048:
                                          (quarter + 1) * 2048],
                               wstg[hs % 2][:]).then_inc(pb.h, 1)
                    for hs in range(8):
                        wcast(hs)
                    for m in range(64):
                        if m % 4 == 0 and m >= 4:
                            q = m // 4 + 1
                            if q <= 15:
                                for quarter in range(4):
                                    wcast(4 * q + quarter)
                        for n in range(4):
                            g = 4 * m + n
                            e.wait_ge(mm.h, M0 + g + 1)
                            if m < 32:
                                if n == 0 and m >= 2:
                                    e.wait_ge(dv.h, V0 + m - 1)
                                e.copy(xs_pad[m % 2][:, 3 + n * 512:3 + n * 512 + 512],
                                       ps_xz[g % 4][:]).then_inc(ev.h, 1)
                            else:
                                if n == 0 and m >= 34:
                                    e.wait_ge([s2a, s2b][m % 2].h, zmark[m - 2])
                                e.copy(z_t[m % 2][:, n * 512:(n + 1) * 512],
                                       ps_xz[g % 4][:]).then_inc(ev.h, 1)
                        if m < 32:
                            e.wait_ge(dv.h, V0 + m + 1)
                            if m >= 2:
                                e.wait_ge([spa, spb][m % 2].h, xcmark[m - 2])
                                e.wait_ge(mm2.h, M20 + 4 * (m - 1))
                            e.activation(xc_t[m % 2][:], acc_b[:], AF.Silu,
                                         bias=packT[:, 128 + m:129 + m]) \
                                .then_inc(ev2.h, 1)

                ev.v = E0 + 256
                ev2.v = E20 + 32

                @blk.vector
                def _(e):
                    for m in range(32):
                        xs = xs_pad[m % 2]
                        e.wait_ge(ev.h, E0 + 4 * (m + 1))
                        e.tensor_scalar(acc_a[:], xs[:, 0:2048],
                                        packT[:, m:m + 1], None, ALU.mult)
                        if m >= 1:
                            e.wait_ge(ev2.h, E20 + m)
                        e.scalar_tensor_tensor(acc_b[:], xs[:, 1:2049],
                                               packT[:, 32 + m:33 + m], acc_a[:],
                                               ALU.mult, ALU.add)
                        e.scalar_tensor_tensor(acc_a[:], xs[:, 2:2050],
                                               packT[:, 64 + m:65 + m], acc_b[:],
                                               ALU.mult, ALU.add)
                        e.scalar_tensor_tensor(acc_b[:], xs[:, 3:2051],
                                               packT[:, 96 + m:97 + m], acc_a[:],
                                               ALU.mult, ALU.add).then_inc(dv.h, 1)
                        pm = m - 2
                        if pm >= 0:
                            for n in range(4):
                                e.wait_ge(mm2.h, M20 + pm * 4 + n + 1)
                                e.tensor_tensor(p_sb[0:33, n * 512:(n + 1) * 512],
                                                p_sb[0:33, n * 512:(n + 1) * 512],
                                                ps_p[n][0:33, :], ALU.add) \
                                    .then_inc(dv2.h, 1)
                    for pm in [30, 31]:
                        for n in range(4):
                            e.wait_ge(mm2.h, M20 + pm * 4 + n + 1)
                            e.tensor_tensor(p_sb[0:33, n * 512:(n + 1) * 512],
                                            p_sb[0:33, n * 512:(n + 1) * 512],
                                            ps_p[n][0:33, :], ALU.add) \
                                .then_inc(dv2.h, 1)

                dv.v = V0 + 32
                dv2.v = V20 + 128

        with contextlib.ExitStack() as sc:
            xn_l = [sc.enter_context(nc.sbuf_tensor(f"e0x{i}", [128, T], BF16))
                    for i in range(2)]
            sq2 = [sc.enter_context(nc.sbuf_tensor(f"e0q{i}", [128, T], F32))
                   for i in range(2)]
            h1t = [sc.enter_context(nc.sbuf_tensor(f"e0h1{i}", [128, T], BF16))
                   for i in range(2)]
            h2t = [sc.enter_context(nc.sbuf_tensor(f"e0h2{i}", [128, T], BF16))
                   for i in range(2)]
            rms_e = sc.enter_context(nc.sbuf_tensor("e0rms", [128, T], F32))
            ps_r = [sc.enter_context(nc.psum_tensor(f"e0p{i}", [128, 512], F32))
                    for i in range(4)]
            row_a = sc.enter_context(nc.sbuf_tensor("e0ra", [1, T], F32))
            row_c = sc.enter_context(nc.sbuf_tensor("e0rc", [1, T], F32))
            blk = sc.enter_context(nc.Block())
            rst()
            L0, L2, E0, M0, V0 = ld.v, ld2.v, ev.v, mm.v, dv.v
            SPP1 = sp.v
            LP = [lda.v, ldb.v]
            xmark = {}
            for kt in range(NKT):
                LP[kt % 2] += 16
                xmark[kt] = LP[kt % 2]
            L2P = [l2a.v, l2b.v]
            xmark2 = {}
            for kt in range(NKT):
                L2P[kt % 2] += 16
                xmark2[kt] = L2P[kt % 2]
            S2P = [s2a.v, s2b.v]

            @blk.sync
            def _(e):
                for kt in range(NKT):
                    e.wait_ge(sp.h, SPP1 - 16 * (NKT - 1 - kt))
                    if kt >= 2:
                        e.wait_ge(ev.h, E0 + kt - 1)
                    e.dma_start(xn_l[kt % 2][:],
                                sc_xn[kt * 128:(kt + 1) * 128, :]) \
                        .then_inc([lda, ldb][kt % 2].h, 16)
                e.wait_ge(ev.h, E0 + NKT + 2)
                for kt in range(NKT):
                    if kt >= 2:
                        e.wait_ge(dv.h, V0 + 2 + 2 * (kt - 2) + 2)
                    e.dma_start(xn_l[kt % 2][:],
                                sc_xn[kt * 128:(kt + 1) * 128, :]) \
                        .then_inc([l2a, l2b][kt % 2].h, 16)

            lda.v, ldb.v = LP[0], LP[1]
            l2a.v, l2b.v = L2P[0], L2P[1]

            @blk.scalar
            def _(e):
                for kt in range(NKT):
                    e.wait_ge([lda, ldb][kt % 2].h, xmark[kt])
                    if kt >= 2:
                        e.wait_ge(mm.h, M0 + kt - 1)
                    e.activation(sq2[kt % 2][:], xn_l[kt % 2][:], AF.Square) \
                        .then_inc(ev.h, 1)
                e.wait_ge(dv.h, V0 + 1)
                e.activation(row_c[:], row_a[:], AF.Sqrt, bias=eps6[0:1, 0:1]) \
                    .then_inc(ev.h, 1)
                e.wait_ge(mm.h, M0 + NKT + 1)
                last = None
                for n in range(4):
                    last = e.copy(rms_e[:, n * 512:(n + 1) * 512], ps_r[n][:])
                last.then_inc(ev.h, 1)

            ev.v = E0 + NKT + 2

            @blk.tensor
            def _(e):
                e.wait_ge(dv.h, V0)
                for kt in range(NKT):
                    e.wait_ge(ev.h, E0 + kt + 1)
                    last = None
                    for n in range(4):
                        last = e.matmul(ps_r[n][0:1, :], ones_c_f[:],
                                        sq2[kt % 2][:, n * 512:(n + 1) * 512],
                                        start=(kt == 0), stop=(kt == NKT - 1))
                    last.then_inc(mm.h, 1)
                e.wait_ge(dv.h, V0 + 2)
                last = None
                for n in range(4):
                    last = e.matmul(ps_r[n][:], ones_r_f[64:65, :],
                                    rows65[64:65, n * 512:(n + 1) * 512],
                                    start=True, stop=True)
                last.then_inc(mm.h, 1)

            mm.v = M0 + NKT + 1

            @blk.vector
            def _(e):
                e.wait_ge(mm.h, M0 + NKT)
                last = None
                for n in range(4):
                    last = e.tensor_scalar(row_a[0:1, n * 512:(n + 1) * 512],
                                           ps_r[n][0:1, :], 1.0 / D, None, ALU.mult)
                last.then_inc(dv.h, 1)
                e.wait_ge(ev.h, E0 + NKT + 1)
                e.reciprocal(rows65[64:65, :], row_c[:]).then_inc(dv.h, 1)
                e.wait_ge(ev.h, E0 + NKT + 2)
                for kt in range(NKT):
                    e.wait_ge([l2a, l2b][kt % 2].h, xmark2[kt])
                    if kt >= 2:
                        e.wait_ge(s2a.h, S2P[0] + 16 * (kt - 1))
                        e.wait_ge(s2b.h, S2P[1] + 16 * (kt - 1))
                    e.scalar_tensor_tensor(h1t[kt % 2][:],
                                           xn_l[kt % 2][:],
                                           packT[:, 192 + kt:193 + kt],
                                           rms_e[:], ALU.mult, ALU.mult) \
                        .then_inc(dv.h, 1)
                    e.scalar_tensor_tensor(h2t[kt % 2][:],
                                           xn_l[kt % 2][:],
                                           packT[:, 208 + kt:209 + kt],
                                           rms_e[:], ALU.mult, ALU.mult) \
                        .then_inc(dv.h, 1)

            dv.v = V0 + 2 + 2 * NKT

            @blk.gpsimd
            def _(e):
                for kt in range(NKT):
                    e.wait_ge(dv.h, V0 + 2 + 2 * kt + 2)
                    e.dma_start(sc_h1[kt * 128:(kt + 1) * 128, :],
                                h1t[kt % 2][:]).then_inc(s2a.h, 16)
                    e.dma_start(sc_h2[kt * 128:(kt + 1) * 128, :],
                                h2t[kt % 2][:]).then_inc(s2b.h, 16)
                e.wait_ge(s2a.h, S2P[0] + 16 * NKT)
                e.wait_ge(s2b.h, S2P[1] + 16 * NKT)

            s2a.v = S2P[0] + 16 * NKT
            s2b.v = S2P[1] + 16 * NKT

        m2ctx = contextlib.ExitStack()
        dtb_sb = m2ctx.enter_context(nc.sbuf_tensor("dtb_sb", [128, T], F32))
        carry = m2ctx.enter_context(nc.sbuf_tensor("carry", [128, 512], F32))
        with contextlib.ExitStack() as sc:
            ps_bc = [sc.enter_context(nc.psum_tensor(f"m2b{i}", [128, 512], F32))
                     for i in range(4)]
            row_a = sc.enter_context(nc.sbuf_tensor("m2ra", [1, T], F32))
            row_b = sc.enter_context(nc.sbuf_tensor("m2rb", [1, T], F32))
            row_c = sc.enter_context(nc.sbuf_tensor("m2rc", [1, T], F32))
            blk = sc.enter_context(nc.Block())
            rst()
            L0, E0, V0, M0 = ld.v, ev.v, dv.v, mm.v

            @blk.sync
            def _(e):
                e.dma_start(row_a[:], p_sb[32:33, :]).then_inc(ld.h, 16)
                if debug:
                    e.wait_ge(ev.h, E0 + 2)
                    e.dma_start(dbg["dbg_p"], p_sb[:]).then_inc(ld.h, 16)
                    e.dma_start(dbg["dbg_rows"][2:3, :], dt_row[0:1, :]).then_inc(ld.h, 16)
                    e.dma_start(dbg["dbg_aux"][0:1, :], row_a[:]).then_inc(ld.h, 16)
                    e.dma_start(dbg["dbg_aux"][1:2, :], row_b[:]).then_inc(ld.h, 16)
                    e.dma_start(dbg["dbg_aux"][2:3, :], row_c[:]).then_inc(ld.h, 16)
                    e.dma_start(dbg["dbg_aux"][3:4, 0:1], mdtb[0:1, 0:1]).then_inc(ld.h, 16)
                    e.wait_ge(ld.h, L0 + 16 * 7)

            ld.v = L0 + (16 * 7 if debug else 16)

            @blk.scalar
            def _(e):
                e.wait_ge(ld.h, L0 + 16)
                e.activation(row_b[:], row_a[:], AF.Abs, bias=mdtb[0:1, 0:1])
                e.activation(row_c[:], row_b[:], AF.Exp, scale=-1.0)
                e.activation(row_b[:], row_c[:], AF.Ln, bias=1.0)
                e.activation(row_c[:], row_a[:], AF.Relu, bias=mdtb[0:1, 0:1]) \
                    .then_inc(ev.h, 1)
                e.wait_ge(mm.h, M0 + 1)
                last = None
                for n in range(4):
                    last = e.copy(dtb_sb[:, n * 512:(n + 1) * 512], ps_bc[n][:])
                last.then_inc(ev.h, 1)

            ev.v = E0 + 2

            @blk.vector
            def _(e):
                e.wait_ge(ev.h, E0 + 1)
                e.tensor_tensor(dt_row[0:1, :], row_b[:], row_c[:], ALU.add) \
                    .then_inc(dv.h, 1)

            dv.v = V0 + 1

            @blk.tensor
            def _(e):
                e.wait_ge(ev.h, E0)
                e.wait_ge(dv.h, V0 + 1)
                last = None
                for n in range(4):
                    last = e.matmul(ps_bc[n][:], ones_r_f[0:1, :],
                                    dt_row[0:1, n * 512:(n + 1) * 512],
                                    start=True, stop=True)
                last.then_inc(mm.h, 1)

            mm.v = M0 + 1

        with contextlib.ExitStack() as sc:
            wsb_all = sc.enter_context(nc.sbuf_tensor("m3w", [128, 16 * 1024], BF16))
            cb_all = sc.enter_context(nc.sbuf_tensor("m3c", [128, 16 * 1024], BF16))
            xc_l = [sc.enter_context(nc.sbuf_tensor(f"m3xc{i}", [128, 1024], BF16))
                    for i in range(2)]
            z_l = [sc.enter_context(nc.sbuf_tensor(f"m3z{i}", [128, 1024], BF16))
                   for i in range(2)]
            a_t = [sc.enter_context(nc.sbuf_tensor(f"m3a{i}", [128, 1024], F32))
                   for i in range(2)]
            b_t = [sc.enter_context(nc.sbuf_tensor(f"m3b{i}", [128, 1024], BF16))
                   for i in range(2)]
            h_t = [sc.enter_context(nc.sbuf_tensor(f"m3h{i}", [128, 1024], BF16))
                   for i in range(2)]
            hc_t = [sc.enter_context(nc.sbuf_tensor(f"m3hc{i}", [128, 1024], BF16))
                    for i in range(2)]
            zs_t = [sc.enter_context(nc.sbuf_tensor(f"m3zs{i}", [128, 1024], F32))
                    for i in range(2)]
            y1_t = sc.enter_context(nc.sbuf_tensor("m3y1", [128, 1024], F32))
            yg_t = [sc.enter_context(nc.sbuf_tensor(f"m3yg{i}", [128, 1024], BF16))
                    for i in range(2)]
            ps_y = [sc.enter_context(nc.psum_tensor(f"m3py{i}", [128, 1024], F32))
                    for i in range(2)]
            ps_w = sc.enter_context(nc.psum_tensor("m3pw", [128, 1024], F32))
            ps_c = sc.enter_context(nc.psum_tensor("m3pc", [128, 1024], F32))
            row_a = sc.enter_context(nc.sbuf_tensor("m3ra", [1, T], F32))
            row_b = sc.enter_context(nc.sbuf_tensor("m3rb", [1, T], F32))
            row_c = sc.enter_context(nc.sbuf_tensor("m3rc", [1, T], F32))
            blk = sc.enter_context(nc.Block())
            rst()
            L0, L2, E0, E20, V0, V20, M0, S0 = \
                ld.v, ld2.v, ev.v, ev2.v, dv.v, dv2.v, mm.v, sp.v
            PB, PC, PD = pb.v, pc.v, pd.v
            RP = [lda.v, ldb.v]
            rmark = {}
            for ps_ in range(32):
                RP[ps_ % 2] += 32
                rmark[ps_] = RP[ps_ % 2]
            LDP = [s2a.v, s2b.v]
            lmark = {}
            for g3 in range(64):
                LDP[g3 % 2] += 32
                lmark[g3] = LDP[g3 % 2]
            YP = [spa.v, spb.v]
            ymark = {}
            for g3 in range(64):
                YP[g3 % 2] += 16
                ymark[g3] = YP[g3 % 2]

            @blk.sync
            def _(e):
                for th in range(2):
                    for s in range(16):
                        ps_ = th * 16 + s
                        if ps_ >= 1:
                            e.wait_ge(pc.h, PC + ps_)
                        e.dma_start(row_b[:, 0:1024],
                                    p_sb[s:s + 1, th * 1024:(th + 1) * 1024]) \
                            .then_inc([lda, ldb][ps_ % 2].h, 16)
                        e.dma_start(row_c[:, 0:1024],
                                    p_sb[16 + s:17 + s, th * 1024:(th + 1) * 1024]) \
                            .then_inc([lda, ldb][ps_ % 2].h, 16)

            @blk.vector
            def _(e):
                it = 0
                for th in range(2):
                    for s in range(16):
                        ps_ = th * 16 + s
                        e.wait_ge([lda, ldb][ps_ % 2].h, rmark[ps_])
                        if ps_ >= 1:
                            e.wait_ge(pc.h, PC + ps_)
                        e.tensor_tensor(row_a[0:1, 0:1024], row_b[0:1, 0:1024],
                                        dt_row[0:1, th * 1024:(th + 1) * 1024],
                                        ALU.mult).then_inc(pb.h, 1)
                    for dg in range(NDG):
                        g3 = th * 32 + dg
                        for s in range(16):
                            p = it % 2
                            if s == 0:
                                e.wait_ge([s2a, s2b][g3 % 2].h, lmark[g3])
                                e.wait_ge(pd.h, PD + th * 16 + 16)
                            e.wait_ge(ev.h, E0 + it + 1)
                            e.tensor_tensor(b_t[p][:], xc_l[dg % 2][:],
                                            wsb_all[:, s * 1024:(s + 1) * 1024],
                                            ALU.mult)
                            if th == 0:
                                e.tensor_tensor_scan(h_t[p][:], a_t[p][:], b_t[p][:],
                                                     0.0, ALU.mult, ALU.add)
                            else:
                                e.tensor_tensor_scan(
                                    h_t[p][:], a_t[p][:], b_t[p][:],
                                    carry[:, dg * 16 + s:dg * 16 + s + 1],
                                    ALU.mult, ALU.add)
                            if it >= 2:
                                e.wait_ge(mm.h, M0 + it - 1)
                            e.tensor_tensor(hc_t[p][:], h_t[p][:],
                                            cb_all[:, s * 1024:(s + 1) * 1024],
                                            ALU.mult).then_inc(dv.h, 1)
                            if th == 0:
                                e.tensor_copy(carry[:, dg * 16 + s:dg * 16 + s + 1],
                                              h_t[p][:, 1023:1024])
                            it += 1
                        e.wait_ge(mm.h, M0 + 16 * (g3 + 1))
                        e.wait_ge(ev2.h, E20 + g3 + 1)
                        if g3 >= 2:
                            e.wait_ge([spa, spb][g3 % 2].h, ymark[g3 - 2])
                        e.scalar_tensor_tensor(y1_t[:], xc_l[dg % 2][:],
                                               packT[:, 160 + dg:161 + dg],
                                               ps_y[dg % 2][:], ALU.mult, ALU.add)
                        e.tensor_tensor(yg_t[dg % 2][:], y1_t[:], zs_t[dg % 2][:],
                                        ALU.mult).then_inc(dv2.h, 1)

            dv.v = V0 + 1024
            dv2.v = V20 + 64
            pb.v = PB + 32

            @blk.tensor
            def _(e):
                e.wait_ge(ev.h, E0)
                e.wait_ge(dv.h, V0)
                it = 0
                for th in range(2):
                    for s in range(16):
                        ps_ = th * 16 + s
                        e.wait_ge(pb.h, PB + ps_ + 1)
                        e.wait_ge([lda, ldb][ps_ % 2].h, rmark[ps_])
                        if ps_ >= 1:
                            e.wait_ge(pd.h, PD + ps_)
                        for nn in range(2):
                            e.matmul(ps_w[:, nn * 512:(nn + 1) * 512],
                                     ones_r_f[0:1, :],
                                     row_a[0:1, nn * 512:(nn + 1) * 512],
                                     start=True, stop=True)
                        last = None
                        for nn in range(2):
                            last = e.matmul(ps_c[:, nn * 512:(nn + 1) * 512],
                                            ones_r_f[0:1, :],
                                            row_c[0:1, nn * 512:(nn + 1) * 512],
                                            start=True, stop=True)
                        last.then_inc(pc.h, 1)
                    for dg in range(NDG):
                        g3 = th * 32 + dg
                        for s in range(16):
                            p = it % 2
                            e.wait_ge(dv.h, V0 + it + 1)
                            if s == 0 and g3 >= 2:
                                e.wait_ge(dv2.h, V20 + g3 - 1)
                            last = None
                            for nn in range(2):
                                last = e.matmul(
                                    ps_y[dg % 2][:, nn * 512:(nn + 1) * 512],
                                    ident_b[:],
                                    hc_t[p][:, nn * 512:(nn + 1) * 512],
                                    start=(s == 0), stop=(s == 15))
                            last.then_inc(mm.h, 1)
                            it += 1

            mm.v = M0 + 1024
            pc.v = PC + 32

            @blk.scalar
            def _(e):
                it = 0
                for th in range(2):
                    for s in range(16):
                        ps_ = th * 16 + s
                        e.wait_ge(pc.h, PC + ps_ + 1)
                        if th == 1:
                            e.wait_ge(dv.h, V0 + 512)
                        e.copy(wsb_all[:, s * 1024:(s + 1) * 1024], ps_w[:])
                        e.copy(cb_all[:, s * 1024:(s + 1) * 1024], ps_c[:]) \
                            .then_inc(pd.h, 1)
                    for dg in range(NDG):
                        g3 = th * 32 + dg
                        for s in range(16):
                            p = it % 2
                            if it >= 2:
                                e.wait_ge(dv.h, V0 + it - 1)
                            e.activation(a_t[p][:],
                                         dtb_sb[:, th * 1024:(th + 1) * 1024],
                                         AF.Exp,
                                         scale=aneg[:, dg * 16 + s:dg * 16 + s + 1]) \
                                .then_inc(ev.h, 1)
                            it += 1
                        e.wait_ge([s2a, s2b][g3 % 2].h, lmark[g3])
                        if g3 >= 2:
                            e.wait_ge(dv2.h, V20 + g3 - 1)
                        e.activation(zs_t[dg % 2][:], z_l[dg % 2][:], AF.Silu) \
                            .then_inc(ev2.h, 1)

            ev.v = E0 + 1024
            ev2.v = E20 + 64
            pd.v = PD + 32

            @blk.gpsimd
            def _(e):
                for th in range(2):
                    for dg in range(NDG):
                        g3 = th * 32 + dg
                        if dg == 0:
                            if g3 >= 2:
                                e.wait_ge(dv2.h, V20 + g3 - 1)
                            e.dma_start(xc_l[0][:],
                                        sc_xc[dg * 128:(dg + 1) * 128,
                                              th * 1024:(th + 1) * 1024]) \
                                .then_inc([s2a, s2b][g3 % 2].h, 16)
                            e.dma_start(z_l[0][:],
                                        sc_z[dg * 128:(dg + 1) * 128,
                                             th * 1024:(th + 1) * 1024]) \
                                .then_inc([s2a, s2b][g3 % 2].h, 16)
                        if dg < NDG - 1:
                            ng = g3 + 1
                            if ng >= 2:
                                e.wait_ge(dv2.h, V20 + ng - 1)
                            e.dma_start(xc_l[(dg + 1) % 2][:],
                                        sc_xc[(dg + 1) * 128:(dg + 2) * 128,
                                              th * 1024:(th + 1) * 1024]) \
                                .then_inc([s2a, s2b][ng % 2].h, 16)
                            e.dma_start(z_l[(dg + 1) % 2][:],
                                        sc_z[(dg + 1) * 128:(dg + 2) * 128,
                                             th * 1024:(th + 1) * 1024]) \
                                .then_inc([s2a, s2b][ng % 2].h, 16)
                        e.wait_ge(dv2.h, V20 + g3 + 1)
                        e.dma_start(sc_yg[dg * 128:(dg + 1) * 128,
                                          th * 1024:(th + 1) * 1024],
                                    yg_t[dg % 2][:]) \
                            .then_inc([spa, spb][g3 % 2].h, 16)
                e.wait_ge(spa.h, YP[0])
                e.wait_ge(spb.h, YP[1])

            lda.v, ldb.v = RP[0], RP[1]
            spa.v, spb.v = YP[0], YP[1]
            s2a.v, s2b.v = LDP[0], LDP[1]

            if debug:
                @blk.sync
                def _(e):
                    e.wait_ge(spa.h, YP[0])
                    e.wait_ge(spb.h, YP[1])
                    e.dma_start(dbg["dbg_xc0"], sc_xc[0:128, :]).then_inc(ld.h, 16)
                    e.dma_start(dbg["dbg_z0"], sc_z[0:128, :]).then_inc(ld.h, 16)
                    e.dma_start(dbg["dbg_yg0"], sc_yg[0:128, :]).then_inc(ld.h, 16)
                    e.wait_ge(ld.h, ld.v + 48)
                ld.v += 48

        m2ctx.close()
        mctx.close()

        with contextlib.ExitStack() as sc:
            yg_h = sc.enter_context(nc.sbuf_tensor("m4yg", [128, 32 * 1024], BF16))
            wstg4 = [sc.enter_context(nc.sbuf_tensor(f"m4ws{i}", [128, 512], F32))
                     for i in range(2)]
            wb4 = [sc.enter_context(nc.sbuf_tensor(f"m4wb{i}", [128, 32 * 512], BF16))
                   for i in range(2)]
            s_t = [sc.enter_context(nc.sbuf_tensor(f"m4st{i}", [128, 512], F32))
                   for i in range(2)]
            ps4 = [sc.enter_context(nc.psum_tensor(f"m4p{i}", [128, 512], F32))
                   for i in range(4)]
            blk = sc.enter_context(nc.Block())
            rst()
            L0, L2, E0, M0, P0, S20 = ld.v, ld2.v, ev.v, mm.v, po.v, sp2.v
            WP = [l2a.v, l2b.v]
            wmark = {}
            for jk in range(256):
                WP[jk % 2] += 16
                wmark[jk] = WP[jk % 2]
            STP = [spa.v, spb.v]
            smark = {}
            for gg in range(64):
                STP[gg % 2] += 16
                smark[gg] = STP[gg % 2]
            S2A0 = s2a.v

            PB4 = pb.v
            pb.v = PB4 + 256

            @blk.sync
            def _(e):
                for nh in range(2):
                    for mg in range(4):
                        j = nh * 4 + mg
                        for k in range(32):
                            jk = j * 32 + k
                            if jk >= 2:
                                e.wait_ge(pb.h, PB4 + jk - 1)
                            e.dma_start(wstg4[k % 2][:],
                                        outWT[k * 128:(k + 1) * 128,
                                              mg * 512:(mg + 1) * 512]) \
                                .then_inc([l2a, l2b][jk % 2].h, 16)

            @blk.gpsimd
            def _(e):
                for k in range(32):
                    e.dma_start(yg_h[:, k * 1024:(k + 1) * 1024],
                                sc_yg[k * 128:(k + 1) * 128, 0:1024]) \
                        .then_inc(s2a.h, 16)
                for j in range(8):
                    for g in range(8):
                        ms, nsub = g // 2, g % 2
                        gg = j * 8 + g
                        nh, mg = j // 4, j % 4
                        m2 = mg * 4 + ms
                        e.wait_ge(ev.h, E0 + gg + 1)
                        e.dma_start(sc_s[m2 * 128:(m2 + 1) * 128,
                                         nh * 1024 + nsub * 512:
                                         nh * 1024 + nsub * 512 + 512],
                                    s_t[gg % 2][:]) \
                            .then_inc([spa, spb][gg % 2].h, 16)
                    if j == 3:
                        e.wait_ge(mm.h, M0 + 32)
                        for k in range(32):
                            e.dma_start(yg_h[:, k * 1024:(k + 1) * 1024],
                                        sc_yg[k * 128:(k + 1) * 128, 1024:2048]) \
                                .then_inc(s2a.h, 16)
                e.wait_ge(spa.h, STP[0])
                e.wait_ge(spb.h, STP[1])

            spa.v, spb.v = STP[0], STP[1]
            s2a.v = S2A0 + 16 * 64



            @blk.tensor
            def _(e):
                for nh in range(2):
                    for mg in range(4):
                        j = nh * 4 + mg
                        e.wait_ge(pb.h, PB4 + 32 * (j + 1))
                        e.wait_ge(s2a.h, S2A0 + 16 * 32 * (nh + 1))
                        for g in range(8):
                            ms, nsub = g // 2, g % 2
                            gg = j * 8 + g
                            if gg >= 4:
                                e.wait_ge(ev.h, E0 + gg - 3)
                            last = None
                            for k in range(32):
                                last = e.matmul(
                                    ps4[g % 4][:],
                                    wb4[j % 2][:, k * 512 + ms * 128:
                                               k * 512 + ms * 128 + 128],
                                    yg_h[:, k * 1024 + nsub * 512:
                                         k * 1024 + nsub * 512 + 512],
                                    start=(k == 0), stop=(k == 31))
                            last.then_inc(mm.h, 1)

            mm.v = M0 + 64

            @blk.scalar
            def _(e):
                def cast4(j, g):
                    for k in range(4 * g, 4 * g + 4):
                        jk = j * 32 + k
                        e.wait_ge([l2a, l2b][jk % 2].h, wmark[jk])
                        e.copy(wb4[j % 2][:, k * 512:(k + 1) * 512],
                               wstg4[k % 2][:]).then_inc(pb.h, 1)
                for j in range(2):
                    for g in range(8):
                        cast4(j, g)
                for j in range(8):
                    for g in range(8):
                        gg = j * 8 + g
                        e.wait_ge(mm.h, M0 + gg + 1)
                        if gg >= 2:
                            e.wait_ge([spa, spb][gg % 2].h, smark[gg - 2])
                        e.copy(s_t[gg % 2][:], ps4[gg % 4][:]).then_inc(ev.h, 1)
                        if 1 <= j < 7:
                            cast4(j + 1, g)

            ev.v = E0 + 64

            l2a.v, l2b.v = WP[0], WP[1]

            if debug:
                @blk.sync
                def _(e):
                    e.wait_ge(spa.h, STP[0])
                    e.wait_ge(spb.h, STP[1])
                    e.dma_start(dbg["dbg_s0"], sc_s[0:128, :]).then_inc(ld.h, 16)
                    e.wait_ge(ld.h, ld.v + 16)
                ld.v += 16


        h1ctx = contextlib.ExitStack()
        h1_sb = h1ctx.enter_context(nc.sbuf_tensor("h1_sb", [128, NKT * T], BF16))

        with contextlib.ExitStack() as sc:
            blk = sc.enter_context(nc.Block())
            rst()
            M0F0 = mm.v
            L2P = [l2a.v, l2b.v]
            for kt in range(NKT):
                L2P[kt % 2] += 16
            H1A, H1B = L2P[0], L2P[1]

            @blk.sync
            def _(e):
                e.wait_ge(mm.h, M0F0)
                e.wait_ge(ev.h, ev.v)
                e.wait_ge(spa.h, spa.v)
                e.wait_ge(spb.h, spb.v)
                e.wait_ge(s2a.h, s2a.v)
                e.wait_ge(s2b.h, s2b.v)
                for kt in range(NKT):
                    e.dma_start(h1_sb[:, kt * T:(kt + 1) * T],
                                sc_h1[kt * 128:(kt + 1) * 128, :]) \
                        .then_inc([l2a, l2b][kt % 2].h, 16)

            l2a.v, l2b.v = L2P[0], L2P[1]

        with contextlib.ExitStack() as sc:
            wsl = [sc.enter_context(nc.sbuf_tensor(f"f1sl{i}", [128, 8192], BF16))
                   for i in range(3)]
            wstg1 = [sc.enter_context(nc.sbuf_tensor(f"f1stg{i}", [128, 1024], F32))
                     for i in range(2)]
            q_sb = [sc.enter_context(nc.sbuf_tensor(f"f1q{i}", [128, T], BF16))
                    for i in range(2)]
            k_sb = [sc.enter_context(nc.sbuf_tensor(f"f1k{i}", [128, T], BF16))
                    for i in range(2)]
            v_sb = sc.enter_context(nc.sbuf_tensor("f1v", [128, T], BF16))
            vT_sb = [sc.enter_context(nc.sbuf_tensor(f"f1vt{i}", [128, T], BF16))
                     for i in range(2)]
            sqn = sc.enter_context(nc.sbuf_tensor("f1sq", [128, T], BF16))
            sqnk = sc.enter_context(nc.sbuf_tensor("f1sqk", [128, T], BF16))
            nb_col = sc.enter_context(nc.sbuf_tensor("f1nb", [128, 8], F32))
            nrx = sc.enter_context(nc.sbuf_tensor("f1nx", [128, 64], F32))
            nrxk = sc.enter_context(nc.sbuf_tensor("f1nxk", [128, 64], F32))
            nrow2 = sc.enter_context(nc.sbuf_tensor("f1nr2", [33, T], F32))
            nrow1 = sc.enter_context(nc.sbuf_tensor("f1nr1", [33, 512], F32))
            pt_t = [sc.enter_context(nc.sbuf_tensor(f"f1pt{i}", [128, 512], BF16))
                    for i in range(4)]
            ptm_t = [sc.enter_context(nc.sbuf_tensor(f"f1pm{i}", [128, 512], BF16))
                     for i in range(2)]
            rb_sb = sc.enter_context(nc.sbuf_tensor("f1rb", [128, 512], F32))
            at_t = sc.enter_context(nc.sbuf_tensor("f1at", [128, 512], BF16))
            psA = sc.enter_context(nc.psum_tensor("f1pA", [128, 512], F32))
            psB = sc.enter_context(nc.psum_tensor("f1pB", [128, 512], F32))
            psC = sc.enter_context(nc.psum_tensor("f1pC", [128, 128], BF16))
            psD = sc.enter_context(nc.psum_tensor("f1pD", [128, 128], BF16))
            psE = sc.enter_context(nc.psum_tensor("f1pE", [128, 512], F32))
            psF = sc.enter_context(nc.psum_tensor("f1pF", [128, 512], F32))
            psG = sc.enter_context(nc.psum_tensor("f1pG", [128, 512], F32))
            psH = sc.enter_context(nc.psum_tensor("f1pH", [128, 512], F32))
            PS4 = [psE, psF, psG, psH]
            blk = sc.enter_context(nc.Block())
            rst()

            mk = {}
            c = dict(mm=mm.v, mm2=mm2.v, ev=ev.v, ev2=ev2.v, dv=dv.v,
                     dv2=dv2.v, po=po.v, sp=sp.v, pb=pb.v, ld=ld.v,
                     lda=lda.v, ldb=ldb.v)

            def bump(sem, key, amt=1):
                c[sem] += amt
                mk[key] = c[sem]

            for s in range(24):
                bump('ld', ('ldW', 0, s), 32)
                bump('pb', ('cast', 0, s))
            for h in range(17):
                g = h - 1
                par = ['lda', 'ldb'][h % 2]
                if h < 16:
                    for j in range(12):
                        bump('mm', ('qkv', h, j))
                    for kt in range(16):
                        bump('mm2', ('tr', h, kt))
                    bump('mm', ('nq', h))
                    bump('mm', ('nk', h))
                    for j in range(4):
                        bump('ev', ('evac', h, j))
                    bump('ev', ('sqq', h))
                    for j in range(4, 8):
                        bump('ev', ('evac', h, j))
                    bump('ev', ('sqk', h))
                    for j in range(8, 12):
                        bump('ev', ('evac', h, j))
                    for kt in range(16):
                        bump('dv', ('vt', h, kt))
                    bump('dv', ('nqc', h))
                    bump('dv', ('nkc', h))
                    for r in range(1, 6):
                        bump(par, ('pyqd', h, r), 16)
                        bump(par, ('pykd', h, r), 16)
                    if h % 4 == 3 and h // 4 + 1 < 4:
                        gr = h // 4 + 1
                        for s in range(24):
                            bump('ld', ('ldW', gr, s), 32)
                            bump('pb', ('cast', gr, s))
                for r in range(1, 6):
                    if h < 16:
                        bump('dv', ('pyq', h, r))
                        bump('dv', ('pyk', h, r))
                    if h >= 1 and r >= 2:
                        qc = r - 2
                        bump('dv', ('rc', g, qc))
                        bump('dv2', ('at', g, qc))
                if h >= 1:
                    jj = 0
                    for qc in range(4):
                        nkt_q = 4 * qc + 4
                        for kt in range(nkt_q):
                            bump('mm', ('sc', g, jj))
                            if kt == 2 and qc >= 1:
                                bump('mm', ('rb', g, qc - 1))
                            bump('mm2', ('pv', g, jj))
                            bump('ev2', ('exp', g, jj))
                            if kt >= 4 * qc:
                                bump('po', ('aff', g, jj))
                            jj += 1
                        bump('ev', ('rbe', g, qc))
                        bump('sp', ('sp', g, qc), 16)
                    bump('mm', ('rb', g, 3))
                if h < 16:
                    bump('mm', ('negB', h))
                    bump('ev', ('sn', h))
                    bump('ev', ('nbe', h))

            WMATS = [wqT, wkT, wvT]

            def slab_stage(e, gr, s):
                mat, c8 = s // 8, s % 8
                if s >= 2:
                    e.wait_ge(pb.h, mk[('cast', gr, s - 2)])
                elif gr >= 1:
                    e.wait_ge(pb.h, mk[('cast', gr - 1, 22 + s)])
                for kq in range(2):
                    ktg = c8 * 2 + kq
                    e.dma_start(wstg1[s % 2][:, kq * 512:(kq + 1) * 512],
                                WMATS[mat][ktg * 128:(ktg + 1) * 128,
                                           gr * 512:(gr + 1) * 512]) \
                        .then_inc(ld.h, 16)

            @blk.sync
            def _(e):
                for s in range(24):
                    slab_stage(e, 0, s)
                for h in range(16):
                    par = [lda, ldb][h % 2]
                    bnd = (h % 4 == 3 and h // 4 + 1 < 4)
                    gr = h // 4 + 1
                    e.wait_ge(dv.h, mk[('nqc', h)])
                    e.dma_start(nrx[:, 0:16], nrow2[0:1, 0:2048]).then_inc(par.h, 16)
                    e.wait_ge(dv.h, mk[('nkc', h)])
                    e.dma_start(nrxk[:, 0:16], nrow2[32:33, 0:2048]) \
                        .then_inc(par.h, 16)
                    if bnd:
                        for s_ in range(8):
                            slab_stage(e, gr, s_)
                    e.wait_ge(dv.h, mk[('pyq', h, 1)])
                    e.dma_start(nrx[0:16, 24:32], nrx[:, 16:17]).then_inc(par.h, 16)
                    e.wait_ge(dv.h, mk[('pyk', h, 1)])
                    e.dma_start(nrxk[0:16, 24:32], nrxk[:, 16:17]).then_inc(par.h, 16)
                    e.wait_ge(dv.h, mk[('pyq', h, 2)])
                    e.dma_start(nrx[0:4, 40:44], nrx[0:16, 32:33]).then_inc(par.h, 16)
                    e.wait_ge(dv.h, mk[('pyk', h, 2)])
                    e.dma_start(nrxk[0:4, 40:44], nrxk[0:16, 32:33]).then_inc(par.h, 16)
                    e.wait_ge(dv.h, mk[('pyq', h, 3)])
                    e.dma_start(nrx[0:2, 48:50], nrx[0:4, 44:45]).then_inc(par.h, 16)
                    e.wait_ge(dv.h, mk[('pyk', h, 3)])
                    e.dma_start(nrxk[0:2, 48:50], nrxk[0:4, 44:45]).then_inc(par.h, 16)
                    e.wait_ge(dv.h, mk[('pyq', h, 4)])
                    e.dma_start(nrx[0:1, 56:58], nrx[0:2, 50:51]).then_inc(par.h, 16)
                    e.wait_ge(dv.h, mk[('pyk', h, 4)])
                    e.dma_start(nrxk[0:1, 56:58], nrxk[0:2, 50:51]).then_inc(par.h, 16)
                    if bnd:
                        for s_ in range(8, 24):
                            slab_stage(e, gr, s_)

            @blk.tensor
            def _(e):
                for h in range(17):
                    g = h - 1
                    hh = h % 4
                    if h < 16:
                        for j in range(12):
                            mat, n = j // 4, j % 4
                            if h == 0 and j == 0:
                                e.wait_ge(l2a.h, H1A)
                                e.wait_ge(l2b.h, H1B)
                            if j < 2 and h >= 1:
                                e.wait_ge(ev.h, mk[('nbe', h - 1)])
                            if j >= 2:
                                e.wait_ge(ev.h, mk[('evac', h, j - 2)])
                            if n == 0:
                                e.wait_ge(pb.h, mk[('cast', h // 4, 8 * mat + 7)])
                            last = None
                            for kt in range(NKT):
                                last = e.matmul(
                                    [psA, psB][j % 2][:],
                                    wsl[mat][:, kt * 512 + hh * 128:
                                             kt * 512 + hh * 128 + 128],
                                    h1_sb[:, kt * T + n * 512:kt * T + n * 512 + 512],
                                    start=(kt == 0), stop=(kt == NKT - 1))
                            last.then_inc(mm.h, 1)
                        for kt in range(16):
                            if kt == 0:
                                e.wait_ge(ev.h, mk[('evac', h, 11)])
                            if kt >= 2:
                                e.wait_ge(dv.h, mk[('vt', h, kt - 2)])
                            elif h >= 1:
                                e.wait_ge(dv.h, mk[('vt', h - 1, 15)])
                            e.transpose([psC, psD][kt % 2][:],
                                        v_sb[:, kt * 128:(kt + 1) * 128],
                                        ident_b[:]).then_inc(mm2.h, 1)
                        for half in range(2):
                            if half == 0:
                                e.wait_ge(ev.h, mk[('sqq', h)])
                                if h >= 2:
                                    e.wait_ge(ev2.h, mk[('exp', h - 2, 39)])
                                    e.wait_ge(dv.h, mk[('rc', h - 2, 3)])
                                    e.wait_ge(dv2.h, mk[('at', h - 2, 3)])
                                src2 = sqn
                            else:
                                e.wait_ge(ev.h, mk[('sqk', h)])
                                e.wait_ge(dv.h, mk[('nqc', h)])
                                src2 = sqnk
                            last = None
                            for n in range(4):
                                last = e.matmul(PS4[n][0:1, :], ones_c_b[:],
                                                src2[:, n * 512:(n + 1) * 512],
                                                start=True, stop=True)
                            last.then_inc(mm.h, 1)
                    if h >= 1:
                        SCB = [psE, psF, psA, psB]
                        jjb = 0
                        for qc in range(4):
                            nkt_q = 4 * qc + 4
                            for step in range(nkt_q + 4):
                                if step < nkt_q:
                                    jj = jjb + step
                                    kt = step
                                    if jj >= 4:
                                        e.wait_ge(ev2.h, mk[('exp', g, jj - 4)])
                                    elif jj >= 2:
                                        if h < 16:
                                            e.wait_ge(ev.h, mk[('evac', h, 11)])
                                        else:
                                            e.wait_ge(ev.h, mk[('nbe', 15)])
                                    else:
                                        if h < 16:
                                            e.wait_ge(dv.h, mk[('nkc', h)])
                                        else:
                                            e.wait_ge(ev2.h, mk[('exp', g - 1, 39)])
                                            e.wait_ge(ev.h, mk[('rbe', g - 1, 3)])
                                            e.wait_ge(dv.h, mk[('rc', g - 1, 3)])
                                            e.wait_ge(dv2.h, mk[('at', g - 1, 3)])
                                    e.matmul(SCB[jj % 4][:],
                                             k_sb[g % 2][:, kt * 128:(kt + 1) * 128],
                                             q_sb[g % 2][:, qc * 512:(qc + 1) * 512],
                                             start=True, stop=True).then_inc(mm.h, 1)
                                    if step == 2 and qc >= 1:
                                        e.wait_ge(dv.h, mk[('rc', g, qc - 1)])
                                        e.matmul(psG[:], ones_r_f[0:1, :],
                                                 nrow1[0:1, 0:512],
                                                 start=True, stop=True) \
                                            .then_inc(mm.h, 1)
                                if step >= 4:
                                    jj = jjb + step - 4
                                    kt = step - 4
                                    e.wait_ge(ev2.h, mk[('exp', g, jj)])
                                    if kt >= 4 * qc:
                                        e.wait_ge(po.h, mk[('aff', g, jj)])
                                        src = ptm_t[jj % 2]
                                    else:
                                        src = pt_t[jj % 4]
                                    if kt == 0 and qc >= 1:
                                        e.wait_ge(ev.h, mk[('rbe', g, qc - 1)])
                                        e.wait_ge(dv2.h, mk[('at', g, qc - 1)])
                                    e.matmul(psG[0:1, :], ones_c_b[:], src[:],
                                             start=(kt == 0), stop=(kt == nkt_q - 1))
                                    e.matmul(psH[:],
                                             vT_sb[g % 2][:, kt * 128:(kt + 1) * 128],
                                             src[:],
                                             start=(kt == 0), stop=(kt == nkt_q - 1)) \
                                        .then_inc(mm2.h, 1)
                            jjb += nkt_q
                        e.wait_ge(dv.h, mk[('rc', g, 3)])
                        e.matmul(psG[:], ones_r_f[0:1, :], nrow1[0:1, 0:512],
                                 start=True, stop=True).then_inc(mm.h, 1)
                    if h < 16:
                        if h >= 1:
                            e.wait_ge(ev2.h, mk[('exp', h - 1, 38)])
                        e.wait_ge(ev.h, mk[('sn', h)])
                        e.matmul(psA[:, 0:1], ones_r_f[0:1, :], nrx[0:1, 63:64],
                                 start=True, stop=True).then_inc(mm.h, 1)

            @blk.scalar
            def _(e):
                for s in range(24):
                    e.wait_ge(ld.h, mk[('ldW', 0, s)])
                    mat, c8 = s // 8, s % 8
                    e.copy(wsl[mat][:, c8 * 1024:(c8 + 1) * 1024],
                           wstg1[s % 2][:]).then_inc(pb.h, 1)
                for h in range(17):
                    g = h - 1
                    bnd = (h < 16 and h % 4 == 3 and h // 4 + 1 < 4)
                    gr = h // 4 + 1

                    def cast4(s0):
                        for s in range(s0, s0 + 4):
                            mat, c8 = s // 8, s % 8
                            e.wait_ge(ld.h, mk[('ldW', gr, s)])
                            e.wait_ge(mm.h, mk[('qkv', h, 4 * mat + 3)])
                            e.copy(wsl[mat][:, c8 * 1024:(c8 + 1) * 1024],
                                   wstg1[s % 2][:]).then_inc(pb.h, 1)
                    if h < 16:
                        for j in range(12):
                            mat, n = j // 4, j % 4
                            e.wait_ge(mm.h, mk[('qkv', h, j)])
                            dst = [q_sb[h % 2], k_sb[h % 2], v_sb][mat]
                            if mat == 0:
                                e.activation(dst[:, n * 512:(n + 1) * 512],
                                             [psA, psB][j % 2][:],
                                             AF.Identity, scale=RSQ_HD) \
                                    .then_inc(ev.h, 1)
                            else:
                                e.copy(dst[:, n * 512:(n + 1) * 512],
                                       [psA, psB][j % 2][:]).then_inc(ev.h, 1)
                            if j == 3:
                                e.activation(sqn[:], q_sb[h % 2][:], AF.Square) \
                                    .then_inc(ev.h, 1)
                            if j == 7:
                                e.activation(sqnk[:], k_sb[h % 2][:], AF.Square) \
                                    .then_inc(ev.h, 1)
                        if bnd:
                            cast4(0)
                            cast4(4)
                    if h >= 1:
                        jj = 0
                        for qc in range(4):
                            nkt_q = 4 * qc + 4
                            for kt in range(nkt_q):
                                e.wait_ge(mm.h, mk[('sc', g, jj)])
                                if jj >= 4:
                                    e.wait_ge(mm2.h, mk[('pv', g, jj - 4)])
                                e.activation(pt_t[jj % 4][:],
                                             [psE, psF, psA, psB][jj % 4][:],
                                             AF.Exp, bias=nb_col[:, 0:1]) \
                                    .then_inc(ev2.h, 1)
                                jj += 1
                            e.wait_ge(mm.h, mk[('rb', g, qc)])
                            e.copy(rb_sb[:], psG[:]).then_inc(ev.h, 1)
                            if bnd and qc >= 2:
                                cast4(8 * qc - 8)
                                cast4(8 * qc - 4)
                    if h < 16:
                        e.wait_ge(dv.h, mk[('pyk', h, 5)])
                        e.activation(nrx[0:1, 62:63], nrx[0:1, 60:61],
                                     AF.Sqrt, scale=nrx[0:1, 61:62])
                        e.activation(nrx[0:1, 63:64], nrx[0:1, 62:63],
                                     AF.Identity, scale=-1.0).then_inc(ev.h, 1)
                        e.wait_ge(mm.h, mk[('negB', h)])
                        e.copy(nb_col[:, 0:1], psA[:, 0:1]).then_inc(ev.h, 1)

            @blk.vector
            def _(e):
                for h in range(17):
                    g = h - 1
                    par = [lda, ldb][h % 2]
                    if h < 16:
                        for kt in range(16):
                            e.wait_ge(mm2.h, mk[('tr', h, kt)])
                            e.tensor_copy(vT_sb[h % 2][:, kt * 128:(kt + 1) * 128],
                                          [psC, psD][kt % 2][:]).then_inc(dv.h, 1)
                        e.wait_ge(mm.h, mk[('nq', h)])
                        last = None
                        for n in range(4):
                            last = e.tensor_copy(nrow2[0:1, n * 512:(n + 1) * 512],
                                                 PS4[n][0:1, :])
                        last.then_inc(dv.h, 1)
                        e.wait_ge(mm.h, mk[('nk', h)])
                        last = None
                        for n in range(4):
                            last = e.tensor_copy(nrow2[32:33, n * 512:(n + 1) * 512],
                                                 PS4[n][0:1, :])
                        last.then_inc(dv.h, 1)
                    for r in range(1, 6):
                        if h < 16:
                            e.wait_ge(par.h, mk[('pyqd', h, r)])
                            if r == 1:
                                e.reduce_max(nrx[:, 16:17], nrx[:, 0:16], AX.X) \
                                    .then_inc(dv.h, 1)
                            elif r == 2:
                                e.reduce_max(nrx[0:16, 32:33], nrx[0:16, 24:32],
                                             AX.X).then_inc(dv.h, 1)
                            elif r == 3:
                                e.reduce_max(nrx[0:4, 44:45], nrx[0:4, 40:44],
                                             AX.X).then_inc(dv.h, 1)
                            elif r == 4:
                                e.reduce_max(nrx[0:2, 50:51], nrx[0:2, 48:50],
                                             AX.X).then_inc(dv.h, 1)
                            else:
                                e.tensor_tensor(nrx[0:1, 60:61], nrx[0:1, 56:57],
                                                nrx[0:1, 57:58], ALU.max) \
                                    .then_inc(dv.h, 1)
                            e.wait_ge(par.h, mk[('pykd', h, r)])
                            if r == 1:
                                e.reduce_max(nrxk[:, 16:17], nrxk[:, 0:16], AX.X) \
                                    .then_inc(dv.h, 1)
                            elif r == 2:
                                e.reduce_max(nrxk[0:16, 32:33], nrxk[0:16, 24:32],
                                             AX.X).then_inc(dv.h, 1)
                            elif r == 3:
                                e.reduce_max(nrxk[0:4, 44:45], nrxk[0:4, 40:44],
                                             AX.X).then_inc(dv.h, 1)
                            elif r == 4:
                                e.reduce_max(nrxk[0:2, 50:51], nrxk[0:2, 48:50],
                                             AX.X).then_inc(dv.h, 1)
                            else:
                                e.tensor_tensor(nrx[0:1, 61:62], nrxk[0:1, 56:57],
                                                nrxk[0:1, 57:58], ALU.max) \
                                    .then_inc(dv.h, 1)
                        if h >= 1 and r >= 2:
                            qc = r - 2
                            e.wait_ge(mm2.h, mk[('pv', g, sum(4 * q_ + 4 for q_ in range(qc + 1)) - 1)])
                            e.tensor_copy(nrow1[32:33, 0:512], psG[0:1, :])
                            e.reciprocal(nrow1[0:1, 0:512], nrow1[32:33, 0:512]) \
                                .then_inc(dv.h, 1)
                            e.wait_ge(ev.h, mk[('rbe', g, qc)])
                            if (g, qc) != (0, 0):
                                pk = ('sp', g, qc - 1) if qc >= 1 else ('sp', g - 1, 3)
                                e.wait_ge(sp.h, mk[pk])
                            e.tensor_tensor(at_t[:], psH[:], rb_sb[:], ALU.mult) \
                                .then_inc(dv2.h, 1)

            @blk.gpsimd
            def _(e):
                for h in range(1, 17):
                    g = h - 1
                    jj = 0
                    for qc in range(4):
                        nkt_q = 4 * qc + 4
                        for kt in range(nkt_q):
                            if kt >= 4 * qc:
                                e.wait_ge(ev2.h, mk[('exp', g, jj)])
                                if jj >= 2:
                                    e.wait_ge(mm2.h, mk[('pv', g, jj - 2)])
                                e.affine_select(ptm_t[jj % 2][:],
                                                pt_t[jj % 4][:],
                                                [[1, 512]], ALU.is_ge, 0.0,
                                                base=qc * 512 - kt * 128,
                                                channel_multiplier=-1) \
                                    .then_inc(po.h, 1)
                            jj += 1
                        e.wait_ge(dv2.h, mk[('at', g, qc)])
                        e.dma_start(sc_attn[g * 128:(g + 1) * 128,
                                            qc * 512:(qc + 1) * 512],
                                    at_t[:]).then_inc(sp.h, 16)
                e.wait_ge(sp.h, mk[('sp', 15, 3)])

            mm.v, mm2.v, ev.v, ev2.v = c['mm'], c['mm2'], c['ev'], c['ev2']
            dv.v, dv2.v, po.v, sp.v = c['dv'], c['dv2'], c['po'], c['sp']
            pb.v, ld.v = c['pb'], c['ld']
            lda.v, ldb.v = c['lda'], c['ldb']

        h1ctx.close()
        M_F1END = mm.v

        fctx = contextlib.ExitStack()
        af_b = fctx.enter_context(nc.sbuf_tensor("af_b", [128, 16 * NKT], BF16))
        mod_row = fctx.enter_context(nc.sbuf_tensor("mod_row", [16, T], F32))
        modhA_b = fctx.enter_context(nc.sbuf_tensor("modhA_b", [16, T], BF16))

        with contextlib.ExitStack() as sc:
            attn_all = sc.enter_context(nc.sbuf_tensor("f2at", [128, NKT * T], BF16))
            wstg2 = sc.enter_context(nc.sbuf_tensor("f2stg", [128, 4096], F32))
            wo_sl = sc.enter_context(nc.sbuf_tensor("f2wo", [128, 8192], BF16))
            afstg = sc.enter_context(nc.sbuf_tensor("f2af", [128, 256], F32))
            ao_sb = [sc.enter_context(nc.sbuf_tensor(f"f2ao{i}", [128, 512], BF16))
                     for i in range(2)]
            ps_ao = [sc.enter_context(nc.psum_tensor(f"f2pa{i}", [128, 512], F32))
                     for i in range(2)]
            ps_mod = [sc.enter_context(nc.psum_tensor(f"f2pm{i}", [128, 512], F32))
                      for i in range(4)]
            blk = sc.enter_context(nc.Block())
            rst()
            L0, L2, E0, E20, M0, M20, V0, P0 = \
                ld.v, ld2.v, ev.v, ev2.v, mm.v, mm2.v, dv.v, po.v
            PBF2 = pb.v
            pb.v = PBF2 + 9

            @blk.sync
            def _(e):
                for kt in range(NKT):
                    e.dma_start(afstg[:, kt * 16:(kt + 1) * 16],
                                a_ffn[kt * 128:(kt + 1) * 128, :]).then_inc(ld.h, 16)
                for kt in range(NKT):
                    e.dma_start(attn_all[:, kt * T:(kt + 1) * T],
                                sc_attn[kt * 128:(kt + 1) * 128, :]).then_inc(lda.h, 16)
                for mg in range(4):
                    for c in range(2):
                        mc = mg * 2 + c
                        if mc >= 1:
                            e.wait_ge(pb.h, PBF2 + 1 + mc)
                        for kt in range(8):
                            ktg = c * 8 + kt
                            e.dma_start(wstg2[:, kt * 512:(kt + 1) * 512],
                                        woT[ktg * 128:(ktg + 1) * 128,
                                            mg * 512:(mg + 1) * 512]) \
                                .then_inc(ld2.h, 16)

            ld.v = L0 + 16 * 16
            ld2.v = L2 + 16 * 64
            lda.v = lda.v + 16 * 16
            SPF2 = lda.v

            @blk.tensor
            def _(e):
                e.wait_ge(lda.h, SPF2)
                e.wait_ge(pb.h, PBF2 + 1)
                for j in range(64):
                    m, n = j // 4, j % 4
                    mg = m // 4
                    if j >= 1:
                        pj = j - 1
                        pm_, pn_ = pj // 4, pj % 4
                        e.wait_ge(ev.h, E0 + pj + 1)
                        e.matmul(ps_mod[pn_][0:16, :],
                                 af_b[:, pm_ * 16:(pm_ + 1) * 16],
                                 ao_sb[pj % 2][:],
                                 start=(pm_ == 0), stop=(pm_ == 15)) \
                            .then_inc(mm2.h, 1)
                    if n == 0 and m % 4 == 0:
                        e.wait_ge(pb.h, PBF2 + 1 + 2 * (mg + 1))
                    if j >= 2:
                        e.wait_ge(ev.h, E0 + j - 1)
                    last = None
                    for kt in range(NKT):
                        last = e.matmul(ps_ao[j % 2][:],
                                        wo_sl[:, kt * 512 + (m % 4) * 128:
                                              kt * 512 + (m % 4) * 128 + 128],
                                        attn_all[:, kt * T + n * 512:kt * T + n * 512 + 512],
                                        start=(kt == 0), stop=(kt == NKT - 1))
                    last.then_inc(mm.h, 1)
                e.wait_ge(ev.h, E0 + 64)
                e.matmul(ps_mod[3][0:16, :], af_b[:, 15 * 16:16 * 16],
                         ao_sb[63 % 2][:], start=False, stop=True).then_inc(mm2.h, 1)

            mm.v = M0 + 64
            mm2.v = M20 + 64

            @blk.scalar
            def _(e):
                e.wait_ge(ld.h, L0 + 16 * 16)
                e.copy(af_b[:], afstg[:]).then_inc(pb.h, 1)
                for c in range(2):
                    e.wait_ge(ld2.h, L2 + 16 * 8 * (c + 1))
                    e.copy(wo_sl[:, c * 4096:(c + 1) * 4096],
                           wstg2[:]).then_inc(pb.h, 1)
                for j in range(64):
                    if j % 16 == 0 and j > 0:
                        mg = j // 16
                        for c in range(2):
                            mc = mg * 2 + c
                            e.wait_ge(ld2.h, L2 + 16 * 8 * (mc + 1))
                            e.wait_ge(mm.h, M0 + 16 * mg)
                            e.copy(wo_sl[:, c * 4096:(c + 1) * 4096],
                                   wstg2[:]).then_inc(pb.h, 1)
                    e.wait_ge(mm.h, M0 + j + 1)
                    if j >= 2:
                        e.wait_ge(mm2.h, M20 + j - 1)
                    e.copy(ao_sb[j % 2][:], ps_ao[j % 2][:]).then_inc(ev.h, 1)
                e.wait_ge(mm2.h, M20 + 64)
                last = None
                for n in range(4):
                    last = e.activation(mod_row[0:16, n * 512:(n + 1) * 512],
                                        ps_mod[n][0:16, :], AF.Tanh)
                last.then_inc(ev.h, 1)

            ev.v = E0 + 65

            @blk.vector
            def _(e):
                e.wait_ge(ev.h, E0 + 65)
                e.tensor_scalar(mod_row[0:16, :], mod_row[0:16, :], 0.1, None,
                                ALU.mult).then_inc(dv.h, 1)

            dv.v = V0 + 1

            if debug:
                @blk.sync
                def _(e):
                    e.wait_ge(dv.h, V0 + 1)
                    e.dma_start(dbg["dbg_attn0"], sc_attn[0:128, :]).then_inc(ld.h, 16)
                    e.dma_start(dbg["dbg_mod"], mod_row[0:16, :]).then_inc(ld.h, 16)
                    e.wait_ge(ld.h, ld.v + 32)
                ld.v += 32

        h2ctx = contextlib.ExitStack()
        h2_sb = h2ctx.enter_context(nc.sbuf_tensor("h2_sb", [128, NKT * T], BF16))
        with contextlib.ExitStack() as sc:
            ps_h = [sc.enter_context(nc.psum_tensor(f"f3p{i}", [128, 512], F32))
                    for i in range(4)]
            blk = sc.enter_context(nc.Block())
            rst()
            E0, M0, V0 = ev.v, mm.v, dv.v
            LP = [lda.v, ldb.v]
            xmark = {}
            for kt in range(NKT):
                LP[kt % 2] += 16
                xmark[kt] = LP[kt % 2]
            H2A, H2B = LP[0], LP[1]

            @blk.sync
            def _(e):
                e.wait_ge(mm.h, M_F1END)
                e.wait_ge(ev.h, E0)
                for kt in range(NKT):
                    e.dma_start(h2_sb[:, kt * T:(kt + 1) * T],
                                sc_h2[kt * 128:(kt + 1) * 128, :]) \
                        .then_inc([lda, ldb][kt % 2].h, 16)

            lda.v, ldb.v = LP[0], LP[1]

            @blk.tensor
            def _(e):
                for kt in range(NKT):
                    e.wait_ge([lda, ldb][kt % 2].h, xmark[kt])
                    last = None
                    for n in range(4):
                        last = e.matmul(ps_h[n][0:16, :],
                                        af_b[:, kt * 16:(kt + 1) * 16],
                                        h2_sb[:, kt * T + n * 512:kt * T + n * 512 + 512],
                                        start=(kt == 0), stop=(kt == NKT - 1))
                    last.then_inc(mm.h, 1)

            mm.v = M0 + NKT

            @blk.scalar
            def _(e):
                e.wait_ge(mm.h, M0 + NKT)
                last = None
                for n in range(4):
                    last = e.copy(hA_g[0:16, n * 512:(n + 1) * 512],
                                  ps_h[n][0:16, :])
                last.then_inc(ev.h, 1)

            ev.v = E0 + 1

            @blk.vector
            def _(e):
                e.wait_ge(ev.h, E0 + 1)
                e.tensor_tensor(modhA_b[0:16, :], mod_row[0:16, :], hA_g[0:16, :],
                                ALU.mult).then_inc(dv.h, 1)

            dv.v = V0 + 1

        with contextlib.ExitStack() as sc:
            wstg4f = sc.enter_context(nc.sbuf_tensor("f4stg", [128, 2048], F32))
            wib = [sc.enter_context(nc.sbuf_tensor(f"f4wi{i}", [128, 8192], BF16))
                   for i in range(2)]
            wub = [sc.enter_context(nc.sbuf_tensor(f"f4wu{i}", [128, 8192], BF16))
                   for i in range(2)]
            bstg = sc.enter_context(nc.sbuf_tensor("f4bs", [16, 1024], F32))
            bf_b = sc.enter_context(nc.sbuf_tensor("bf_b", [16, FF], BF16))
            gate_t = [sc.enter_context(nc.sbuf_tensor(f"f4g{i}", [128, 512], BF16))
                      for i in range(2)]
            gu_t = [sc.enter_context(nc.sbuf_tensor(f"f4u{i}", [128, 512], BF16))
                    for i in range(2)]
            ps_g = [sc.enter_context(nc.psum_tensor(f"f4pg{i}", [128, 512], F32))
                    for i in range(2)]
            ps_u = [sc.enter_context(nc.psum_tensor(f"f4pu{i}", [128, 512], F32))
                    for i in range(2)]
            blk = sc.enter_context(nc.Block())
            rst()
            L0, L2, E0, M0, M20, V0, P0, S0 = \
                ld.v, ld2.v, ev.v, mm.v, mm2.v, dv.v, po.v, sp.v
            GSP = [spa.v, spb.v]
            gmark = {}
            for g in range(256):
                GSP[g % 2] += 16
                gmark[g] = GSP[g % 2]

            PBF4 = pb.v
            pb.v = PBF4 + 136

            @blk.sync
            def _(e):
                for bc in range(8):
                    if bc >= 1:
                        e.wait_ge(pb.h, PBF4 + bc)
                    e.dma_start(bstg[:], b_ffn[:, bc * 1024:(bc + 1) * 1024]) \
                        .then_inc(ld.h, 16)
                for fq in range(16):
                    for mat, wsrc in enumerate([winbT, wupbT]):
                        for q4 in range(4):
                            mc = fq * 8 + mat * 4 + q4
                            if mc >= 1:
                                e.wait_ge(pb.h, PBF4 + 8 + mc)
                            for kq in range(4):
                                ktg = q4 * 4 + kq
                                e.dma_start(wstg4f[:, kq * 512:(kq + 1) * 512],
                                            wsrc[ktg * 128:(ktg + 1) * 128,
                                                 fq * 512:(fq + 1) * 512]) \
                                    .then_inc(ld2.h, 16)

            ld.v = L0 + 128
            ld2.v = L2 + 16 * 512

            @blk.gpsimd
            def _(e):
                for g in range(256):
                    f, n = g // 4, g % 4
                    e.wait_ge(dv.h, V0 + g + 1)
                    e.dma_start(sc_gu[f * 128:(f + 1) * 128,
                                      n * 512:(n + 1) * 512],
                                gu_t[g % 2][:]) \
                        .then_inc([spa, spb][g % 2].h, 16)
                e.wait_ge(spa.h, GSP[0])
                e.wait_ge(spb.h, GSP[1])

            spa.v, spb.v = GSP[0], GSP[1]

            @blk.tensor
            def _(e):
                e.wait_ge(lda.h, H2A)
                e.wait_ge(ldb.h, H2B)
                for fq in range(16):
                    e.wait_ge(pb.h, PBF4 + 8 + 8 * (fq + 1))
                    for ms in range(4):
                        f = fq * 4 + ms
                        for n in range(4):
                            g = f * 4 + n
                            if g >= 2:
                                e.wait_ge(ev.h, E0 + g - 1)
                            last = None
                            for kt in range(NKT):
                                last = e.matmul(
                                    ps_g[g % 2][:],
                                    wib[fq % 2][:, kt * 512 + ms * 128:
                                               kt * 512 + ms * 128 + 128],
                                    h2_sb[:, kt * T + n * 512:kt * T + n * 512 + 512],
                                    start=(kt == 0), stop=False)
                            e.matmul(ps_g[g % 2][:], bf_b[:, f * 128:(f + 1) * 128],
                                     modhA_b[0:16, n * 512:(n + 1) * 512],
                                     start=False, stop=True).then_inc(mm.h, 1)
                            if g >= 2:
                                e.wait_ge(dv.h, V0 + g - 1)
                            last = None
                            for kt in range(NKT):
                                last = e.matmul(
                                    ps_u[g % 2][:],
                                    wub[fq % 2][:, kt * 512 + ms * 128:
                                               kt * 512 + ms * 128 + 128],
                                    h2_sb[:, kt * T + n * 512:kt * T + n * 512 + 512],
                                    start=(kt == 0), stop=(kt == NKT - 1))
                            last.then_inc(mm2.h, 1)

            mm.v = M0 + 256
            mm2.v = M20 + 256

            @blk.scalar
            def _(e):
                for g in range(256):
                    e.wait_ge(mm.h, M0 + g + 1)
                    if g >= 2:
                        e.wait_ge(dv.h, V0 + g - 1)
                    e.activation(gate_t[g % 2][:], ps_g[g % 2][:], AF.Silu) \
                        .then_inc(ev.h, 1)

            ev.v = E0 + 256

            @blk.vector
            def _(e):
                def wcast1(fq, i):
                    mat, q4 = i // 4, i % 4
                    mc = fq * 8 + mat * 4 + q4
                    e.wait_ge(ld2.h, L2 + 16 * 4 * (mc + 1))
                    dst = [wib, wub][mat][fq % 2]
                    e.tensor_copy(dst[:, q4 * 2048:(q4 + 1) * 2048],
                                  wstg4f[:]).then_inc(pb.h, 1)
                for bc in range(8):
                    e.wait_ge(ld.h, L0 + 16 * (bc + 1))
                    e.tensor_copy(bf_b[:, bc * 1024:(bc + 1) * 1024],
                                  bstg[:]).then_inc(pb.h, 1)
                for i in range(8):
                    wcast1(0, i)
                for i in range(8):
                    wcast1(1, i)
                for g in range(256):
                    if g % 2 == 0 and g >= 16 and g // 16 + 1 <= 15:
                        wcast1(g // 16 + 1, (g % 16) // 2)
                    e.wait_ge(ev.h, E0 + g + 1)
                    e.wait_ge(mm2.h, M20 + g + 1)
                    if g >= 2:
                        e.wait_ge([spa, spb][g % 2].h, gmark[g - 2])
                    e.tensor_tensor(gu_t[g % 2][:], gate_t[g % 2][:],
                                    ps_u[g % 2][:], ALU.mult).then_inc(dv.h, 1)

            dv.v = V0 + 256


            if debug:
                @blk.sync
                def _(e):
                    e.wait_ge(spa.h, GSP[0])
                    e.wait_ge(spb.h, GSP[1])
                    e.dma_start(dbg["dbg_gu0"], sc_gu[0:128, :]).then_inc(ld.h, 16)
                    e.wait_ge(ld.h, ld.v + 16)
                ld.v += 16

        h2ctx.close()
        fctx.close()

        with contextlib.ExitStack() as sc:
            gu_sb = sc.enter_context(nc.sbuf_tensor("f5gu", [128, 64 * 1024], BF16))
            wstg5 = [sc.enter_context(nc.sbuf_tensor(f"f5ws{i}", [128, 512], F32))
                     for i in range(4)]
            wd_b = [sc.enter_context(nc.sbuf_tensor(f"f5wd{i}", [128, 512], BF16))
                    for i in range(8)]
            mean_b5 = sc.enter_context(nc.sbuf_tensor("f5mb", [128, 1024], F32))
            rstd_b5 = sc.enter_context(nc.sbuf_tensor("f5rb", [128, 1024], F32))
            x_l = [sc.enter_context(nc.sbuf_tensor(f"f5x{i}", [128, 512], F32))
                   for i in range(4)]
            s_l = [sc.enter_context(nc.sbuf_tensor(f"f5s{i}", [128, 512], F32))
                   for i in range(4)]
            g_t = sc.enter_context(nc.sbuf_tensor("f5g", [128, 512], F32))
            t1 = sc.enter_context(nc.sbuf_tensor("f5t1", [128, 512], F32))
            t2 = sc.enter_context(nc.sbuf_tensor("f5t2", [128, 512], F32))
            o_t = [sc.enter_context(nc.sbuf_tensor(f"f5o{i}", [128, 512], F32))
                   for i in range(2)]
            ps5 = [sc.enter_context(nc.psum_tensor(f"f5p{i}", [128, 512], F32))
                   for i in range(8)]
            blk = sc.enter_context(nc.Block())
            rst()
            L0, L2, E0, E20, M0, M20, V0, V20, P0, S0, FI = \
                ld.v, ld2.v, ev.v, ev2.v, mm.v, mm2.v, dv.v, dv2.v, po.v, sp.v, fin.v
            WP5 = [l2a.v, l2b.v, lda.v, ldb.v]
            wmark5 = {}
            for kk in range(512):
                WP5[kk % 4] += 16
                wmark5[kk] = WP5[kk % 4]
            XSP = [x.v for x in xs4]
            xsmark = {}
            for gg in range(64):
                XSP[gg % 4] += 32
                xsmark[gg] = XSP[gg % 4]
            FOP = [fo2[0].v, fo2[1].v]
            fomark = {}
            for gg in range(64):
                FOP[gg % 2] += 16
                fomark[gg] = FOP[gg % 2]

            @blk.tensor
            def _(e):
                for nh in range(2):
                    if nh == 1:
                        e.wait_ge(ev2.h, E20 + 1)
                        e.wait_ge(dv2.h, V20 + 32)
                    for nn in range(2):
                        e.matmul(ps5[nn][:], ones_r_f[0:1, :],
                                 rows65[0:1, nh * 1024 + nn * 512:
                                          nh * 1024 + nn * 512 + 512],
                                 start=True, stop=True)
                    last = None
                    for nn in range(2):
                        last = e.matmul(ps5[2 + nn][:], ones_r_f[32:33, :],
                                        rows65[32:33, nh * 1024 + nn * 512:
                                                 nh * 1024 + nn * 512 + 512],
                                        start=True, stop=True)
                    last.then_inc(mm2.h, 1)
                    e.wait_ge(ld.h, L0 + 16 * 64 * (nh + 1))
                    e.wait_ge(ev2.h, E20 + nh + 1)
                    for mq in range(4):
                        hj = nh * 4 + mq
                        if hj >= 1:
                            e.wait_ge(dv2.h, V20 + 8 * hj)
                        for k in range(64):
                            kk = hj * 64 + k
                            e.wait_ge(po.h, P0 + kk + 1)
                            last = None
                            for g in range(8):
                                ms, nsub = g // 2, g % 2
                                last = e.matmul(ps5[g][:],
                                                wd_b[k % 8][:, ms * 128:(ms + 1) * 128],
                                                gu_sb[:, k * 1024 + nsub * 512:
                                                      k * 1024 + nsub * 512 + 512],
                                                start=(k == 0), stop=(k == 63))
                            last.then_inc(mm.h, 1)

            mm.v = M0 + 512
            mm2.v = M20 + 2

            @blk.scalar
            def _(e):
                def wcast(kk):
                    e.wait_ge([l2a, l2b, lda, ldb][kk % 4].h, wmark5[kk])
                    if kk >= 8:
                        e.wait_ge(mm.h, M0 + kk - 7)
                    e.copy(wd_b[kk % 8][:], wstg5[kk % 4][:]) \
                        .then_inc(po.h, 1)

                for hj in range(9):
                    nh, mq = hj // 4, hj % 4
                    if hj >= 1:
                        for g in range(8):
                            gg = (hj - 1) * 8 + g
                            e.wait_ge(dv.h, V0 + 3 * gg + 3)
                            e.activation(g_t[:], t2[:], AF.Silu).then_inc(ev.h, 1)
                    if hj < 8 and mq == 0:
                        e.wait_ge(mm2.h, M20 + nh + 1)
                        for nn in range(2):
                            e.copy(mean_b5[:, nn * 512:(nn + 1) * 512], ps5[nn][:])
                        last = None
                        for nn in range(2):
                            last = e.copy(rstd_b5[:, nn * 512:(nn + 1) * 512],
                                          ps5[2 + nn][:])
                        last.then_inc(ev2.h, 1)
                    if hj < 8:
                        for k in range(64):
                            wcast(hj * 64 + k)

            ev.v = E0 + 64
            ev2.v = E20 + 2
            po.v = P0 + 512

            @blk.vector
            def _(e):
                for nh in range(2):
                    for mq in range(4):
                        hj = nh * 4 + mq
                        e.wait_ge(mm.h, M0 + 64 * (hj + 1))
                        for g in range(8):
                            ms, nsub = g // 2, g % 2
                            gg = hj * 8 + g
                            e.wait_ge(xs4[gg % 4].h, xsmark[gg])
                            e.tensor_tensor(t1[:], x_l[g % 4][:],
                                            mean_b5[:, nsub * 512:(nsub + 1) * 512],
                                            ALU.subtract)
                            e.tensor_tensor(t2[:], t1[:],
                                            rstd_b5[:, nsub * 512:(nsub + 1) * 512],
                                            ALU.mult)
                            e.tensor_tensor(t2[:], t2[:], ps5[g][:], ALU.add) \
                                .then_inc(dv.h, 3)
                            e.wait_ge(ev.h, E0 + gg + 1)
                            e.tensor_tensor(t1[:], g_t[:], s_l[g % 4][:], ALU.mult)
                            if gg >= 2:
                                e.wait_ge(fo2[gg % 2].h, fomark[gg - 2])
                            e.tensor_tensor(o_t[gg % 2][:], t1[:], x_l[g % 4][:],
                                            ALU.add).then_inc(dv2.h, 1)

            dv.v = V0 + 3 * 64
            dv2.v = V20 + 64

            @blk.gpsimd
            def _(e):
                def coords(gg):
                    hj, g = gg // 8, gg % 8
                    nh, mq = hj // 4, hj % 4
                    ms, nsub = g // 2, g % 2
                    m2 = mq * 4 + ms
                    col = nh * 1024 + nsub * 512
                    return g, m2, col

                def load(gg):
                    g, m2, col = coords(gg)
                    if gg >= 4:
                        e.wait_ge(dv2.h, V20 + gg - 3)
                    e.dma_start(x_l[g % 4][:],
                                xT[m2 * 128:(m2 + 1) * 128, col:col + 512]) \
                        .then_inc(xs4[gg % 4].h, 16)
                    e.dma_start(s_l[g % 4][:],
                                sc_s[m2 * 128:(m2 + 1) * 128, col:col + 512]) \
                        .then_inc(xs4[gg % 4].h, 16)

                for gg in range(4):
                    load(gg)
                for hj in range(8):
                    for g in range(8):
                        gg = hj * 8 + g
                        _, m2, col = coords(gg)
                        e.wait_ge(dv2.h, V20 + gg + 1)
                        e.dma_start(outT[m2 * 128:(m2 + 1) * 128, col:col + 512],
                                    o_t[gg % 2][:]).then_inc(fo2[gg % 2].h, 16)
                        if gg + 4 < 64:
                            load(gg + 4)

            for i in range(4):
                xs4[i].v = XSP[i]
            fo2[0].v, fo2[1].v = FOP[0], FOP[1]

            @blk.sync
            def _(e):
                for nh in range(2):
                    if nh == 1:
                        e.wait_ge(mm.h, M0 + 256)
                    for k in range(64):
                        e.dma_start(gu_sb[:, k * 1024:(k + 1) * 1024],
                                    sc_gu[k * 128:(k + 1) * 128,
                                          nh * 1024:(nh + 1) * 1024]) \
                            .then_inc(ld.h, 16)
                    for mq in range(4):
                        hj = nh * 4 + mq
                        for k in range(64):
                            kk = hj * 64 + k
                            if kk >= 4:
                                e.wait_ge(po.h, P0 + kk - 3)
                            e.dma_start(wstg5[kk % 4][:],
                                        wdownT[k * 128:(k + 1) * 128,
                                               mq * 512:(mq + 1) * 512]) \
                                .then_inc([l2a, l2b, lda, ldb][kk % 4].h, 16)
                e.wait_ge(fo2[0].h, FOP[0])
                e.wait_ge(fo2[1].h, FOP[1])

            ld.v = L0 + 16 * 128
            l2a.v, l2b.v = WP5[0], WP5[1]
            lda.v, ldb.v = WP5[2], WP5[3]


    return nc



def _f32c(a):
    return np.ascontiguousarray(np.asarray(a), dtype=np.float32)


def kernel(**inputs):
    global LAST_DEBUG
    I = inputs
    x = _f32c(I["L0_attn_x"])
    W_in = _f32c(I["L0_attn_W_in"])
    conv_w = _f32c(I["L0_attn_conv_w"])
    conv_b = _f32c(I["L0_attn_conv_b"])
    W_x = _f32c(I["L0_attn_W_x"])
    A_log = _f32c(I["L0_attn_A_log"])
    dt_bias = _f32c(I["L0_attn_dt_bias"])
    out_W = _f32c(I["L0_attn_out_W"])
    Dvec = _f32c(I["L0_attn_D"])
    ln1 = _f32c(I["L0_ffn_ln1_w"])
    ln2 = _f32c(I["L0_ffn_ln2_w"])
    Wq = _f32c(I["L0_ffn_Wq"])
    Wk = _f32c(I["L0_ffn_Wk"])
    Wv = _f32c(I["L0_ffn_Wv"])
    Wo = _f32c(I["L0_ffn_Wo"])
    A = _f32c(I["L0_ffn_A"])
    B = _f32c(I["L0_ffn_B"])
    W_in_base = _f32c(I["L0_ffn_W_in_base"])
    W_up_base = _f32c(I["L0_ffn_W_up_base"])
    W_down = _f32c(I["L0_ffn_W_down"])

    packv = np.zeros((256, 128), np.float32)
    packv[0:128] = conv_w[:, 0, :].reshape(32, 128, 4).transpose(2, 0, 1) \
        .reshape(128, 128)
    packv[128:160] = conv_b.reshape(32, 128)
    packv[160:192] = Dvec.reshape(32, 128)
    packv[192:208] = ln1.reshape(16, 128)
    packv[208:224] = ln2.reshape(16, 128)

    shared = dict(
        w_inT=_f32c(W_in.T),
        wxT=_f32c(W_x.T),
        outWT=_f32c(out_W.T),
        wqT=_f32c(Wq.T), wkT=_f32c(Wk.T), wvT=_f32c(Wv.T), woT=_f32c(Wo.T),
        a_ffn=A, b_ffn=B,
        winbT=_f32c(W_in_base.T), wupbT=_f32c(W_up_base.T),
        wdownT=_f32c(W_down.T),
        a_perm=_f32c(A_log.reshape(32, 128, 16).transpose(1, 0, 2)
                     .reshape(128, 512)),
        packv=packv,
        dtb2d=_f32c(dt_bias.reshape(32, 128)),
        ident_in=np.eye(128, dtype=np.float32),
    )

    ncores = _NCORES
    nc = build_program(debug=_DEBUG)
    in_maps = []
    for c in range(ncores):
        m = dict(shared)
        m["xT"] = _f32c(x[c].T)
        in_maps.append(m)
    res = run_bass_kernel_spmd(nc, in_maps, core_ids=list(range(ncores)))
    LAST_DEBUG = res.results
    out = np.empty((8, T, D), np.float32)
    for c in range(ncores):
        out[c] = np.asarray(res.results[c]["outT"]).T
    return out


# version 12
# speedup vs baseline: 1.0968x; 1.0050x over previous
import contextlib
import numpy as np
import ml_dtypes

import concourse.bass as bass
import concourse.mybir as mybir
from concourse.bass_utils import run_bass_kernel_spmd

F32 = mybir.dt.float32
BF16 = mybir.dt.bfloat16
AF = mybir.ActivationFunctionType
ALU = mybir.AluOpType
AX = mybir.AxisListType

D = 2048
DI = 4096
T = 2048
FF = 8192
NKT = 16
NDG = 32
NFT = 64
NS = 16
NH = 16
RSQ_HD = 0.08838834764831845

_DEBUG = False
_NCORES = 8
LAST_DEBUG = None


class Ctr:
    def __init__(self, handle):
        self.h = handle
        self.v = 0


def build_program(debug=False):
    nc = bass.Bass("TRN2", target_bir_lowering=False)

    xT = nc.dram_tensor("xT", [D, T], F32, kind="ExternalInput").ap()
    w_inT = nc.dram_tensor("w_inT", [D, 2 * DI], F32, kind="ExternalInput").ap()
    wxT = nc.dram_tensor("wxT", [DI, 33], F32, kind="ExternalInput").ap()
    outWT = nc.dram_tensor("outWT", [DI, D], F32, kind="ExternalInput").ap()
    wqT = nc.dram_tensor("wqT", [D, D], F32, kind="ExternalInput").ap()
    wkT = nc.dram_tensor("wkT", [D, D], F32, kind="ExternalInput").ap()
    wvT = nc.dram_tensor("wvT", [D, D], F32, kind="ExternalInput").ap()
    woT = nc.dram_tensor("woT", [D, D], F32, kind="ExternalInput").ap()
    a_ffn = nc.dram_tensor("a_ffn", [D, NS], F32, kind="ExternalInput").ap()
    b_ffn = nc.dram_tensor("b_ffn", [NS, FF], F32, kind="ExternalInput").ap()
    winbT = nc.dram_tensor("winbT", [D, FF], F32, kind="ExternalInput").ap()
    wupbT = nc.dram_tensor("wupbT", [D, FF], F32, kind="ExternalInput").ap()
    wdownT = nc.dram_tensor("wdownT", [FF, D], F32, kind="ExternalInput").ap()
    a_perm = nc.dram_tensor("a_perm", [128, 512], F32, kind="ExternalInput").ap()
    packv = nc.dram_tensor("packv", [256, 128], F32, kind="ExternalInput").ap()
    dtb2d = nc.dram_tensor("dtb2d", [32, 128], F32, kind="ExternalInput").ap()
    ident_in = nc.dram_tensor("ident_in", [128, 128], F32, kind="ExternalInput").ap()

    outT = nc.dram_tensor("outT", [D, T], F32, kind="ExternalOutput").ap()

    sc_xn = nc.dram_tensor("sc_xn", [D, T], BF16, kind="Internal").ap()
    sc_h1 = nc.dram_tensor("sc_h1", [D, T], BF16, kind="Internal").ap()
    sc_h2 = nc.dram_tensor("sc_h2", [D, T], BF16, kind="Internal").ap()
    sc_xc = nc.dram_tensor("sc_xc", [DI, T], BF16, kind="Internal").ap()
    sc_z = nc.dram_tensor("sc_z", [DI, T], BF16, kind="Internal").ap()
    sc_yg = nc.dram_tensor("sc_yg", [DI, T], BF16, kind="Internal").ap()
    sc_s = nc.dram_tensor("sc_s", [D, T], F32, kind="Internal").ap()
    sc_attn = nc.dram_tensor("sc_attn", [D, T], BF16, kind="Internal").ap()
    sc_gu = nc.dram_tensor("sc_gu", [FF, T], BF16, kind="Internal").ap()

    dbg = {}
    if debug:
        for nm, shp, dt_ in [("dbg_p", [33, T], F32), ("dbg_rows", [4, T], F32),
                             ("dbg_xc0", [128, T], BF16), ("dbg_z0", [128, T], BF16),
                             ("dbg_yg0", [128, T], BF16), ("dbg_s0", [128, T], F32),
                             ("dbg_xn0", [128, T], BF16), ("dbg_h10", [128, T], BF16),
                             ("dbg_attn0", [128, T], BF16), ("dbg_mod", [16, T], F32),
                             ("dbg_gu0", [128, T], BF16), ("dbg_hA", [16, T], F32),
                             ("dbg_aux", [4, T], F32)]:
            dbg[nm] = nc.dram_tensor(nm, shp, dt_, kind="ExternalOutput").ap()

    ctx = contextlib.ExitStack()
    with ctx:
        packT = ctx.enter_context(nc.sbuf_tensor("packT", [128, 256], F32))
        ident_f = ctx.enter_context(nc.sbuf_tensor("ident_f", [128, 128], F32))
        ident_b = ctx.enter_context(nc.sbuf_tensor("ident_b", [128, 128], BF16))
        ones_r_f = ctx.enter_context(nc.sbuf_tensor("ones_r_f", [65, 128], F32))
        ones_c_f = ctx.enter_context(nc.sbuf_tensor("ones_c_f", [128, 1], F32))
        ones_c_b = ctx.enter_context(nc.sbuf_tensor("ones_c_b", [128, 1], BF16))
        ones32 = ctx.enter_context(nc.sbuf_tensor("ones32", [32, 1], F32))
        mdtb = ctx.enter_context(nc.sbuf_tensor("mdtb", [1, 128], F32))
        rows65 = ctx.enter_context(nc.sbuf_tensor("rows65", [65, T], F32))
        dt_row = ctx.enter_context(nc.sbuf_tensor("dt_row", [1, T], F32))
        hA_g = ctx.enter_context(nc.sbuf_tensor("hA_g", [16, T], F32))
        eps5 = ctx.enter_context(nc.sbuf_tensor("eps5", [1, 1], F32))
        eps6 = ctx.enter_context(nc.sbuf_tensor("eps6", [1, 1], F32))

        sems = {}
        for nm in ["ld", "ld2", "sp", "sp2", "mm", "mm2", "ev", "ev2",
                   "dv", "dv2", "po", "fin", "pb", "pc", "pd",
                   "lda", "ldb", "l2a", "l2b", "spa", "spb", "s2a", "s2b",
                   "xs0", "xs1", "xs2", "xs3", "fo0", "fo1"]:
            sems[nm] = Ctr(ctx.enter_context(nc.semaphore(nm)))
        ld, ld2, sp, sp2 = sems["ld"], sems["ld2"], sems["sp"], sems["sp2"]
        mm, mm2, ev, ev2 = sems["mm"], sems["mm2"], sems["ev"], sems["ev2"]
        dv, dv2, po, fin = sems["dv"], sems["dv2"], sems["po"], sems["fin"]
        pb, pc, pd = sems["pb"], sems["pc"], sems["pd"]
        lda, ldb = sems["lda"], sems["ldb"]
        l2a, l2b = sems["l2a"], sems["l2b"]
        spa, spb = sems["spa"], sems["spb"]
        s2a, s2b = sems["s2a"], sems["s2b"]
        xs4 = [sems["xs0"], sems["xs1"], sems["xs2"], sems["xs3"]]
        fo2 = [sems["fo0"], sems["fo1"]]
        def rst():
            pass

        mctx = contextlib.ExitStack()
        wx_b = mctx.enter_context(nc.sbuf_tensor("wx_b", [128, 33 * NDG], BF16))
        aneg = mctx.enter_context(nc.sbuf_tensor("aneg", [128, 512], F32))
        p_sb = mctx.enter_context(nc.sbuf_tensor("p_sb", [33, T], F32))

        with contextlib.ExitStack() as sc:
            stg = sc.enter_context(nc.sbuf_tensor("s_stg", [128, 1056], F32))
            stg2 = sc.enter_context(nc.sbuf_tensor("s_stg2", [128, 256], F32))
            dtb32 = sc.enter_context(nc.sbuf_tensor("s_dtb32", [32, 128], F32))
            s_row = sc.enter_context(nc.sbuf_tensor("s_row", [1, 256], F32))
            expa = sc.enter_context(nc.sbuf_tensor("s_expa", [128, 512], F32))
            ps_a = sc.enter_context(nc.psum_tensor("s_psa", [128, 512], F32))
            ps_b = sc.enter_context(nc.psum_tensor("s_psb", [128, 512], F32))
            blk = sc.enter_context(nc.Block())
            rst()
            L0, M0, E0, P0, V0 = ld.v, mm.v, ev.v, po.v, dv.v

            @blk.sync
            def _(e):
                e.dma_start(ident_f[:], ident_in).then_inc(ld.h, 16)
                e.dma_start(stg2[:, 0:128], packv[0:128, :]).then_inc(ld.h, 16)
                e.dma_start(stg2[:, 128:256], packv[128:256, :]).then_inc(ld.h, 16)
                e.dma_start(expa[:], a_perm).then_inc(ld.h, 16)
                e.dma_start(dtb32[:], dtb2d).then_inc(ld.h, 16)
                for dg in range(NDG):
                    e.dma_start(stg[:, dg * 33:(dg + 1) * 33],
                                wxT[dg * 128:(dg + 1) * 128, :]).then_inc(ld.h, 16)

            ld.v = L0 + 16 * (5 + NDG)

            @blk.gpsimd
            def _(e):
                e.memset(ones_r_f[:], 1.0)
                e.memset(ones_c_f[:], 1.0)
                e.memset(ones_c_b[:], 1.0)
                e.memset(ones32[:], 1.0)
                e.memset(eps5[:], 1e-5)
                e.memset(eps6[:], 1e-6)
                e.memset(p_sb[:], 0.0).then_inc(po.h, 1)
                e.wait_ge(ld.h, L0 + 16 * (5 + NDG))
                e.tensor_copy(ident_b[:], ident_f[:])
                e.tensor_copy(wx_b[:], stg[:, 0:1056]).then_inc(po.h, 1)

            po.v = P0 + 2

            @blk.tensor
            def _(e):
                e.wait_ge(ld.h, L0 + 16 * (5 + NDG))
                e.transpose(ps_a[:, 0:128], stg2[:, 0:128], ident_f[:])
                e.transpose(ps_b[:, 0:128], stg2[:, 128:256], ident_f[:]) \
                    .then_inc(mm.h, 1)
                e.wait_ge(ev.h, E0 + 1)
                e.wait_ge(po.h, P0 + 1)
                e.wait_ge(ld.h, L0 + 16 * (5 + NDG))
                e.matmul(ps_a[0:1, 128:256], ones32[:], dtb32[:],
                         start=True, stop=True).then_inc(mm.h, 1)

            mm.v = M0 + 2

            @blk.scalar
            def _(e):
                e.wait_ge(mm.h, M0 + 1)
                e.copy(packT[:, 0:128], ps_a[:, 0:128])
                e.copy(packT[:, 128:256], ps_b[:, 0:128]).then_inc(ev.h, 1)
                e.wait_ge(ld.h, L0 + 16 * (5 + NDG))
                e.activation(aneg[:], expa[:], AF.Exp).then_inc(ev.h, 1)

            ev.v = E0 + 2

            @blk.vector
            def _(e):
                e.wait_ge(mm.h, M0 + 2)
                e.tensor_copy(s_row[0:1, 0:128], ps_a[0:1, 128:256])
                e.wait_ge(ev.h, E0 + 2)
                e.tensor_scalar(aneg[:], aneg[:], -1.0, None, ALU.mult)
                e.scalar_tensor_tensor(s_row[0:1, 128:256], s_row[0:1, 0:128],
                                       1.0 / DI, s_row[0:1, 0:128],
                                       ALU.mult, ALU.bypass,
                                       accum_out=mdtb[0:1, 0:1]) \
                    .then_inc(dv.h, 1)

            dv.v = V0 + 1

        with contextlib.ExitStack() as xscope:
            xn_sb = xscope.enter_context(
                nc.sbuf_tensor("xn_sb", [128, NKT * T], BF16))

            with contextlib.ExitStack() as sc:
                x_t = [sc.enter_context(nc.sbuf_tensor(f"p1x{i}", [128, T], F32))
                       for i in range(2)]
                xsq2 = [sc.enter_context(nc.sbuf_tensor(f"p1sq{i}", [128, T], F32))
                        for i in range(2)]
                mean_b = sc.enter_context(nc.sbuf_tensor("p1mb", [128, T], F32))
                rstd_b = sc.enter_context(nc.sbuf_tensor("p1rb", [128, T], F32))
                ps_s = [sc.enter_context(nc.psum_tensor(f"p1s{i}", [128, 512], F32))
                        for i in range(4)]
                ps_q = [sc.enter_context(nc.psum_tensor(f"p1q{i}", [128, 512], F32))
                        for i in range(4)]
                row_a = sc.enter_context(nc.sbuf_tensor("p1ra", [1, T], F32))
                row_b = sc.enter_context(nc.sbuf_tensor("p1rb_", [1, T], F32))
                row_c = sc.enter_context(nc.sbuf_tensor("p1rc", [1, T], F32))
                blk = sc.enter_context(nc.Block())
                rst()
                L0, L2, E0, M0, V0, S0 = ld.v, ld2.v, ev.v, mm.v, dv.v, sp.v
                LP = [lda.v, ldb.v]
                xmark = {}
                for kt in range(NKT):
                    LP[kt % 2] += 16
                    xmark[kt] = LP[kt % 2]
                L2P = [l2a.v, l2b.v]
                xmark2 = {}
                for kt in range(NKT):
                    L2P[kt % 2] += 16
                    xmark2[kt] = L2P[kt % 2]

                @blk.sync
                def _(e):
                    for kt in range(NKT):
                        if kt >= 2:
                            e.wait_ge(ev.h, E0 + kt - 1)
                            e.wait_ge(mm.h, M0 + 2 * (kt - 1))
                        e.dma_start(x_t[kt % 2][:],
                                    xT[kt * 128:(kt + 1) * 128, :]) \
                            .then_inc([lda, ldb][kt % 2].h, 16)

                @blk.scalar
                def _(e):
                    for kt in range(NKT):
                        e.wait_ge([lda, ldb][kt % 2].h, xmark[kt])
                        if kt >= 2:
                            e.wait_ge(mm.h, M0 + 2 * (kt - 1))
                        e.activation(xsq2[kt % 2][:], x_t[kt % 2][:], AF.Square) \
                            .then_inc(ev.h, 1)

                ev.v = E0 + NKT

                @blk.tensor
                def _(e):
                    for kt in range(NKT):
                        e.wait_ge([lda, ldb][kt % 2].h, xmark[kt])
                        for n in range(4):
                            e.matmul(ps_s[n][0:1, :], ones_c_f[:],
                                     x_t[kt % 2][:, n * 512:(n + 1) * 512],
                                     start=(kt == 0), stop=(kt == NKT - 1))
                        e.wait_ge(ev.h, E0 + kt + 1)
                        last = None
                        for n in range(4):
                            last = e.matmul(ps_q[n][0:1, :], ones_c_f[:],
                                            xsq2[kt % 2][:, n * 512:(n + 1) * 512],
                                            start=(kt == 0), stop=(kt == NKT - 1))
                        last.then_inc(mm.h, 2)
                    e.wait_ge(dv.h, V0 + 2)
                    for n in range(4):
                        e.matmul(ps_s[n][:], ones_r_f[0:1, :],
                                 rows65[0:1, n * 512:(n + 1) * 512],
                                 start=True, stop=True)
                    last = None
                    for n in range(4):
                        last = e.matmul(ps_q[n][:], ones_r_f[32:33, :],
                                        rows65[32:33, n * 512:(n + 1) * 512],
                                        start=True, stop=True)
                    last.then_inc(mm.h, 1)

                mm.v = M0 + 2 * NKT + 1

                @blk.vector
                def _(e):
                    e.wait_ge(mm.h, M0 + 2 * NKT)
                    for n in range(4):
                        e.tensor_scalar(rows65[0:1, n * 512:(n + 1) * 512],
                                        ps_s[n][0:1, :], 1.0 / D, None, ALU.mult)
                        e.tensor_scalar(row_a[0:1, n * 512:(n + 1) * 512],
                                        ps_q[n][0:1, :], 1.0 / D, None, ALU.mult)
                    e.tensor_tensor(row_b[:], rows65[0:1, :], rows65[0:1, :], ALU.mult)
                    e.tensor_tensor(row_a[:], row_a[:], row_b[:], ALU.subtract) \
                        .then_inc(dv.h, 1)
                    e.wait_ge(ev.h, E0 + NKT + 1)
                    e.reciprocal(rows65[32:33, :], row_c[:]).then_inc(dv.h, 1)

                dv.v = V0 + 2

                @blk.scalar
                def _(e):
                    e.wait_ge(dv.h, V0 + 1)
                    e.activation(row_c[:], row_a[:], AF.Sqrt, bias=eps5[0:1, 0:1]) \
                        .then_inc(ev.h, 1)
                    e.wait_ge(mm.h, M0 + 2 * NKT + 1)
                    for n in range(4):
                        e.copy(mean_b[:, n * 512:(n + 1) * 512], ps_s[n][:])
                    last = None
                    for n in range(4):
                        last = e.copy(rstd_b[:, n * 512:(n + 1) * 512], ps_q[n][:])
                    last.then_inc(ev.h, 1)

                ev.v = E0 + NKT + 2

                @blk.sync
                def _(e):
                    e.wait_ge(ev.h, E0 + NKT + 2)
                    for kt in range(NKT):
                        if kt >= 2:
                            e.wait_ge(dv.h, V0 + 2 + kt - 1)
                        e.dma_start(x_t[kt % 2][:],
                                    xT[kt * 128:(kt + 1) * 128, :]) \
                            .then_inc([l2a, l2b][kt % 2].h, 16)

                @blk.vector
                def _(e):
                    for kt in range(NKT):
                        e.wait_ge([l2a, l2b][kt % 2].h, xmark2[kt])
                        e.tensor_tensor(xsq2[kt % 2][:], x_t[kt % 2][:], mean_b[:],
                                        ALU.subtract)
                        e.tensor_tensor(xn_sb[:, kt * T:(kt + 1) * T],
                                        xsq2[kt % 2][:],
                                        rstd_b[:], ALU.mult).then_inc(dv.h, 1)

                dv.v = V0 + 2 + NKT

                @blk.gpsimd
                def _(e):
                    for kt in range(NKT):
                        e.wait_ge(dv.h, V0 + 2 + kt + 1)
                        e.dma_start(sc_xn[kt * 128:(kt + 1) * 128, :],
                                    xn_sb[:, kt * T:(kt + 1) * T]).then_inc(sp.h, 16)
                    e.wait_ge(sp.h, S0 + 16 * NKT)

                sp.v = S0 + 16 * NKT
                lda.v, ldb.v = LP[0], LP[1]
                l2a.v, l2b.v = L2P[0], L2P[1]

                if debug:
                    @blk.sync
                    def _(e):
                        e.wait_ge(dv.h, V0 + 2 + NKT)
                        e.dma_start(dbg["dbg_rows"][0:1, :], rows65[0:1, :]) \
                            .then_inc(ld2.h, 16)
                        e.dma_start(dbg["dbg_rows"][1:2, :], rows65[32:33, :]) \
                            .then_inc(ld2.h, 16)
                        e.dma_start(dbg["dbg_xn0"], sc_xn[0:128, :]) \
                            .then_inc(ld2.h, 16)
                        e.wait_ge(ld2.h, ld2.v + 48)
                    ld2.v = ld2.v + 48

            with contextlib.ExitStack() as sc:
                wstg = [sc.enter_context(nc.sbuf_tensor(f"m1wstg{i}", [128, 2048], F32))
                        for i in range(2)]
                wqb = [sc.enter_context(nc.sbuf_tensor(f"m1wq{i}", [128, 8192], BF16))
                       for i in range(2)]
                xs_pad = [sc.enter_context(nc.sbuf_tensor(f"m1xp{i}", [128, 2052], BF16))
                          for i in range(2)]
                acc_a = sc.enter_context(nc.sbuf_tensor("m1aa", [128, T], F32))
                acc_b = sc.enter_context(nc.sbuf_tensor("m1ab", [128, T], F32))
                xc_t = [sc.enter_context(nc.sbuf_tensor(f"m1xc{i}", [128, T], BF16))
                        for i in range(2)]
                z_t = [sc.enter_context(nc.sbuf_tensor(f"m1z{i}", [128, T], BF16))
                       for i in range(2)]
                ps_xz = [sc.enter_context(nc.psum_tensor(f"m1p{i}", [128, 512], F32))
                         for i in range(4)]
                ps_p = [sc.enter_context(nc.psum_tensor(f"m1pp{i}", [128, 512], F32))
                        for i in range(4)]
                blk = sc.enter_context(nc.Block())
                rst()
                L0, P0, M0, M20, E0, E20, V0, V20, S0, S20 = \
                    ld.v, po.v, mm.v, mm2.v, ev.v, ev2.v, dv.v, dv2.v, sp.v, sp2.v
                SPP = [spa.v, spb.v]
                xcmark = {}
                for m in range(32):
                    SPP[m % 2] += 16
                    xcmark[m] = SPP[m % 2]
                SZP = [s2a.v, s2b.v]
                zmark = {}
                for m in range(32, 64):
                    SZP[m % 2] += 16
                    zmark[m] = SZP[m % 2]

                @blk.gpsimd
                def _(e):
                    e.memset(xs_pad[0][:, 0:3], 0.0)
                    e.memset(xs_pad[1][:, 0:3], 0.0).then_inc(po.h, 1)

                P1 = P0 + 1
                po.v = P1

                PB1 = pb.v
                pb.v = PB1 + 64

                @blk.sync
                def _(e):
                    for q in range(16):
                        for quarter in range(4):
                            hs = 4 * q + quarter
                            if hs >= 2:
                                e.wait_ge(pb.h, PB1 + hs - 1)
                            for kt in range(4):
                                ktg = quarter * 4 + kt
                                e.dma_start(
                                    wstg[hs % 2][:, kt * 512:(kt + 1) * 512],
                                    w_inT[ktg * 128:(ktg + 1) * 128,
                                          q * 512:(q + 1) * 512]).then_inc(ld.h, 16)

                ld.v = L0 + 16 * 256

                @blk.gpsimd
                def _(e):
                    def spills(qq):
                        for ms in range(4):
                            m = qq * 4 + ms
                            if m < 32:
                                e.wait_ge(ev2.h, E20 + m + 1)
                                e.dma_start(sc_xc[m * 128:(m + 1) * 128, :],
                                            xc_t[m % 2][:]) \
                                    .then_inc([spa, spb][m % 2].h, 16)
                            else:
                                e.wait_ge(ev.h, E0 + 4 * (m + 1))
                                e.dma_start(sc_z[(m - 32) * 128:(m - 31) * 128, :],
                                            z_t[m % 2][:]) \
                                    .then_inc([s2a, s2b][m % 2].h, 16)
                    for q in range(16):
                        if q >= 2:
                            spills(q - 2)
                    spills(14)
                    spills(15)
                    e.wait_ge(spa.h, SPP[0])
                    e.wait_ge(spb.h, SPP[1])
                    e.wait_ge(s2a.h, SZP[0])
                    e.wait_ge(s2b.h, SZP[1])

                spa.v, spb.v = SPP[0], SPP[1]
                s2a.v, s2b.v = SZP[0], SZP[1]

                @blk.tensor
                def _(e):
                    for m in range(64):
                        q, msub = m // 4, m % 4
                        pm = m - 2
                        if 0 <= pm < 32:
                            e.wait_ge(ev2.h, E20 + pm + 1)
                            for n in range(4):
                                if pm >= 1:
                                    e.wait_ge(dv2.h, V20 + (pm - 1) * 4 + n + 1)
                                e.matmul(ps_p[n][0:33, :],
                                         wx_b[:, pm * 33:(pm + 1) * 33],
                                         xc_t[pm % 2][:, n * 512:(n + 1) * 512],
                                         start=True, stop=True).then_inc(mm2.h, 1)
                        if msub == 0:
                            e.wait_ge(pb.h, PB1 + 4 * (q + 1))
                        for n in range(4):
                            g = 4 * m + n
                            if g >= 4:
                                e.wait_ge(ev.h, E0 + g - 3)
                            last = None
                            for kt in range(NKT):
                                last = e.matmul(
                                    ps_xz[g % 4][:],
                                    wqb[q % 2][:, kt * 512 + msub * 128:
                                               kt * 512 + msub * 128 + 128],
                                    xn_sb[:, kt * T + n * 512:kt * T + n * 512 + 512],
                                    start=(kt == 0), stop=(kt == NKT - 1))
                            last.then_inc(mm.h, 1)

                mm.v = M0 + 256
                mm2.v = M20 + 128

                @blk.scalar
                def _(e):
                    def wcast(hs):
                        q = hs // 4
                        quarter = hs % 4
                        e.wait_ge(ld.h, L0 + 16 * 4 * (hs + 1))
                        if q >= 2 and quarter == 0:
                            e.wait_ge(mm.h, M0 + 16 * (q - 1))
                        e.copy(wqb[q % 2][:, quarter * 2048:
                                          (quarter + 1) * 2048],
                               wstg[hs % 2][:]).then_inc(pb.h, 1)
                    for hs in range(8):
                        wcast(hs)
                    for m in range(64):
                        if m % 4 == 0 and m >= 4:
                            q = m // 4 + 1
                            if q <= 15:
                                for quarter in range(4):
                                    wcast(4 * q + quarter)
                        for n in range(4):
                            g = 4 * m + n
                            e.wait_ge(mm.h, M0 + g + 1)
                            if m < 32:
                                if n == 0 and m >= 2:
                                    e.wait_ge(dv.h, V0 + m - 1)
                                e.copy(xs_pad[m % 2][:, 3 + n * 512:3 + n * 512 + 512],
                                       ps_xz[g % 4][:]).then_inc(ev.h, 1)
                            else:
                                if n == 0 and m >= 34:
                                    e.wait_ge([s2a, s2b][m % 2].h, zmark[m - 2])
                                e.copy(z_t[m % 2][:, n * 512:(n + 1) * 512],
                                       ps_xz[g % 4][:]).then_inc(ev.h, 1)
                        if m < 32:
                            e.wait_ge(dv.h, V0 + m + 1)
                            if m >= 2:
                                e.wait_ge([spa, spb][m % 2].h, xcmark[m - 2])
                                e.wait_ge(mm2.h, M20 + 4 * (m - 1))
                            e.activation(xc_t[m % 2][:], acc_b[:], AF.Silu,
                                         bias=packT[:, 128 + m:129 + m]) \
                                .then_inc(ev2.h, 1)

                ev.v = E0 + 256
                ev2.v = E20 + 32

                @blk.vector
                def _(e):
                    for m in range(32):
                        xs = xs_pad[m % 2]
                        e.wait_ge(ev.h, E0 + 4 * (m + 1))
                        e.tensor_scalar(acc_a[:], xs[:, 0:2048],
                                        packT[:, m:m + 1], None, ALU.mult)
                        if m >= 1:
                            e.wait_ge(ev2.h, E20 + m)
                        e.scalar_tensor_tensor(acc_b[:], xs[:, 1:2049],
                                               packT[:, 32 + m:33 + m], acc_a[:],
                                               ALU.mult, ALU.add)
                        e.scalar_tensor_tensor(acc_a[:], xs[:, 2:2050],
                                               packT[:, 64 + m:65 + m], acc_b[:],
                                               ALU.mult, ALU.add)
                        e.scalar_tensor_tensor(acc_b[:], xs[:, 3:2051],
                                               packT[:, 96 + m:97 + m], acc_a[:],
                                               ALU.mult, ALU.add).then_inc(dv.h, 1)
                        pm = m - 2
                        if pm >= 0:
                            for n in range(4):
                                e.wait_ge(mm2.h, M20 + pm * 4 + n + 1)
                                e.tensor_tensor(p_sb[0:33, n * 512:(n + 1) * 512],
                                                p_sb[0:33, n * 512:(n + 1) * 512],
                                                ps_p[n][0:33, :], ALU.add) \
                                    .then_inc(dv2.h, 1)
                    for pm in [30, 31]:
                        for n in range(4):
                            e.wait_ge(mm2.h, M20 + pm * 4 + n + 1)
                            e.tensor_tensor(p_sb[0:33, n * 512:(n + 1) * 512],
                                            p_sb[0:33, n * 512:(n + 1) * 512],
                                            ps_p[n][0:33, :], ALU.add) \
                                .then_inc(dv2.h, 1)

                dv.v = V0 + 32
                dv2.v = V20 + 128

        with contextlib.ExitStack() as sc:
            xn_l = [sc.enter_context(nc.sbuf_tensor(f"e0x{i}", [128, T], BF16))
                    for i in range(2)]
            sq2 = [sc.enter_context(nc.sbuf_tensor(f"e0q{i}", [128, T], F32))
                   for i in range(2)]
            h1t = [sc.enter_context(nc.sbuf_tensor(f"e0h1{i}", [128, T], BF16))
                   for i in range(2)]
            h2t = [sc.enter_context(nc.sbuf_tensor(f"e0h2{i}", [128, T], BF16))
                   for i in range(2)]
            rms_e = sc.enter_context(nc.sbuf_tensor("e0rms", [128, T], F32))
            ps_r = [sc.enter_context(nc.psum_tensor(f"e0p{i}", [128, 512], F32))
                    for i in range(4)]
            row_a = sc.enter_context(nc.sbuf_tensor("e0ra", [1, T], F32))
            row_c = sc.enter_context(nc.sbuf_tensor("e0rc", [1, T], F32))
            blk = sc.enter_context(nc.Block())
            rst()
            L0, L2, E0, M0, V0 = ld.v, ld2.v, ev.v, mm.v, dv.v
            SPP1 = sp.v
            LP = [lda.v, ldb.v]
            xmark = {}
            for kt in range(NKT):
                LP[kt % 2] += 16
                xmark[kt] = LP[kt % 2]
            L2P = [l2a.v, l2b.v]
            xmark2 = {}
            for kt in range(NKT):
                L2P[kt % 2] += 16
                xmark2[kt] = L2P[kt % 2]
            S2P = [s2a.v, s2b.v]

            @blk.sync
            def _(e):
                for kt in range(NKT):
                    e.wait_ge(sp.h, SPP1 - 16 * (NKT - 1 - kt))
                    if kt >= 2:
                        e.wait_ge(ev.h, E0 + kt - 1)
                    e.dma_start(xn_l[kt % 2][:],
                                sc_xn[kt * 128:(kt + 1) * 128, :]) \
                        .then_inc([lda, ldb][kt % 2].h, 16)
                e.wait_ge(ev.h, E0 + NKT + 2)
                for kt in range(NKT):
                    if kt >= 2:
                        e.wait_ge(dv.h, V0 + 2 + 2 * (kt - 2) + 2)
                    e.dma_start(xn_l[kt % 2][:],
                                sc_xn[kt * 128:(kt + 1) * 128, :]) \
                        .then_inc([l2a, l2b][kt % 2].h, 16)

            lda.v, ldb.v = LP[0], LP[1]
            l2a.v, l2b.v = L2P[0], L2P[1]

            @blk.scalar
            def _(e):
                for kt in range(NKT):
                    e.wait_ge([lda, ldb][kt % 2].h, xmark[kt])
                    if kt >= 2:
                        e.wait_ge(mm.h, M0 + kt - 1)
                    e.activation(sq2[kt % 2][:], xn_l[kt % 2][:], AF.Square) \
                        .then_inc(ev.h, 1)
                e.wait_ge(dv.h, V0 + 1)
                e.activation(row_c[:], row_a[:], AF.Sqrt, bias=eps6[0:1, 0:1]) \
                    .then_inc(ev.h, 1)
                e.wait_ge(mm.h, M0 + NKT + 1)
                last = None
                for n in range(4):
                    last = e.copy(rms_e[:, n * 512:(n + 1) * 512], ps_r[n][:])
                last.then_inc(ev.h, 1)

            ev.v = E0 + NKT + 2

            @blk.tensor
            def _(e):
                e.wait_ge(dv.h, V0)
                for kt in range(NKT):
                    e.wait_ge(ev.h, E0 + kt + 1)
                    last = None
                    for n in range(4):
                        last = e.matmul(ps_r[n][0:1, :], ones_c_f[:],
                                        sq2[kt % 2][:, n * 512:(n + 1) * 512],
                                        start=(kt == 0), stop=(kt == NKT - 1))
                    last.then_inc(mm.h, 1)
                e.wait_ge(dv.h, V0 + 2)
                last = None
                for n in range(4):
                    last = e.matmul(ps_r[n][:], ones_r_f[64:65, :],
                                    rows65[64:65, n * 512:(n + 1) * 512],
                                    start=True, stop=True)
                last.then_inc(mm.h, 1)

            mm.v = M0 + NKT + 1

            @blk.vector
            def _(e):
                e.wait_ge(mm.h, M0 + NKT)
                last = None
                for n in range(4):
                    last = e.tensor_scalar(row_a[0:1, n * 512:(n + 1) * 512],
                                           ps_r[n][0:1, :], 1.0 / D, None, ALU.mult)
                last.then_inc(dv.h, 1)
                e.wait_ge(ev.h, E0 + NKT + 1)
                e.reciprocal(rows65[64:65, :], row_c[:]).then_inc(dv.h, 1)
                e.wait_ge(ev.h, E0 + NKT + 2)
                for kt in range(NKT):
                    e.wait_ge([l2a, l2b][kt % 2].h, xmark2[kt])
                    if kt >= 2:
                        e.wait_ge(s2a.h, S2P[0] + 16 * (kt - 1))
                        e.wait_ge(s2b.h, S2P[1] + 16 * (kt - 1))
                    e.scalar_tensor_tensor(h1t[kt % 2][:],
                                           xn_l[kt % 2][:],
                                           packT[:, 192 + kt:193 + kt],
                                           rms_e[:], ALU.mult, ALU.mult) \
                        .then_inc(dv.h, 1)
                    e.scalar_tensor_tensor(h2t[kt % 2][:],
                                           xn_l[kt % 2][:],
                                           packT[:, 208 + kt:209 + kt],
                                           rms_e[:], ALU.mult, ALU.mult) \
                        .then_inc(dv.h, 1)

            dv.v = V0 + 2 + 2 * NKT

            @blk.gpsimd
            def _(e):
                for kt in range(NKT):
                    e.wait_ge(dv.h, V0 + 2 + 2 * kt + 2)
                    e.dma_start(sc_h1[kt * 128:(kt + 1) * 128, :],
                                h1t[kt % 2][:]).then_inc(s2a.h, 16)
                    e.dma_start(sc_h2[kt * 128:(kt + 1) * 128, :],
                                h2t[kt % 2][:]).then_inc(s2b.h, 16)
                e.wait_ge(s2a.h, S2P[0] + 16 * NKT)
                e.wait_ge(s2b.h, S2P[1] + 16 * NKT)

            s2a.v = S2P[0] + 16 * NKT
            s2b.v = S2P[1] + 16 * NKT
            EV_RMS = E0 + NKT + 2
            DV_RR = V0 + 2
            S2A_H = s2a.v
            S2B_H = s2b.v

        m2ctx = contextlib.ExitStack()
        dtb_sb = m2ctx.enter_context(nc.sbuf_tensor("dtb_sb", [128, T], F32))
        carry = m2ctx.enter_context(nc.sbuf_tensor("carry", [128, 512], F32))
        with contextlib.ExitStack() as sc:
            ps_bc = [sc.enter_context(nc.psum_tensor(f"m2b{i}", [128, 512], F32))
                     for i in range(4)]
            row_a = sc.enter_context(nc.sbuf_tensor("m2ra", [1, T], F32))
            row_b = sc.enter_context(nc.sbuf_tensor("m2rb", [1, T], F32))
            row_c = sc.enter_context(nc.sbuf_tensor("m2rc", [1, T], F32))
            blk = sc.enter_context(nc.Block())
            rst()
            L0, E0, V0, M0 = ld.v, ev.v, dv.v, mm.v

            @blk.sync
            def _(e):
                e.dma_start(row_a[:], p_sb[32:33, :]).then_inc(ld.h, 16)
                if debug:
                    e.wait_ge(ev.h, E0 + 2)
                    e.dma_start(dbg["dbg_p"], p_sb[:]).then_inc(ld.h, 16)
                    e.dma_start(dbg["dbg_rows"][2:3, :], dt_row[0:1, :]).then_inc(ld.h, 16)
                    e.dma_start(dbg["dbg_aux"][0:1, :], row_a[:]).then_inc(ld.h, 16)
                    e.dma_start(dbg["dbg_aux"][1:2, :], row_b[:]).then_inc(ld.h, 16)
                    e.dma_start(dbg["dbg_aux"][2:3, :], row_c[:]).then_inc(ld.h, 16)
                    e.dma_start(dbg["dbg_aux"][3:4, 0:1], mdtb[0:1, 0:1]).then_inc(ld.h, 16)
                    e.wait_ge(ld.h, L0 + 16 * 7)

            ld.v = L0 + (16 * 7 if debug else 16)

            @blk.scalar
            def _(e):
                e.wait_ge(ld.h, L0 + 16)
                e.activation(row_b[:], row_a[:], AF.Abs, bias=mdtb[0:1, 0:1])
                e.activation(row_c[:], row_b[:], AF.Exp, scale=-1.0)
                e.activation(row_b[:], row_c[:], AF.Ln, bias=1.0)
                e.activation(row_c[:], row_a[:], AF.Relu, bias=mdtb[0:1, 0:1]) \
                    .then_inc(ev.h, 1)
                e.wait_ge(mm.h, M0 + 1)
                last = None
                for n in range(4):
                    last = e.copy(dtb_sb[:, n * 512:(n + 1) * 512], ps_bc[n][:])
                last.then_inc(ev.h, 1)

            ev.v = E0 + 2

            @blk.vector
            def _(e):
                e.wait_ge(ev.h, E0 + 1)
                e.tensor_tensor(dt_row[0:1, :], row_b[:], row_c[:], ALU.add) \
                    .then_inc(dv.h, 1)

            dv.v = V0 + 1

            @blk.tensor
            def _(e):
                e.wait_ge(ev.h, EV_RMS)
                e.wait_ge(dv.h, V0 + 1)
                last = None
                for n in range(4):
                    last = e.matmul(ps_bc[n][:], ones_r_f[0:1, :],
                                    dt_row[0:1, n * 512:(n + 1) * 512],
                                    start=True, stop=True)
                last.then_inc(mm.h, 1)

            mm.v = M0 + 1

        with contextlib.ExitStack() as sc:
            wsb_all = sc.enter_context(nc.sbuf_tensor("m3w", [128, 16 * 1024], BF16))
            cb_all = sc.enter_context(nc.sbuf_tensor("m3c", [128, 16 * 1024], BF16))
            xc_l = [sc.enter_context(nc.sbuf_tensor(f"m3xc{i}", [128, 1024], BF16))
                    for i in range(2)]
            z_l = [sc.enter_context(nc.sbuf_tensor(f"m3z{i}", [128, 1024], BF16))
                   for i in range(2)]
            a_t = [sc.enter_context(nc.sbuf_tensor(f"m3a{i}", [128, 1024], F32))
                   for i in range(2)]
            b_t = [sc.enter_context(nc.sbuf_tensor(f"m3b{i}", [128, 1024], BF16))
                   for i in range(2)]
            h_t = [sc.enter_context(nc.sbuf_tensor(f"m3h{i}", [128, 1024], BF16))
                   for i in range(2)]
            hc_t = [sc.enter_context(nc.sbuf_tensor(f"m3hc{i}", [128, 1024], BF16))
                    for i in range(2)]
            zs_t = [sc.enter_context(nc.sbuf_tensor(f"m3zs{i}", [128, 1024], F32))
                    for i in range(2)]
            y1_t = sc.enter_context(nc.sbuf_tensor("m3y1", [128, 1024], F32))
            yg_t = [sc.enter_context(nc.sbuf_tensor(f"m3yg{i}", [128, 1024], BF16))
                    for i in range(2)]
            ps_y = [sc.enter_context(nc.psum_tensor(f"m3py{i}", [128, 1024], F32))
                    for i in range(2)]
            ps_w = sc.enter_context(nc.psum_tensor("m3pw", [128, 1024], F32))
            ps_c = sc.enter_context(nc.psum_tensor("m3pc", [128, 1024], F32))
            row_a = sc.enter_context(nc.sbuf_tensor("m3ra", [1, T], F32))
            row_b = sc.enter_context(nc.sbuf_tensor("m3rb", [1, T], F32))
            row_c = sc.enter_context(nc.sbuf_tensor("m3rc", [1, T], F32))
            h2r = [sc.enter_context(nc.sbuf_tensor(f"m3h2r{i}", [128, T], BF16))
                   for i in range(2)]
            afstg3 = sc.enter_context(nc.sbuf_tensor("m3afs", [128, 16 * NKT], F32))
            af3 = sc.enter_context(nc.sbuf_tensor("m3af", [128, 16 * NKT], BF16))
            blk = sc.enter_context(nc.Block())
            rst()
            L0, L2, E0, E20, V0, V20, M0, S0 = \
                ld.v, ld2.v, ev.v, ev2.v, dv.v, dv2.v, mm.v, sp.v
            PB, PC, PD = pb.v, pc.v, pd.v
            RP = [lda.v, ldb.v]
            rmark = {}
            for ps_ in range(32):
                RP[ps_ % 2] += 32
                rmark[ps_] = RP[ps_ % 2]
            LDP = [s2a.v, s2b.v]
            lmark = {}
            for g3 in range(64):
                LDP[g3 % 2] += 32
                lmark[g3] = LDP[g3 % 2]
            YP = [spa.v, spb.v]
            ymark = {}
            for g3 in range(64):
                YP[g3 % 2] += 16
                ymark[g3] = YP[g3 % 2]

            @blk.sync
            def _(e):
                for th in range(2):
                    for s in range(16):
                        ps_ = th * 16 + s
                        if ps_ >= 1:
                            e.wait_ge(pc.h, PC + ps_)
                        e.dma_start(row_b[:, 0:1024],
                                    p_sb[s:s + 1, th * 1024:(th + 1) * 1024]) \
                            .then_inc([lda, ldb][ps_ % 2].h, 16)
                        e.dma_start(row_c[:, 0:1024],
                                    p_sb[16 + s:17 + s, th * 1024:(th + 1) * 1024]) \
                            .then_inc([lda, ldb][ps_ % 2].h, 16)
                for c in range(NKT):
                    e.dma_start(afstg3[:, c * 16:(c + 1) * 16],
                                a_ffn[c * 128:(c + 1) * 128, :]).then_inc(ld2.h, 16)
                for kt in range(NKT):
                    if kt == 0:
                        e.wait_ge(s2b.h, S2B_H)
                    if kt >= 2:
                        e.wait_ge(mm.h, M0 + 1024 + kt - 1)
                    e.dma_start(h2r[kt % 2][:],
                                sc_h2[kt * 128:(kt + 1) * 128, :]) \
                        .then_inc(ld2.h, 16)

            @blk.vector
            def _(e):
                it = 0
                for th in range(2):
                    for s in range(16):
                        ps_ = th * 16 + s
                        e.wait_ge([lda, ldb][ps_ % 2].h, rmark[ps_])
                        if ps_ >= 1:
                            e.wait_ge(pc.h, PC + ps_)
                        e.tensor_tensor(row_a[0:1, 0:1024], row_b[0:1, 0:1024],
                                        dt_row[0:1, th * 1024:(th + 1) * 1024],
                                        ALU.mult).then_inc(pb.h, 1)
                    for dg in range(NDG):
                        g3 = th * 32 + dg
                        for s in range(16):
                            p = it % 2
                            if s == 0:
                                e.wait_ge([s2a, s2b][g3 % 2].h, lmark[g3])
                                e.wait_ge(pd.h, PD + th * 16 + 16)
                            e.wait_ge(ev.h, E0 + it + 1)
                            e.tensor_tensor(b_t[p][:], xc_l[dg % 2][:],
                                            wsb_all[:, s * 1024:(s + 1) * 1024],
                                            ALU.mult)
                            if th == 0:
                                e.tensor_tensor_scan(h_t[p][:], a_t[p][:], b_t[p][:],
                                                     0.0, ALU.mult, ALU.add)
                            else:
                                e.tensor_tensor_scan(
                                    h_t[p][:], a_t[p][:], b_t[p][:],
                                    carry[:, dg * 16 + s:dg * 16 + s + 1],
                                    ALU.mult, ALU.add)
                            if it >= 2:
                                e.wait_ge(mm.h, M0 + it - 1)
                            e.tensor_tensor(hc_t[p][:], h_t[p][:],
                                            cb_all[:, s * 1024:(s + 1) * 1024],
                                            ALU.mult).then_inc(dv.h, 1)
                            if th == 0:
                                e.tensor_copy(carry[:, dg * 16 + s:dg * 16 + s + 1],
                                              h_t[p][:, 1023:1024])
                            it += 1
                        e.wait_ge(mm.h, M0 + 16 * (g3 + 1))
                        e.wait_ge(ev2.h, E20 + g3 + 1)
                        if g3 >= 2:
                            e.wait_ge([spa, spb][g3 % 2].h, ymark[g3 - 2])
                        e.scalar_tensor_tensor(y1_t[:], xc_l[dg % 2][:],
                                               packT[:, 160 + dg:161 + dg],
                                               ps_y[dg % 2][:], ALU.mult, ALU.add)
                        e.tensor_tensor(yg_t[dg % 2][:], y1_t[:], zs_t[dg % 2][:],
                                        ALU.mult).then_inc(dv2.h, 1)

            dv.v = V0 + 1024
            dv2.v = V20 + 64
            pb.v = PB + 32

            @blk.tensor
            def _(e):
                e.wait_ge(ev.h, EV_RMS)
                e.wait_ge(dv.h, DV_RR)
                it = 0
                for th in range(2):
                    for s in range(16):
                        ps_ = th * 16 + s
                        e.wait_ge(pb.h, PB + ps_ + 1)
                        e.wait_ge([lda, ldb][ps_ % 2].h, rmark[ps_])
                        if ps_ >= 1:
                            e.wait_ge(pd.h, PD + ps_)
                        for nn in range(2):
                            e.matmul(ps_w[:, nn * 512:(nn + 1) * 512],
                                     ones_r_f[0:1, :],
                                     row_a[0:1, nn * 512:(nn + 1) * 512],
                                     start=True, stop=True)
                        last = None
                        for nn in range(2):
                            last = e.matmul(ps_c[:, nn * 512:(nn + 1) * 512],
                                            ones_r_f[0:1, :],
                                            row_c[0:1, nn * 512:(nn + 1) * 512],
                                            start=True, stop=True)
                        last.then_inc(pc.h, 1)
                    for dg in range(NDG):
                        g3 = th * 32 + dg
                        for s in range(16):
                            p = it % 2
                            e.wait_ge(dv.h, V0 + it + 1)
                            if s == 0 and g3 >= 2:
                                e.wait_ge(dv2.h, V20 + g3 - 1)
                            last = None
                            for nn in range(2):
                                last = e.matmul(
                                    ps_y[dg % 2][:, nn * 512:(nn + 1) * 512],
                                    ident_b[:],
                                    hc_t[p][:, nn * 512:(nn + 1) * 512],
                                    start=(s == 0), stop=(s == 15))
                            last.then_inc(mm.h, 1)
                            it += 1
                e.wait_ge(pd.h, PD + 32)
                e.wait_ge(ev.h, E0 + 1025)
                for kt in range(NKT):
                    e.wait_ge(ld2.h, L2 + 16 * NKT + 16 * (kt + 1))
                    last = None
                    for n in range(4):
                        tgt = ps_w if n < 2 else ps_c
                        last = e.matmul(tgt[0:16, (n % 2) * 512:(n % 2) * 512 + 512],
                                        af3[:, kt * 16:(kt + 1) * 16],
                                        h2r[kt % 2][:, n * 512:(n + 1) * 512],
                                        start=(kt == 0), stop=(kt == NKT - 1))
                    last.then_inc(mm.h, 1)

            mm.v = M0 + 1024 + NKT
            pc.v = PC + 32

            @blk.scalar
            def _(e):
                it = 0
                for th in range(2):
                    for s in range(16):
                        ps_ = th * 16 + s
                        e.wait_ge(pc.h, PC + ps_ + 1)
                        if th == 1:
                            e.wait_ge(dv.h, V0 + 512)
                        e.copy(wsb_all[:, s * 1024:(s + 1) * 1024], ps_w[:])
                        e.copy(cb_all[:, s * 1024:(s + 1) * 1024], ps_c[:]) \
                            .then_inc(pd.h, 1)
                    for dg in range(NDG):
                        g3 = th * 32 + dg
                        for s in range(16):
                            p = it % 2
                            if it >= 2:
                                e.wait_ge(dv.h, V0 + it - 1)
                            e.activation(a_t[p][:],
                                         dtb_sb[:, th * 1024:(th + 1) * 1024],
                                         AF.Exp,
                                         scale=aneg[:, dg * 16 + s:dg * 16 + s + 1]) \
                                .then_inc(ev.h, 1)
                            it += 1
                        e.wait_ge([s2a, s2b][g3 % 2].h, lmark[g3])
                        if g3 >= 2:
                            e.wait_ge(dv2.h, V20 + g3 - 1)
                        e.activation(zs_t[dg % 2][:], z_l[dg % 2][:], AF.Silu) \
                            .then_inc(ev2.h, 1)
                e.wait_ge(ld2.h, L2 + 16 * NKT)
                e.copy(af3[:], afstg3[:]).then_inc(ev.h, 1)
                e.wait_ge(mm.h, M0 + 1024 + NKT)
                e.copy(hA_g[0:16, 0:512], ps_w[0:16, 0:512])
                e.copy(hA_g[0:16, 512:1024], ps_w[0:16, 512:1024])
                e.copy(hA_g[0:16, 1024:1536], ps_c[0:16, 0:512])
                e.copy(hA_g[0:16, 1536:2048], ps_c[0:16, 512:1024]) \
                    .then_inc(ev.h, 1)

            ev.v = E0 + 1026
            ev2.v = E20 + 64
            pd.v = PD + 32
            ld2.v = L2 + 16 * 2 * NKT

            @blk.gpsimd
            def _(e):
                for th in range(2):
                    for dg in range(NDG):
                        g3 = th * 32 + dg
                        if dg == 0:
                            if g3 >= 2:
                                e.wait_ge(dv2.h, V20 + g3 - 1)
                            e.dma_start(xc_l[0][:],
                                        sc_xc[dg * 128:(dg + 1) * 128,
                                              th * 1024:(th + 1) * 1024]) \
                                .then_inc([s2a, s2b][g3 % 2].h, 16)
                            e.dma_start(z_l[0][:],
                                        sc_z[dg * 128:(dg + 1) * 128,
                                             th * 1024:(th + 1) * 1024]) \
                                .then_inc([s2a, s2b][g3 % 2].h, 16)
                        if dg < NDG - 1:
                            ng = g3 + 1
                            if ng >= 2:
                                e.wait_ge(dv2.h, V20 + ng - 1)
                            e.dma_start(xc_l[(dg + 1) % 2][:],
                                        sc_xc[(dg + 1) * 128:(dg + 2) * 128,
                                              th * 1024:(th + 1) * 1024]) \
                                .then_inc([s2a, s2b][ng % 2].h, 16)
                            e.dma_start(z_l[(dg + 1) % 2][:],
                                        sc_z[(dg + 1) * 128:(dg + 2) * 128,
                                             th * 1024:(th + 1) * 1024]) \
                                .then_inc([s2a, s2b][ng % 2].h, 16)
                        e.wait_ge(dv2.h, V20 + g3 + 1)
                        e.dma_start(sc_yg[dg * 128:(dg + 1) * 128,
                                          th * 1024:(th + 1) * 1024],
                                    yg_t[dg % 2][:]) \
                            .then_inc([spa, spb][g3 % 2].h, 16)
                e.wait_ge(spa.h, YP[0])
                e.wait_ge(spb.h, YP[1])

            lda.v, ldb.v = RP[0], RP[1]
            spa.v, spb.v = YP[0], YP[1]
            s2a.v, s2b.v = LDP[0], LDP[1]

            if debug:
                @blk.sync
                def _(e):
                    e.wait_ge(spa.h, YP[0])
                    e.wait_ge(spb.h, YP[1])
                    e.dma_start(dbg["dbg_xc0"], sc_xc[0:128, :]).then_inc(ld.h, 16)
                    e.dma_start(dbg["dbg_z0"], sc_z[0:128, :]).then_inc(ld.h, 16)
                    e.dma_start(dbg["dbg_yg0"], sc_yg[0:128, :]).then_inc(ld.h, 16)
                    e.wait_ge(ld.h, ld.v + 48)
                ld.v += 48

        m2ctx.close()
        mctx.close()

        with contextlib.ExitStack() as sc:
            yg_h = sc.enter_context(nc.sbuf_tensor("m4yg", [128, 32 * 1024], BF16))
            wstg4 = [sc.enter_context(nc.sbuf_tensor(f"m4ws{i}", [128, 512], F32))
                     for i in range(2)]
            wb4 = [sc.enter_context(nc.sbuf_tensor(f"m4wb{i}", [128, 32 * 512], BF16))
                   for i in range(2)]
            s_t = [sc.enter_context(nc.sbuf_tensor(f"m4st{i}", [128, 512], F32))
                   for i in range(2)]
            ps4 = [sc.enter_context(nc.psum_tensor(f"m4p{i}", [128, 512], F32))
                   for i in range(4)]
            blk = sc.enter_context(nc.Block())
            rst()
            L0, L2, E0, M0, P0, S20 = ld.v, ld2.v, ev.v, mm.v, po.v, sp2.v
            WP = [l2a.v, l2b.v]
            wmark = {}
            for jk in range(256):
                WP[jk % 2] += 16
                wmark[jk] = WP[jk % 2]
            STP = [spa.v, spb.v]
            smark = {}
            for gg in range(64):
                STP[gg % 2] += 16
                smark[gg] = STP[gg % 2]
            S2A0 = s2a.v

            PB4 = pb.v
            pb.v = PB4 + 256

            @blk.sync
            def _(e):
                for nh in range(2):
                    for mg in range(4):
                        j = nh * 4 + mg
                        for k in range(32):
                            jk = j * 32 + k
                            if jk >= 2:
                                e.wait_ge(pb.h, PB4 + jk - 1)
                            e.dma_start(wstg4[k % 2][:],
                                        outWT[k * 128:(k + 1) * 128,
                                              mg * 512:(mg + 1) * 512]) \
                                .then_inc([l2a, l2b][jk % 2].h, 16)

            @blk.gpsimd
            def _(e):
                for k in range(32):
                    e.dma_start(yg_h[:, k * 1024:(k + 1) * 1024],
                                sc_yg[k * 128:(k + 1) * 128, 0:1024]) \
                        .then_inc(s2a.h, 16)
                for j in range(8):
                    for g in range(8):
                        ms, nsub = g // 2, g % 2
                        gg = j * 8 + g
                        nh, mg = j // 4, j % 4
                        m2 = mg * 4 + ms
                        e.wait_ge(ev.h, E0 + gg + 1)
                        e.dma_start(sc_s[m2 * 128:(m2 + 1) * 128,
                                         nh * 1024 + nsub * 512:
                                         nh * 1024 + nsub * 512 + 512],
                                    s_t[gg % 2][:]) \
                            .then_inc([spa, spb][gg % 2].h, 16)
                    if j == 3:
                        e.wait_ge(mm.h, M0 + 32)
                        for k in range(32):
                            e.dma_start(yg_h[:, k * 1024:(k + 1) * 1024],
                                        sc_yg[k * 128:(k + 1) * 128, 1024:2048]) \
                                .then_inc(s2a.h, 16)
                e.wait_ge(spa.h, STP[0])
                e.wait_ge(spb.h, STP[1])

            spa.v, spb.v = STP[0], STP[1]
            s2a.v = S2A0 + 16 * 64



            @blk.tensor
            def _(e):
                for nh in range(2):
                    for mg in range(4):
                        j = nh * 4 + mg
                        e.wait_ge(pb.h, PB4 + 32 * (j + 1))
                        e.wait_ge(s2a.h, S2A0 + 16 * 32 * (nh + 1))
                        for g in range(8):
                            ms, nsub = g // 2, g % 2
                            gg = j * 8 + g
                            if gg >= 4:
                                e.wait_ge(ev.h, E0 + gg - 3)
                            last = None
                            for k in range(32):
                                last = e.matmul(
                                    ps4[g % 4][:],
                                    wb4[j % 2][:, k * 512 + ms * 128:
                                               k * 512 + ms * 128 + 128],
                                    yg_h[:, k * 1024 + nsub * 512:
                                         k * 1024 + nsub * 512 + 512],
                                    start=(k == 0), stop=(k == 31))
                            last.then_inc(mm.h, 1)

            mm.v = M0 + 64

            @blk.scalar
            def _(e):
                def cast4(j, g):
                    for k in range(4 * g, 4 * g + 4):
                        jk = j * 32 + k
                        e.wait_ge([l2a, l2b][jk % 2].h, wmark[jk])
                        e.copy(wb4[j % 2][:, k * 512:(k + 1) * 512],
                               wstg4[k % 2][:]).then_inc(pb.h, 1)
                for j in range(2):
                    for g in range(8):
                        cast4(j, g)
                for j in range(8):
                    for g in range(8):
                        gg = j * 8 + g
                        e.wait_ge(mm.h, M0 + gg + 1)
                        if gg >= 2:
                            e.wait_ge([spa, spb][gg % 2].h, smark[gg - 2])
                        e.copy(s_t[gg % 2][:], ps4[gg % 4][:]).then_inc(ev.h, 1)
                        if 1 <= j < 7:
                            cast4(j + 1, g)

            ev.v = E0 + 64

            l2a.v, l2b.v = WP[0], WP[1]

            if debug:
                @blk.sync
                def _(e):
                    e.wait_ge(spa.h, STP[0])
                    e.wait_ge(spb.h, STP[1])
                    e.dma_start(dbg["dbg_s0"], sc_s[0:128, :]).then_inc(ld.h, 16)
                    e.wait_ge(ld.h, ld.v + 16)
                ld.v += 16


        h1ctx = contextlib.ExitStack()
        h1_sb = h1ctx.enter_context(nc.sbuf_tensor("h1_sb", [128, NKT * T], BF16))

        with contextlib.ExitStack() as sc:
            blk = sc.enter_context(nc.Block())
            rst()
            M0F0 = mm.v
            L2P = [l2a.v, l2b.v]
            for kt in range(NKT):
                L2P[kt % 2] += 16
            H1A, H1B = L2P[0], L2P[1]

            @blk.sync
            def _(e):
                e.wait_ge(mm.h, M0F0)
                e.wait_ge(ev.h, ev.v)
                e.wait_ge(spa.h, spa.v)
                e.wait_ge(spb.h, spb.v)
                e.wait_ge(s2a.h, s2a.v)
                e.wait_ge(s2b.h, s2b.v)
                for kt in range(NKT):
                    e.dma_start(h1_sb[:, kt * T:(kt + 1) * T],
                                sc_h1[kt * 128:(kt + 1) * 128, :]) \
                        .then_inc([l2a, l2b][kt % 2].h, 16)

            l2a.v, l2b.v = L2P[0], L2P[1]

        with contextlib.ExitStack() as sc:
            wsl = [sc.enter_context(nc.sbuf_tensor(f"f1sl{i}", [128, 8192], BF16))
                   for i in range(3)]
            wstg1 = [sc.enter_context(nc.sbuf_tensor(f"f1stg{i}", [128, 1024], F32))
                     for i in range(2)]
            q_sb = [sc.enter_context(nc.sbuf_tensor(f"f1q{i}", [128, T], BF16))
                    for i in range(2)]
            k_sb = [sc.enter_context(nc.sbuf_tensor(f"f1k{i}", [128, T], BF16))
                    for i in range(2)]
            v_sb = sc.enter_context(nc.sbuf_tensor("f1v", [128, T], BF16))
            vT_sb = [sc.enter_context(nc.sbuf_tensor(f"f1vt{i}", [128, T], BF16))
                     for i in range(2)]
            sqn = sc.enter_context(nc.sbuf_tensor("f1sq", [128, T], BF16))
            sqnk = sc.enter_context(nc.sbuf_tensor("f1sqk", [128, T], BF16))
            nb_col = sc.enter_context(nc.sbuf_tensor("f1nb", [128, 8], F32))
            nrx = sc.enter_context(nc.sbuf_tensor("f1nx", [128, 64], F32))
            nrxk = sc.enter_context(nc.sbuf_tensor("f1nxk", [128, 64], F32))
            nrow2 = sc.enter_context(nc.sbuf_tensor("f1nr2", [33, T], F32))
            nrow1 = sc.enter_context(nc.sbuf_tensor("f1nr1", [33, 512], F32))
            pt_t = [sc.enter_context(nc.sbuf_tensor(f"f1pt{i}", [128, 512], BF16))
                    for i in range(4)]
            ptm_t = [sc.enter_context(nc.sbuf_tensor(f"f1pm{i}", [128, 512], BF16))
                     for i in range(2)]
            rb_sb = sc.enter_context(nc.sbuf_tensor("f1rb", [128, 512], F32))
            at_t = sc.enter_context(nc.sbuf_tensor("f1at", [128, 512], BF16))
            psA = sc.enter_context(nc.psum_tensor("f1pA", [128, 512], F32))
            psB = sc.enter_context(nc.psum_tensor("f1pB", [128, 512], F32))
            psC = sc.enter_context(nc.psum_tensor("f1pC", [128, 128], BF16))
            psD = sc.enter_context(nc.psum_tensor("f1pD", [128, 128], BF16))
            psE = sc.enter_context(nc.psum_tensor("f1pE", [128, 512], F32))
            psF = sc.enter_context(nc.psum_tensor("f1pF", [128, 512], F32))
            psG = sc.enter_context(nc.psum_tensor("f1pG", [128, 512], F32))
            psH = sc.enter_context(nc.psum_tensor("f1pH", [128, 512], F32))
            PS4 = [psE, psF, psG, psH]
            blk = sc.enter_context(nc.Block())
            rst()

            mk = {}
            c = dict(mm=mm.v, mm2=mm2.v, ev=ev.v, ev2=ev2.v, dv=dv.v,
                     dv2=dv2.v, po=po.v, sp=sp.v, pb=pb.v, ld=ld.v,
                     lda=lda.v, ldb=ldb.v)

            def bump(sem, key, amt=1):
                c[sem] += amt
                mk[key] = c[sem]

            for s in range(24):
                bump('ld', ('ldW', 0, s), 32)
                bump('pb', ('cast', 0, s))
            for h in range(17):
                g = h - 1
                par = ['lda', 'ldb'][h % 2]
                if h < 16:
                    for j in range(12):
                        bump('mm', ('qkv', h, j))
                    for kt in range(16):
                        bump('mm2', ('tr', h, kt))
                    bump('mm', ('nq', h))
                    bump('mm', ('nk', h))
                    for j in range(4):
                        bump('ev', ('evac', h, j))
                    bump('ev', ('sqq', h))
                    for j in range(4, 8):
                        bump('ev', ('evac', h, j))
                    bump('ev', ('sqk', h))
                    for j in range(8, 12):
                        bump('ev', ('evac', h, j))
                    for kt in range(16):
                        bump('dv', ('vt', h, kt))
                    bump('dv', ('nqc', h))
                    bump('dv', ('nkc', h))
                    for r in range(1, 6):
                        bump(par, ('pyqd', h, r), 16)
                        bump(par, ('pykd', h, r), 16)
                    if h % 4 == 3 and h // 4 + 1 < 4:
                        gr = h // 4 + 1
                        for s in range(24):
                            bump('ld', ('ldW', gr, s), 32)
                            bump('pb', ('cast', gr, s))
                for r in range(1, 6):
                    if h < 16:
                        bump('dv', ('pyq', h, r))
                        bump('dv', ('pyk', h, r))
                    if h >= 1 and r >= 2:
                        qc = r - 2
                        bump('dv', ('rc', g, qc))
                        bump('dv2', ('at', g, qc))
                if h >= 1:
                    jj = 0
                    for qc in range(4):
                        nkt_q = 4 * qc + 4
                        for kt in range(nkt_q):
                            bump('mm', ('sc', g, jj))
                            if kt == 2 and qc >= 1:
                                bump('mm', ('rb', g, qc - 1))
                            bump('mm2', ('pv', g, jj))
                            bump('ev2', ('exp', g, jj))
                            if kt >= 4 * qc:
                                bump('po', ('aff', g, jj))
                            jj += 1
                        bump('ev', ('rbe', g, qc))
                        bump('sp', ('sp', g, qc), 16)
                    bump('mm', ('rb', g, 3))
                if h < 16:
                    bump('mm', ('negB', h))
                    bump('ev', ('sn', h))
                    bump('ev', ('nbe', h))

            WMATS = [wqT, wkT, wvT]

            def slab_stage(e, gr, s):
                mat, c8 = s // 8, s % 8
                if s >= 2:
                    e.wait_ge(pb.h, mk[('cast', gr, s - 2)])
                elif gr >= 1:
                    e.wait_ge(pb.h, mk[('cast', gr - 1, 22 + s)])
                for kq in range(2):
                    ktg = c8 * 2 + kq
                    e.dma_start(wstg1[s % 2][:, kq * 512:(kq + 1) * 512],
                                WMATS[mat][ktg * 128:(ktg + 1) * 128,
                                           gr * 512:(gr + 1) * 512]) \
                        .then_inc(ld.h, 16)

            @blk.sync
            def _(e):
                for s in range(24):
                    slab_stage(e, 0, s)
                for h in range(16):
                    par = [lda, ldb][h % 2]
                    bnd = (h % 4 == 3 and h // 4 + 1 < 4)
                    gr = h // 4 + 1
                    e.wait_ge(dv.h, mk[('nqc', h)])
                    e.dma_start(nrx[:, 0:16], nrow2[0:1, 0:2048]).then_inc(par.h, 16)
                    e.wait_ge(dv.h, mk[('nkc', h)])
                    e.dma_start(nrxk[:, 0:16], nrow2[32:33, 0:2048]) \
                        .then_inc(par.h, 16)
                    if bnd:
                        for s_ in range(8):
                            slab_stage(e, gr, s_)
                    e.wait_ge(dv.h, mk[('pyq', h, 1)])
                    e.dma_start(nrx[0:16, 24:32], nrx[:, 16:17]).then_inc(par.h, 16)
                    e.wait_ge(dv.h, mk[('pyk', h, 1)])
                    e.dma_start(nrxk[0:16, 24:32], nrxk[:, 16:17]).then_inc(par.h, 16)
                    e.wait_ge(dv.h, mk[('pyq', h, 2)])
                    e.dma_start(nrx[0:4, 40:44], nrx[0:16, 32:33]).then_inc(par.h, 16)
                    e.wait_ge(dv.h, mk[('pyk', h, 2)])
                    e.dma_start(nrxk[0:4, 40:44], nrxk[0:16, 32:33]).then_inc(par.h, 16)
                    e.wait_ge(dv.h, mk[('pyq', h, 3)])
                    e.dma_start(nrx[0:2, 48:50], nrx[0:4, 44:45]).then_inc(par.h, 16)
                    e.wait_ge(dv.h, mk[('pyk', h, 3)])
                    e.dma_start(nrxk[0:2, 48:50], nrxk[0:4, 44:45]).then_inc(par.h, 16)
                    e.wait_ge(dv.h, mk[('pyq', h, 4)])
                    e.dma_start(nrx[0:1, 56:58], nrx[0:2, 50:51]).then_inc(par.h, 16)
                    e.wait_ge(dv.h, mk[('pyk', h, 4)])
                    e.dma_start(nrxk[0:1, 56:58], nrxk[0:2, 50:51]).then_inc(par.h, 16)
                    if bnd:
                        for s_ in range(8, 24):
                            slab_stage(e, gr, s_)

            @blk.tensor
            def _(e):
                for h in range(17):
                    g = h - 1
                    hh = h % 4
                    if h < 16:
                        for j in range(12):
                            mat, n = j // 4, j % 4
                            if h == 0 and j == 0:
                                e.wait_ge(l2a.h, H1A)
                                e.wait_ge(l2b.h, H1B)
                            if j < 2 and h >= 1:
                                e.wait_ge(ev.h, mk[('nbe', h - 1)])
                            if j >= 2:
                                e.wait_ge(ev.h, mk[('evac', h, j - 2)])
                            if n == 0:
                                e.wait_ge(pb.h, mk[('cast', h // 4, 8 * mat + 7)])
                            last = None
                            for kt in range(NKT):
                                last = e.matmul(
                                    [psA, psB][j % 2][:],
                                    wsl[mat][:, kt * 512 + hh * 128:
                                             kt * 512 + hh * 128 + 128],
                                    h1_sb[:, kt * T + n * 512:kt * T + n * 512 + 512],
                                    start=(kt == 0), stop=(kt == NKT - 1))
                            last.then_inc(mm.h, 1)
                        for kt in range(16):
                            if kt == 0:
                                e.wait_ge(ev.h, mk[('evac', h, 11)])
                            if kt >= 2:
                                e.wait_ge(dv.h, mk[('vt', h, kt - 2)])
                            elif h >= 1:
                                e.wait_ge(dv.h, mk[('vt', h - 1, 15)])
                            e.transpose([psC, psD][kt % 2][:],
                                        v_sb[:, kt * 128:(kt + 1) * 128],
                                        ident_b[:]).then_inc(mm2.h, 1)
                        for half in range(2):
                            if half == 0:
                                e.wait_ge(ev.h, mk[('sqq', h)])
                                if h >= 2:
                                    e.wait_ge(ev2.h, mk[('exp', h - 2, 39)])
                                    e.wait_ge(dv.h, mk[('rc', h - 2, 3)])
                                    e.wait_ge(dv2.h, mk[('at', h - 2, 3)])
                                src2 = sqn
                            else:
                                e.wait_ge(ev.h, mk[('sqk', h)])
                                e.wait_ge(dv.h, mk[('nqc', h)])
                                src2 = sqnk
                            last = None
                            for n in range(4):
                                last = e.matmul(PS4[n][0:1, :], ones_c_b[:],
                                                src2[:, n * 512:(n + 1) * 512],
                                                start=True, stop=True)
                            last.then_inc(mm.h, 1)
                    if h >= 1:
                        SCB = [psE, psF, psA, psB]
                        jjb = 0
                        for qc in range(4):
                            nkt_q = 4 * qc + 4
                            for step in range(nkt_q + 4):
                                if step < nkt_q:
                                    jj = jjb + step
                                    kt = step
                                    if jj >= 4:
                                        e.wait_ge(ev2.h, mk[('exp', g, jj - 4)])
                                    elif jj >= 2:
                                        if h < 16:
                                            e.wait_ge(ev.h, mk[('evac', h, 11)])
                                        else:
                                            e.wait_ge(ev.h, mk[('nbe', 15)])
                                    else:
                                        if h < 16:
                                            e.wait_ge(dv.h, mk[('nkc', h)])
                                        else:
                                            e.wait_ge(ev2.h, mk[('exp', g - 1, 39)])
                                            e.wait_ge(ev.h, mk[('rbe', g - 1, 3)])
                                            e.wait_ge(dv.h, mk[('rc', g - 1, 3)])
                                            e.wait_ge(dv2.h, mk[('at', g - 1, 3)])
                                    e.matmul(SCB[jj % 4][:],
                                             k_sb[g % 2][:, kt * 128:(kt + 1) * 128],
                                             q_sb[g % 2][:, qc * 512:(qc + 1) * 512],
                                             start=True, stop=True).then_inc(mm.h, 1)
                                    if step == 2 and qc >= 1:
                                        e.wait_ge(dv.h, mk[('rc', g, qc - 1)])
                                        e.matmul(psG[:], ones_r_f[0:1, :],
                                                 nrow1[0:1, 0:512],
                                                 start=True, stop=True) \
                                            .then_inc(mm.h, 1)
                                if step >= 4:
                                    jj = jjb + step - 4
                                    kt = step - 4
                                    e.wait_ge(ev2.h, mk[('exp', g, jj)])
                                    if kt >= 4 * qc:
                                        e.wait_ge(po.h, mk[('aff', g, jj)])
                                        src = ptm_t[jj % 2]
                                    else:
                                        src = pt_t[jj % 4]
                                    if kt == 0 and qc >= 1:
                                        e.wait_ge(ev.h, mk[('rbe', g, qc - 1)])
                                        e.wait_ge(dv2.h, mk[('at', g, qc - 1)])
                                    e.matmul(psG[0:1, :], ones_c_b[:], src[:],
                                             start=(kt == 0), stop=(kt == nkt_q - 1))
                                    e.matmul(psH[:],
                                             vT_sb[g % 2][:, kt * 128:(kt + 1) * 128],
                                             src[:],
                                             start=(kt == 0), stop=(kt == nkt_q - 1)) \
                                        .then_inc(mm2.h, 1)
                            jjb += nkt_q
                        e.wait_ge(dv.h, mk[('rc', g, 3)])
                        e.matmul(psG[:], ones_r_f[0:1, :], nrow1[0:1, 0:512],
                                 start=True, stop=True).then_inc(mm.h, 1)
                    if h < 16:
                        if h >= 1:
                            e.wait_ge(ev2.h, mk[('exp', h - 1, 38)])
                        e.wait_ge(ev.h, mk[('sn', h)])
                        e.matmul(psA[:, 0:1], ones_r_f[0:1, :], nrx[0:1, 63:64],
                                 start=True, stop=True).then_inc(mm.h, 1)

            @blk.scalar
            def _(e):
                for s in range(24):
                    e.wait_ge(ld.h, mk[('ldW', 0, s)])
                    mat, c8 = s // 8, s % 8
                    e.copy(wsl[mat][:, c8 * 1024:(c8 + 1) * 1024],
                           wstg1[s % 2][:]).then_inc(pb.h, 1)
                for h in range(17):
                    g = h - 1
                    bnd = (h < 16 and h % 4 == 3 and h // 4 + 1 < 4)
                    gr = h // 4 + 1

                    def cast4(s0):
                        for s in range(s0, s0 + 4):
                            mat, c8 = s // 8, s % 8
                            e.wait_ge(ld.h, mk[('ldW', gr, s)])
                            e.wait_ge(mm.h, mk[('qkv', h, 4 * mat + 3)])
                            e.copy(wsl[mat][:, c8 * 1024:(c8 + 1) * 1024],
                                   wstg1[s % 2][:]).then_inc(pb.h, 1)
                    if h < 16:
                        for j in range(12):
                            mat, n = j // 4, j % 4
                            e.wait_ge(mm.h, mk[('qkv', h, j)])
                            dst = [q_sb[h % 2], k_sb[h % 2], v_sb][mat]
                            if mat == 0:
                                e.activation(dst[:, n * 512:(n + 1) * 512],
                                             [psA, psB][j % 2][:],
                                             AF.Identity, scale=RSQ_HD) \
                                    .then_inc(ev.h, 1)
                            else:
                                e.copy(dst[:, n * 512:(n + 1) * 512],
                                       [psA, psB][j % 2][:]).then_inc(ev.h, 1)
                            if j == 3:
                                e.activation(sqn[:], q_sb[h % 2][:], AF.Square) \
                                    .then_inc(ev.h, 1)
                            if j == 7:
                                e.activation(sqnk[:], k_sb[h % 2][:], AF.Square) \
                                    .then_inc(ev.h, 1)
                        if bnd:
                            cast4(0)
                            cast4(4)
                    if h >= 1:
                        jj = 0
                        for qc in range(4):
                            nkt_q = 4 * qc + 4
                            for kt in range(nkt_q):
                                e.wait_ge(mm.h, mk[('sc', g, jj)])
                                if jj >= 4:
                                    e.wait_ge(mm2.h, mk[('pv', g, jj - 4)])
                                e.activation(pt_t[jj % 4][:],
                                             [psE, psF, psA, psB][jj % 4][:],
                                             AF.Exp, bias=nb_col[:, 0:1]) \
                                    .then_inc(ev2.h, 1)
                                jj += 1
                            e.wait_ge(mm.h, mk[('rb', g, qc)])
                            e.copy(rb_sb[:], psG[:]).then_inc(ev.h, 1)
                            if bnd and qc >= 2:
                                cast4(8 * qc - 8)
                                cast4(8 * qc - 4)
                    if h < 16:
                        e.wait_ge(dv.h, mk[('pyk', h, 5)])
                        e.activation(nrx[0:1, 62:63], nrx[0:1, 60:61],
                                     AF.Sqrt, scale=nrx[0:1, 61:62])
                        e.activation(nrx[0:1, 63:64], nrx[0:1, 62:63],
                                     AF.Identity, scale=-1.0).then_inc(ev.h, 1)
                        e.wait_ge(mm.h, mk[('negB', h)])
                        e.copy(nb_col[:, 0:1], psA[:, 0:1]).then_inc(ev.h, 1)

            @blk.vector
            def _(e):
                for h in range(17):
                    g = h - 1
                    par = [lda, ldb][h % 2]
                    if h < 16:
                        for kt in range(16):
                            e.wait_ge(mm2.h, mk[('tr', h, kt)])
                            e.tensor_copy(vT_sb[h % 2][:, kt * 128:(kt + 1) * 128],
                                          [psC, psD][kt % 2][:]).then_inc(dv.h, 1)
                        e.wait_ge(mm.h, mk[('nq', h)])
                        last = None
                        for n in range(4):
                            last = e.tensor_copy(nrow2[0:1, n * 512:(n + 1) * 512],
                                                 PS4[n][0:1, :])
                        last.then_inc(dv.h, 1)
                        e.wait_ge(mm.h, mk[('nk', h)])
                        last = None
                        for n in range(4):
                            last = e.tensor_copy(nrow2[32:33, n * 512:(n + 1) * 512],
                                                 PS4[n][0:1, :])
                        last.then_inc(dv.h, 1)
                    for r in range(1, 6):
                        if h < 16:
                            e.wait_ge(par.h, mk[('pyqd', h, r)])
                            if r == 1:
                                e.reduce_max(nrx[:, 16:17], nrx[:, 0:16], AX.X) \
                                    .then_inc(dv.h, 1)
                            elif r == 2:
                                e.reduce_max(nrx[0:16, 32:33], nrx[0:16, 24:32],
                                             AX.X).then_inc(dv.h, 1)
                            elif r == 3:
                                e.reduce_max(nrx[0:4, 44:45], nrx[0:4, 40:44],
                                             AX.X).then_inc(dv.h, 1)
                            elif r == 4:
                                e.reduce_max(nrx[0:2, 50:51], nrx[0:2, 48:50],
                                             AX.X).then_inc(dv.h, 1)
                            else:
                                e.tensor_tensor(nrx[0:1, 60:61], nrx[0:1, 56:57],
                                                nrx[0:1, 57:58], ALU.max) \
                                    .then_inc(dv.h, 1)
                            e.wait_ge(par.h, mk[('pykd', h, r)])
                            if r == 1:
                                e.reduce_max(nrxk[:, 16:17], nrxk[:, 0:16], AX.X) \
                                    .then_inc(dv.h, 1)
                            elif r == 2:
                                e.reduce_max(nrxk[0:16, 32:33], nrxk[0:16, 24:32],
                                             AX.X).then_inc(dv.h, 1)
                            elif r == 3:
                                e.reduce_max(nrxk[0:4, 44:45], nrxk[0:4, 40:44],
                                             AX.X).then_inc(dv.h, 1)
                            elif r == 4:
                                e.reduce_max(nrxk[0:2, 50:51], nrxk[0:2, 48:50],
                                             AX.X).then_inc(dv.h, 1)
                            else:
                                e.tensor_tensor(nrx[0:1, 61:62], nrxk[0:1, 56:57],
                                                nrxk[0:1, 57:58], ALU.max) \
                                    .then_inc(dv.h, 1)
                        if h >= 1 and r >= 2:
                            qc = r - 2
                            e.wait_ge(mm2.h, mk[('pv', g, sum(4 * q_ + 4 for q_ in range(qc + 1)) - 1)])
                            e.tensor_copy(nrow1[32:33, 0:512], psG[0:1, :])
                            e.reciprocal(nrow1[0:1, 0:512], nrow1[32:33, 0:512]) \
                                .then_inc(dv.h, 1)
                            e.wait_ge(ev.h, mk[('rbe', g, qc)])
                            if (g, qc) != (0, 0):
                                pk = ('sp', g, qc - 1) if qc >= 1 else ('sp', g - 1, 3)
                                e.wait_ge(sp.h, mk[pk])
                            e.tensor_tensor(at_t[:], psH[:], rb_sb[:], ALU.mult) \
                                .then_inc(dv2.h, 1)

            @blk.gpsimd
            def _(e):
                for h in range(1, 17):
                    g = h - 1
                    jj = 0
                    for qc in range(4):
                        nkt_q = 4 * qc + 4
                        for kt in range(nkt_q):
                            if kt >= 4 * qc:
                                e.wait_ge(ev2.h, mk[('exp', g, jj)])
                                if jj >= 2:
                                    e.wait_ge(mm2.h, mk[('pv', g, jj - 2)])
                                e.affine_select(ptm_t[jj % 2][:],
                                                pt_t[jj % 4][:],
                                                [[1, 512]], ALU.is_ge, 0.0,
                                                base=qc * 512 - kt * 128,
                                                channel_multiplier=-1) \
                                    .then_inc(po.h, 1)
                            jj += 1
                        e.wait_ge(dv2.h, mk[('at', g, qc)])
                        e.dma_start(sc_attn[g * 128:(g + 1) * 128,
                                            qc * 512:(qc + 1) * 512],
                                    at_t[:]).then_inc(sp.h, 16)
                e.wait_ge(sp.h, mk[('sp', 15, 3)])

            mm.v, mm2.v, ev.v, ev2.v = c['mm'], c['mm2'], c['ev'], c['ev2']
            dv.v, dv2.v, po.v, sp.v = c['dv'], c['dv2'], c['po'], c['sp']
            pb.v, ld.v = c['pb'], c['ld']
            lda.v, ldb.v = c['lda'], c['ldb']

        h1ctx.close()
        M_F1END = mm.v

        fctx = contextlib.ExitStack()
        af_b = fctx.enter_context(nc.sbuf_tensor("af_b", [128, 16 * NKT], BF16))
        mod_row = fctx.enter_context(nc.sbuf_tensor("mod_row", [16, T], F32))
        modhA_b = fctx.enter_context(nc.sbuf_tensor("modhA_b", [16, T], BF16))

        with contextlib.ExitStack() as sc:
            attn_all = sc.enter_context(nc.sbuf_tensor("f2at", [128, NKT * T], BF16))
            wstg2 = sc.enter_context(nc.sbuf_tensor("f2stg", [128, 4096], F32))
            wo_sl = sc.enter_context(nc.sbuf_tensor("f2wo", [128, 8192], BF16))
            afstg = sc.enter_context(nc.sbuf_tensor("f2af", [128, 256], F32))
            ao_sb = [sc.enter_context(nc.sbuf_tensor(f"f2ao{i}", [128, 512], BF16))
                     for i in range(2)]
            ps_ao = [sc.enter_context(nc.psum_tensor(f"f2pa{i}", [128, 512], F32))
                     for i in range(2)]
            ps_mod = [sc.enter_context(nc.psum_tensor(f"f2pm{i}", [128, 512], F32))
                      for i in range(4)]
            blk = sc.enter_context(nc.Block())
            rst()
            L0, L2, E0, E20, M0, M20, V0, P0 = \
                ld.v, ld2.v, ev.v, ev2.v, mm.v, mm2.v, dv.v, po.v
            PBF2 = pb.v
            pb.v = PBF2 + 9

            @blk.sync
            def _(e):
                for kt in range(NKT):
                    e.dma_start(afstg[:, kt * 16:(kt + 1) * 16],
                                a_ffn[kt * 128:(kt + 1) * 128, :]).then_inc(ld.h, 16)
                for kt in range(NKT):
                    e.dma_start(attn_all[:, kt * T:(kt + 1) * T],
                                sc_attn[kt * 128:(kt + 1) * 128, :]).then_inc(lda.h, 16)
                for mg in range(4):
                    for c in range(2):
                        mc = mg * 2 + c
                        if mc >= 1:
                            e.wait_ge(pb.h, PBF2 + 1 + mc)
                        for kt in range(8):
                            ktg = c * 8 + kt
                            e.dma_start(wstg2[:, kt * 512:(kt + 1) * 512],
                                        woT[ktg * 128:(ktg + 1) * 128,
                                            mg * 512:(mg + 1) * 512]) \
                                .then_inc(ld2.h, 16)

            ld.v = L0 + 16 * 16
            ld2.v = L2 + 16 * 64
            lda.v = lda.v + 16 * 16
            SPF2 = lda.v

            @blk.tensor
            def _(e):
                e.wait_ge(lda.h, SPF2)
                e.wait_ge(pb.h, PBF2 + 1)
                for j in range(64):
                    m, n = j // 4, j % 4
                    mg = m // 4
                    if j >= 1:
                        pj = j - 1
                        pm_, pn_ = pj // 4, pj % 4
                        e.wait_ge(ev.h, E0 + pj + 1)
                        e.matmul(ps_mod[pn_][0:16, :],
                                 af_b[:, pm_ * 16:(pm_ + 1) * 16],
                                 ao_sb[pj % 2][:],
                                 start=(pm_ == 0), stop=(pm_ == 15)) \
                            .then_inc(mm2.h, 1)
                    if n == 0 and m % 4 == 0:
                        e.wait_ge(pb.h, PBF2 + 1 + 2 * (mg + 1))
                    if j >= 2:
                        e.wait_ge(ev.h, E0 + j - 1)
                    last = None
                    for kt in range(NKT):
                        last = e.matmul(ps_ao[j % 2][:],
                                        wo_sl[:, kt * 512 + (m % 4) * 128:
                                              kt * 512 + (m % 4) * 128 + 128],
                                        attn_all[:, kt * T + n * 512:kt * T + n * 512 + 512],
                                        start=(kt == 0), stop=(kt == NKT - 1))
                    last.then_inc(mm.h, 1)
                e.wait_ge(ev.h, E0 + 64)
                e.matmul(ps_mod[3][0:16, :], af_b[:, 15 * 16:16 * 16],
                         ao_sb[63 % 2][:], start=False, stop=True).then_inc(mm2.h, 1)

            mm.v = M0 + 64
            mm2.v = M20 + 64

            @blk.scalar
            def _(e):
                e.wait_ge(ld.h, L0 + 16 * 16)
                e.copy(af_b[:], afstg[:]).then_inc(pb.h, 1)
                for c in range(2):
                    e.wait_ge(ld2.h, L2 + 16 * 8 * (c + 1))
                    e.copy(wo_sl[:, c * 4096:(c + 1) * 4096],
                           wstg2[:]).then_inc(pb.h, 1)
                for j in range(64):
                    if j % 16 == 0 and j > 0:
                        mg = j // 16
                        for c in range(2):
                            mc = mg * 2 + c
                            e.wait_ge(ld2.h, L2 + 16 * 8 * (mc + 1))
                            e.wait_ge(mm.h, M0 + 16 * mg)
                            e.copy(wo_sl[:, c * 4096:(c + 1) * 4096],
                                   wstg2[:]).then_inc(pb.h, 1)
                    e.wait_ge(mm.h, M0 + j + 1)
                    if j >= 2:
                        e.wait_ge(mm2.h, M20 + j - 1)
                    e.copy(ao_sb[j % 2][:], ps_ao[j % 2][:]).then_inc(ev.h, 1)
                e.wait_ge(mm2.h, M20 + 64)
                last = None
                for n in range(4):
                    last = e.activation(mod_row[0:16, n * 512:(n + 1) * 512],
                                        ps_mod[n][0:16, :], AF.Tanh)
                last.then_inc(ev.h, 1)

            ev.v = E0 + 65

            @blk.vector
            def _(e):
                e.wait_ge(ev.h, E0 + 65)
                e.tensor_scalar(mod_row[0:16, :], mod_row[0:16, :], 0.1, None,
                                ALU.mult).then_inc(dv.h, 1)

            dv.v = V0 + 1

            if debug:
                @blk.sync
                def _(e):
                    e.wait_ge(dv.h, V0 + 1)
                    e.dma_start(dbg["dbg_attn0"], sc_attn[0:128, :]).then_inc(ld.h, 16)
                    e.dma_start(dbg["dbg_mod"], mod_row[0:16, :]).then_inc(ld.h, 16)
                    e.wait_ge(ld.h, ld.v + 32)
                ld.v += 32

        h2ctx = contextlib.ExitStack()
        h2_sb = h2ctx.enter_context(nc.sbuf_tensor("h2_sb", [128, NKT * T], BF16))
        with contextlib.ExitStack() as sc:
            blk = sc.enter_context(nc.Block())
            rst()
            E0, V0 = ev.v, dv.v
            LP = [lda.v, ldb.v]
            for kt in range(NKT):
                LP[kt % 2] += 16
            H2A, H2B = LP[0], LP[1]

            @blk.sync
            def _(e):
                e.wait_ge(mm.h, M_F1END)
                e.wait_ge(ev.h, E0)
                for kt in range(NKT):
                    e.dma_start(h2_sb[:, kt * T:(kt + 1) * T],
                                sc_h2[kt * 128:(kt + 1) * 128, :]) \
                        .then_inc([lda, ldb][kt % 2].h, 16)

            lda.v, ldb.v = LP[0], LP[1]

            @blk.vector
            def _(e):
                e.wait_ge(ev.h, E0)
                e.tensor_tensor(modhA_b[0:16, :], mod_row[0:16, :], hA_g[0:16, :],
                                ALU.mult).then_inc(dv.h, 1)

            dv.v = V0 + 1

        with contextlib.ExitStack() as sc:
            wstg4f = sc.enter_context(nc.sbuf_tensor("f4stg", [128, 2048], F32))
            wib = [sc.enter_context(nc.sbuf_tensor(f"f4wi{i}", [128, 8192], BF16))
                   for i in range(2)]
            wub = [sc.enter_context(nc.sbuf_tensor(f"f4wu{i}", [128, 8192], BF16))
                   for i in range(2)]
            bstg = sc.enter_context(nc.sbuf_tensor("f4bs", [16, 1024], F32))
            bf_b = sc.enter_context(nc.sbuf_tensor("bf_b", [16, FF], BF16))
            gate_t = [sc.enter_context(nc.sbuf_tensor(f"f4g{i}", [128, 512], BF16))
                      for i in range(2)]
            gu_t = [sc.enter_context(nc.sbuf_tensor(f"f4u{i}", [128, 512], BF16))
                    for i in range(2)]
            ps_g = [sc.enter_context(nc.psum_tensor(f"f4pg{i}", [128, 512], F32))
                    for i in range(2)]
            ps_u = [sc.enter_context(nc.psum_tensor(f"f4pu{i}", [128, 512], F32))
                    for i in range(2)]
            blk = sc.enter_context(nc.Block())
            rst()
            L0, L2, E0, M0, M20, V0, P0, S0 = \
                ld.v, ld2.v, ev.v, mm.v, mm2.v, dv.v, po.v, sp.v
            GSP = [spa.v, spb.v]
            gmark = {}
            for g in range(256):
                GSP[g % 2] += 16
                gmark[g] = GSP[g % 2]

            PBF4 = pb.v
            pb.v = PBF4 + 136

            @blk.sync
            def _(e):
                for bc in range(8):
                    if bc >= 1:
                        e.wait_ge(pb.h, PBF4 + bc)
                    e.dma_start(bstg[:], b_ffn[:, bc * 1024:(bc + 1) * 1024]) \
                        .then_inc(ld.h, 16)
                for fq in range(16):
                    for mat, wsrc in enumerate([winbT, wupbT]):
                        for q4 in range(4):
                            mc = fq * 8 + mat * 4 + q4
                            if mc >= 1:
                                e.wait_ge(pb.h, PBF4 + 8 + mc)
                            for kq in range(4):
                                ktg = q4 * 4 + kq
                                e.dma_start(wstg4f[:, kq * 512:(kq + 1) * 512],
                                            wsrc[ktg * 128:(ktg + 1) * 128,
                                                 fq * 512:(fq + 1) * 512]) \
                                    .then_inc(ld2.h, 16)

            ld.v = L0 + 128
            ld2.v = L2 + 16 * 512

            @blk.gpsimd
            def _(e):
                for g in range(256):
                    f, n = g // 4, g % 4
                    e.wait_ge(dv.h, V0 + g + 1)
                    e.dma_start(sc_gu[f * 128:(f + 1) * 128,
                                      n * 512:(n + 1) * 512],
                                gu_t[g % 2][:]) \
                        .then_inc([spa, spb][g % 2].h, 16)
                e.wait_ge(spa.h, GSP[0])
                e.wait_ge(spb.h, GSP[1])

            spa.v, spb.v = GSP[0], GSP[1]

            @blk.tensor
            def _(e):
                e.wait_ge(lda.h, H2A)
                e.wait_ge(ldb.h, H2B)
                for fq in range(16):
                    e.wait_ge(pb.h, PBF4 + 8 + 8 * (fq + 1))
                    for ms in range(4):
                        f = fq * 4 + ms
                        for n in range(4):
                            g = f * 4 + n
                            if g >= 2:
                                e.wait_ge(ev.h, E0 + g - 1)
                            last = None
                            for kt in range(NKT):
                                last = e.matmul(
                                    ps_g[g % 2][:],
                                    wib[fq % 2][:, kt * 512 + ms * 128:
                                               kt * 512 + ms * 128 + 128],
                                    h2_sb[:, kt * T + n * 512:kt * T + n * 512 + 512],
                                    start=(kt == 0), stop=False)
                            e.matmul(ps_g[g % 2][:], bf_b[:, f * 128:(f + 1) * 128],
                                     modhA_b[0:16, n * 512:(n + 1) * 512],
                                     start=False, stop=True).then_inc(mm.h, 1)
                            if g >= 2:
                                e.wait_ge(dv.h, V0 + g - 1)
                            last = None
                            for kt in range(NKT):
                                last = e.matmul(
                                    ps_u[g % 2][:],
                                    wub[fq % 2][:, kt * 512 + ms * 128:
                                               kt * 512 + ms * 128 + 128],
                                    h2_sb[:, kt * T + n * 512:kt * T + n * 512 + 512],
                                    start=(kt == 0), stop=(kt == NKT - 1))
                            last.then_inc(mm2.h, 1)

            mm.v = M0 + 256
            mm2.v = M20 + 256

            @blk.scalar
            def _(e):
                for g in range(256):
                    e.wait_ge(mm.h, M0 + g + 1)
                    if g >= 2:
                        e.wait_ge(dv.h, V0 + g - 1)
                    e.activation(gate_t[g % 2][:], ps_g[g % 2][:], AF.Silu) \
                        .then_inc(ev.h, 1)

            ev.v = E0 + 256

            @blk.vector
            def _(e):
                def wcast1(fq, i):
                    mat, q4 = i // 4, i % 4
                    mc = fq * 8 + mat * 4 + q4
                    e.wait_ge(ld2.h, L2 + 16 * 4 * (mc + 1))
                    dst = [wib, wub][mat][fq % 2]
                    e.tensor_copy(dst[:, q4 * 2048:(q4 + 1) * 2048],
                                  wstg4f[:]).then_inc(pb.h, 1)
                for bc in range(8):
                    e.wait_ge(ld.h, L0 + 16 * (bc + 1))
                    e.tensor_copy(bf_b[:, bc * 1024:(bc + 1) * 1024],
                                  bstg[:]).then_inc(pb.h, 1)
                for i in range(8):
                    wcast1(0, i)
                for i in range(8):
                    wcast1(1, i)
                for g in range(256):
                    if g % 2 == 0 and g >= 16 and g // 16 + 1 <= 15:
                        wcast1(g // 16 + 1, (g % 16) // 2)
                    e.wait_ge(ev.h, E0 + g + 1)
                    e.wait_ge(mm2.h, M20 + g + 1)
                    if g >= 2:
                        e.wait_ge([spa, spb][g % 2].h, gmark[g - 2])
                    e.tensor_tensor(gu_t[g % 2][:], gate_t[g % 2][:],
                                    ps_u[g % 2][:], ALU.mult).then_inc(dv.h, 1)

            dv.v = V0 + 256


            if debug:
                @blk.sync
                def _(e):
                    e.wait_ge(spa.h, GSP[0])
                    e.wait_ge(spb.h, GSP[1])
                    e.dma_start(dbg["dbg_gu0"], sc_gu[0:128, :]).then_inc(ld.h, 16)
                    e.wait_ge(ld.h, ld.v + 16)
                ld.v += 16

        h2ctx.close()
        fctx.close()

        with contextlib.ExitStack() as sc:
            gu_sb = sc.enter_context(nc.sbuf_tensor("f5gu", [128, 64 * 1024], BF16))
            wstg5 = [sc.enter_context(nc.sbuf_tensor(f"f5ws{i}", [128, 512], F32))
                     for i in range(4)]
            wd_b = [sc.enter_context(nc.sbuf_tensor(f"f5wd{i}", [128, 512], BF16))
                    for i in range(8)]
            mean_b5 = sc.enter_context(nc.sbuf_tensor("f5mb", [128, 1024], F32))
            rstd_b5 = sc.enter_context(nc.sbuf_tensor("f5rb", [128, 1024], F32))
            x_l = [sc.enter_context(nc.sbuf_tensor(f"f5x{i}", [128, 512], F32))
                   for i in range(4)]
            s_l = [sc.enter_context(nc.sbuf_tensor(f"f5s{i}", [128, 512], F32))
                   for i in range(4)]
            g_t = sc.enter_context(nc.sbuf_tensor("f5g", [128, 512], F32))
            t1 = sc.enter_context(nc.sbuf_tensor("f5t1", [128, 512], F32))
            t2 = sc.enter_context(nc.sbuf_tensor("f5t2", [128, 512], F32))
            o_t = [sc.enter_context(nc.sbuf_tensor(f"f5o{i}", [128, 512], F32))
                   for i in range(2)]
            ps5 = [sc.enter_context(nc.psum_tensor(f"f5p{i}", [128, 512], F32))
                   for i in range(8)]
            blk = sc.enter_context(nc.Block())
            rst()
            L0, L2, E0, E20, M0, M20, V0, V20, P0, S0, FI = \
                ld.v, ld2.v, ev.v, ev2.v, mm.v, mm2.v, dv.v, dv2.v, po.v, sp.v, fin.v
            WP5 = [l2a.v, l2b.v, lda.v, ldb.v]
            wmark5 = {}
            for kk in range(512):
                WP5[kk % 4] += 16
                wmark5[kk] = WP5[kk % 4]
            XSP = [x.v for x in xs4]
            xsmark = {}
            for gg in range(64):
                XSP[gg % 4] += 32
                xsmark[gg] = XSP[gg % 4]
            FOP = [fo2[0].v, fo2[1].v]
            fomark = {}
            for gg in range(64):
                FOP[gg % 2] += 16
                fomark[gg] = FOP[gg % 2]

            @blk.tensor
            def _(e):
                for nh in range(2):
                    if nh == 1:
                        e.wait_ge(ev2.h, E20 + 1)
                        e.wait_ge(dv2.h, V20 + 32)
                    for nn in range(2):
                        e.matmul(ps5[nn][:], ones_r_f[0:1, :],
                                 rows65[0:1, nh * 1024 + nn * 512:
                                          nh * 1024 + nn * 512 + 512],
                                 start=True, stop=True)
                    last = None
                    for nn in range(2):
                        last = e.matmul(ps5[2 + nn][:], ones_r_f[32:33, :],
                                        rows65[32:33, nh * 1024 + nn * 512:
                                                 nh * 1024 + nn * 512 + 512],
                                        start=True, stop=True)
                    last.then_inc(mm2.h, 1)
                    e.wait_ge(ld.h, L0 + 16 * 64 * (nh + 1))
                    e.wait_ge(ev2.h, E20 + nh + 1)
                    for mq in range(4):
                        hj = nh * 4 + mq
                        if hj >= 1:
                            e.wait_ge(dv2.h, V20 + 8 * hj)
                        for k in range(64):
                            kk = hj * 64 + k
                            e.wait_ge(po.h, P0 + kk + 1)
                            last = None
                            for g in range(8):
                                ms, nsub = g // 2, g % 2
                                last = e.matmul(ps5[g][:],
                                                wd_b[k % 8][:, ms * 128:(ms + 1) * 128],
                                                gu_sb[:, k * 1024 + nsub * 512:
                                                      k * 1024 + nsub * 512 + 512],
                                                start=(k == 0), stop=(k == 63))
                            last.then_inc(mm.h, 1)

            mm.v = M0 + 512
            mm2.v = M20 + 2

            @blk.scalar
            def _(e):
                def wcast(kk):
                    e.wait_ge([l2a, l2b, lda, ldb][kk % 4].h, wmark5[kk])
                    if kk >= 8:
                        e.wait_ge(mm.h, M0 + kk - 7)
                    e.copy(wd_b[kk % 8][:], wstg5[kk % 4][:]) \
                        .then_inc(po.h, 1)

                for hj in range(9):
                    nh, mq = hj // 4, hj % 4
                    if hj >= 1:
                        for g in range(8):
                            gg = (hj - 1) * 8 + g
                            e.wait_ge(dv.h, V0 + 3 * gg + 3)
                            e.activation(g_t[:], t2[:], AF.Silu).then_inc(ev.h, 1)
                    if hj < 8 and mq == 0:
                        e.wait_ge(mm2.h, M20 + nh + 1)
                        for nn in range(2):
                            e.copy(mean_b5[:, nn * 512:(nn + 1) * 512], ps5[nn][:])
                        last = None
                        for nn in range(2):
                            last = e.copy(rstd_b5[:, nn * 512:(nn + 1) * 512],
                                          ps5[2 + nn][:])
                        last.then_inc(ev2.h, 1)
                    if hj < 8:
                        for k in range(64):
                            wcast(hj * 64 + k)

            ev.v = E0 + 64
            ev2.v = E20 + 2
            po.v = P0 + 512

            @blk.vector
            def _(e):
                for nh in range(2):
                    for mq in range(4):
                        hj = nh * 4 + mq
                        e.wait_ge(mm.h, M0 + 64 * (hj + 1))
                        for g in range(8):
                            ms, nsub = g // 2, g % 2
                            gg = hj * 8 + g
                            e.wait_ge(xs4[gg % 4].h, xsmark[gg])
                            e.tensor_tensor(t1[:], x_l[g % 4][:],
                                            mean_b5[:, nsub * 512:(nsub + 1) * 512],
                                            ALU.subtract)
                            e.tensor_tensor(t2[:], t1[:],
                                            rstd_b5[:, nsub * 512:(nsub + 1) * 512],
                                            ALU.mult)
                            e.tensor_tensor(t2[:], t2[:], ps5[g][:], ALU.add) \
                                .then_inc(dv.h, 3)
                            e.wait_ge(ev.h, E0 + gg + 1)
                            e.tensor_tensor(t1[:], g_t[:], s_l[g % 4][:], ALU.mult)
                            if gg >= 2:
                                e.wait_ge(fo2[gg % 2].h, fomark[gg - 2])
                            e.tensor_tensor(o_t[gg % 2][:], t1[:], x_l[g % 4][:],
                                            ALU.add).then_inc(dv2.h, 1)

            dv.v = V0 + 3 * 64
            dv2.v = V20 + 64

            @blk.gpsimd
            def _(e):
                def coords(gg):
                    hj, g = gg // 8, gg % 8
                    nh, mq = hj // 4, hj % 4
                    ms, nsub = g // 2, g % 2
                    m2 = mq * 4 + ms
                    col = nh * 1024 + nsub * 512
                    return g, m2, col

                def load(gg):
                    g, m2, col = coords(gg)
                    if gg >= 4:
                        e.wait_ge(dv2.h, V20 + gg - 3)
                    e.dma_start(x_l[g % 4][:],
                                xT[m2 * 128:(m2 + 1) * 128, col:col + 512]) \
                        .then_inc(xs4[gg % 4].h, 16)
                    e.dma_start(s_l[g % 4][:],
                                sc_s[m2 * 128:(m2 + 1) * 128, col:col + 512]) \
                        .then_inc(xs4[gg % 4].h, 16)

                for gg in range(4):
                    load(gg)
                for hj in range(8):
                    for g in range(8):
                        gg = hj * 8 + g
                        _, m2, col = coords(gg)
                        e.wait_ge(dv2.h, V20 + gg + 1)
                        e.dma_start(outT[m2 * 128:(m2 + 1) * 128, col:col + 512],
                                    o_t[gg % 2][:]).then_inc(fo2[gg % 2].h, 16)
                        if gg + 4 < 64:
                            load(gg + 4)

            for i in range(4):
                xs4[i].v = XSP[i]
            fo2[0].v, fo2[1].v = FOP[0], FOP[1]

            @blk.sync
            def _(e):
                for nh in range(2):
                    if nh == 1:
                        e.wait_ge(mm.h, M0 + 256)
                    for k in range(64):
                        e.dma_start(gu_sb[:, k * 1024:(k + 1) * 1024],
                                    sc_gu[k * 128:(k + 1) * 128,
                                          nh * 1024:(nh + 1) * 1024]) \
                            .then_inc(ld.h, 16)
                    for mq in range(4):
                        hj = nh * 4 + mq
                        for k in range(64):
                            kk = hj * 64 + k
                            if kk >= 4:
                                e.wait_ge(po.h, P0 + kk - 3)
                            e.dma_start(wstg5[kk % 4][:],
                                        wdownT[k * 128:(k + 1) * 128,
                                               mq * 512:(mq + 1) * 512]) \
                                .then_inc([l2a, l2b, lda, ldb][kk % 4].h, 16)
                e.wait_ge(fo2[0].h, FOP[0])
                e.wait_ge(fo2[1].h, FOP[1])

            ld.v = L0 + 16 * 128
            l2a.v, l2b.v = WP5[0], WP5[1]
            lda.v, ldb.v = WP5[2], WP5[3]


    return nc



def _f32c(a):
    return np.ascontiguousarray(np.asarray(a), dtype=np.float32)


def kernel(**inputs):
    global LAST_DEBUG
    I = inputs
    x = _f32c(I["L0_attn_x"])
    W_in = _f32c(I["L0_attn_W_in"])
    conv_w = _f32c(I["L0_attn_conv_w"])
    conv_b = _f32c(I["L0_attn_conv_b"])
    W_x = _f32c(I["L0_attn_W_x"])
    A_log = _f32c(I["L0_attn_A_log"])
    dt_bias = _f32c(I["L0_attn_dt_bias"])
    out_W = _f32c(I["L0_attn_out_W"])
    Dvec = _f32c(I["L0_attn_D"])
    ln1 = _f32c(I["L0_ffn_ln1_w"])
    ln2 = _f32c(I["L0_ffn_ln2_w"])
    Wq = _f32c(I["L0_ffn_Wq"])
    Wk = _f32c(I["L0_ffn_Wk"])
    Wv = _f32c(I["L0_ffn_Wv"])
    Wo = _f32c(I["L0_ffn_Wo"])
    A = _f32c(I["L0_ffn_A"])
    B = _f32c(I["L0_ffn_B"])
    W_in_base = _f32c(I["L0_ffn_W_in_base"])
    W_up_base = _f32c(I["L0_ffn_W_up_base"])
    W_down = _f32c(I["L0_ffn_W_down"])

    packv = np.zeros((256, 128), np.float32)
    packv[0:128] = conv_w[:, 0, :].reshape(32, 128, 4).transpose(2, 0, 1) \
        .reshape(128, 128)
    packv[128:160] = conv_b.reshape(32, 128)
    packv[160:192] = Dvec.reshape(32, 128)
    packv[192:208] = ln1.reshape(16, 128)
    packv[208:224] = ln2.reshape(16, 128)

    shared = dict(
        w_inT=_f32c(W_in.T),
        wxT=_f32c(W_x.T),
        outWT=_f32c(out_W.T),
        wqT=_f32c(Wq.T), wkT=_f32c(Wk.T), wvT=_f32c(Wv.T), woT=_f32c(Wo.T),
        a_ffn=A, b_ffn=B,
        winbT=_f32c(W_in_base.T), wupbT=_f32c(W_up_base.T),
        wdownT=_f32c(W_down.T),
        a_perm=_f32c(A_log.reshape(32, 128, 16).transpose(1, 0, 2)
                     .reshape(128, 512)),
        packv=packv,
        dtb2d=_f32c(dt_bias.reshape(32, 128)),
        ident_in=np.eye(128, dtype=np.float32),
    )

    ncores = _NCORES
    nc = build_program(debug=_DEBUG)
    in_maps = []
    for c in range(ncores):
        m = dict(shared)
        m["xT"] = _f32c(x[c].T)
        in_maps.append(m)
    res = run_bass_kernel_spmd(nc, in_maps, core_ids=list(range(ncores)))
    LAST_DEBUG = res.results
    out = np.empty((8, T, D), np.float32)
    for c in range(ncores):
        out[c] = np.asarray(res.results[c]["outT"]).T
    return out
